# version 01
# baseline (speedup 1.0000x reference)
import numpy as np
import ml_dtypes
import concourse.bass as bass
import concourse.mybir as mybir
from concourse.bass_utils import run_bass_kernel_spmd
from concourse.bass_core_rust import IndirectOffsetOnAxis

F32 = mybir.dt.float32
BF16 = mybir.dt.bfloat16
I32 = mybir.dt.int32
AF = mybir.ActivationFunctionType
ALU = mybir.AluOpType
BF = ml_dtypes.bfloat16

D = 1024
NCH = 8
WIN = 16
EPS_LN = 1e-5
EPS_NORM = 1e-12
FF = 4096
E = 32
EH = 512
CAP = 1280
HS = 1024
XW = 1096
NONCE_F = None


class Sched:
    ENGS = ("sp", "pool", "pe", "act", "dve")

    def __init__(self, nc):
        self.nc = nc
        self.ops = {e: [] for e in self.ENGS}
        self.sem = {}
        self.cnt = {}
        self.lastwait = {e: {} for e in self.ENGS}
        for name in ("q_sp", "q_pool", "q_act", "q_pf",
                     "c_pe", "c_act", "c_dve", "c_pool", "c_sp"):
            self.sem[name] = nc.alloc_semaphore(name)
            self.cnt[name] = 0
        self.engobj = {}

    def tok(self, sem):
        return (sem, self.cnt[sem])

    def _engine_for(self, sem):
        return {"q_sp": "sp", "q_pool": "pool", "q_act": "act",
                "c_pe": "pe", "c_act": "act", "c_dve": "dve",
                "c_pool": "pool", "c_sp": "sp"}[sem]

    def op(self, eng, emit, deps=(), inc=None, inc_by=None):
        waits = []
        for sem, val in deps:
            if val <= 0:
                continue
            if (self._engine_for(sem) == eng and eng != "pool"
                    and sem in ("c_pe", "c_act", "c_dve", "c_sp")):
                continue
            lw = self.lastwait[eng].get(sem, 0)
            if val > lw:
                waits.append((sem, val))
                self.lastwait[eng][sem] = val
        if inc is not None:
            if inc_by is None:
                inc_by = 16 if inc.startswith("q_") else 1
            self.cnt[inc] += inc_by
        self.ops[eng].append((emit, tuple(waits), inc, inc_by))
        return self.tok(inc) if inc is not None else None

    def barrier_tokens(self):
        return [self.tok(s) for s in self.cnt]

    def build(self):
        nc = self.nc
        sched = self

        def runner(engobj, eng):
            for emit, waits, inc, inc_by in sched.ops[eng]:
                for semname, val in waits:
                    engobj.wait_ge(sched.sem[semname], val)
                ins = emit(engobj)
                if inc is not None:
                    assert ins is not None, f"op with inc={inc} emitted nothing"
                    ins.then_inc(sched.sem[inc], inc_by)

        with nc.Block() as block:
            block.sync(lambda e: runner(e, "sp"))
            block.gpsimd(lambda e: runner(e, "pool"))
            block.tensor(lambda e: runner(e, "pe"))
            block.scalar(lambda e: runner(e, "act"))
            block.vector(lambda e: runner(e, "dve"))


def ecopy(e, out, in_):
    return e.tensor_copy(out, in_) if hasattr(e, "tensor_copy") else e.copy(out, in_)


def ceil_div(a, b):
    return (a + b - 1) // b


class Arena:
    BASE = 17 * 1024
    TOP = 229376

    def __init__(self, nc):
        self.nc = nc
        self.off = self.BASE
        self.marks = []
        self.peak = self.BASE

    def alloc(self, name, shape, dtype, align=64):
        per = int(np.prod(shape[1:])) * {F32: 4, BF16: 2, I32: 4}[dtype]
        self.off = (self.off + align - 1) // align * align
        if not hasattr(self, "names"):
            self.names = {}
        self.names[name] = self.off
        t = self.nc.alloc_sbuf_tensor_at(name, list(shape), dtype, offset=self.off)
        self.off += per
        self.peak = max(self.peak, self.off)
        assert self.off <= self.TOP, f"SBUF overflow at {name}: {self.off}"
        return t

    def at(self, name, shape, dtype, at_name):
        return self.nc.alloc_sbuf_tensor_at(name, list(shape), dtype,
                                            offset=self.names[at_name])

    def mark(self):
        self.marks.append(self.off)

    def release(self):
        self.off = self.marks.pop()


class Prog:
    def __init__(self, T_OWN=4096, T_EXTRA=0, debug=False):
        self.T_OWN = T_OWN
        self.T_EXTRA = T_EXTRA
        self.debug = debug
        nc = bass.Bass("TRN2", target_bir_lowering=False,
                       detect_race_conditions=False)
        self.nc = nc
        self.S = Sched(nc)
        self.ar = Arena(nc)
        self.dbg_outs = {}
        self.war = {}
        self._declare_io()
        self._alloc_psum()
        self._alloc_perm()

    def _declare_io(self):
        nc = self.nc
        ap = lambda *a, **k: nc.dram_tensor(*a, **k).ap()
        self.x_own = ap("x_own", [self.T_OWN, D], F32, kind="ExternalInput")
        self.xhalo = ap("xhalo", [16, D], F32, kind="ExternalInput")
        self.cst_in = ap("cst_in", [128, 6], F32, kind="ExternalInput")
        if self.T_EXTRA:
            self.x_extra = ap("x_extra", [self.T_EXTRA, D], F32, kind="ExternalInput")
        for nm in ("WiT", "Wl1T", "Wl2T", "memWT", "WoT"):
            setattr(self, nm, ap(nm, [D, D], BF16, kind="ExternalInput"))
        self.WgT = ap("WgT", [2 * D, D], BF16, kind="ExternalInput")
        self.W1T = ap("W1T", [D, FF], BF16, kind="ExternalInput")
        self.W2T = ap("W2T", [FF, D], BF16, kind="ExternalInput")
        self.rWg = ap("rWg", [D, E], BF16, kind="ExternalInput")
        self.eW1 = ap("eW1", [E, D, EH], BF16, kind="ExternalInput")
        self.eW3 = ap("eW3", [E, D, EH], BF16, kind="ExternalInput")
        self.eW2 = ap("eW2", [E, EH, D], BF16, kind="ExternalInput")
        self.zWs1 = ap("zWs1", [D, HS], BF16, kind="ExternalInput")
        self.zWs3 = ap("zWs3", [D, HS], BF16, kind="ExternalInput")
        self.zWs2 = ap("zWs2", [HS, D], BF16, kind="ExternalInput")
        for nm, w in (("bi", D), ("bl1", D), ("bl2", D), ("mem_b", D), ("bo", D),
                      ("bg", D), ("b2", D), ("b1", FF)):
            setattr(self, nm + "_c", ap(nm + "_c", [128, w // 128], F32, kind="ExternalInput"))
        for nm, w in (("bl2", D), ("mem_b", D), ("bg", D), ("b2", D)):
            setattr(self, nm + "_r", ap(nm + "_r", [1, w], BF16, kind="ExternalInput"))
        for nm in ("g1", "be1", "g2", "be2", "g3", "be3"):
            setattr(self, nm + "_r", ap(nm + "_r", [1, D], BF16, kind="ExternalInput"))
        self.fgate = ap("fgate", [128, 1], F32, kind="ExternalInput")
        self.lrate = ap("lrate", [128, 1], F32, kind="ExternalInput")
        self.P_in = ap("P_in", [16, D], BF16, kind="ExternalInput")
        self.pscale = ap("pscale", [128, 1], F32, kind="ExternalInput")
        self.maskA = ap("maskA", [128, 144], BF16, kind="ExternalInput")
        self.maskF = ap("maskF", [128, 144], BF16, kind="ExternalInput")
        self.ident_bf = ap("ident_bf", [128, 128], BF16, kind="ExternalInput")
        self.ident_f32 = ap("ident_f32", [128, 128], F32, kind="ExternalInput")
        self.negI_bf = ap("negI_bf", [128, 128], BF16, kind="ExternalInput")
        self.ones_r = ap("ones_r", [1, 512], BF16, kind="ExternalInput")
        self.ltriS128 = ap("ltriS128", [128, 128], BF16, kind="ExternalInput")
        self.ltriS32 = ap("ltriS32", [32, 32], BF16, kind="ExternalInput")
        self.ebase_c = ap("ebase_c", [32, 2], F32, kind="ExternalInput")
        self.tokp = ap("tokp", [128, 2], F32, kind="ExternalInput")
        self.nonce = ap("nonce", [128, 1], F32, kind="ExternalInput")
        self.tok2 = ap("tok2", [2, 2], I32, kind="ExternalInput")
        self.si_grad8 = ap("si_grad8", [128, NCH], I32, kind="ExternalInput")
        self.gi_grad8 = ap("gi_grad8", [128, NCH], I32, kind="ExternalInput")
        self.fi_own = ap("fi_own", [2, 1], I32, kind="ExternalInput")
        self.fi_peer = ap("fi_peer", [2, 1], I32, kind="ExternalInput")
        self.out_d = ap("out_y", [self.T_OWN, D], F32, kind="ExternalOutput")
        self.hdd_spill = ap("hdd_spill", [D, self.T_OWN], BF16)
        self.x1_spill = ap("x1_spill", [self.T_OWN, D], BF16)
        self.x2tm_spill = ap("x2tm_spill", [self.T_OWN, D], BF16)
        self.x2fm_spill = ap("x2fm_spill", [D, self.T_OWN], BF16)
        self.ff_spill = ap("ff_spill", [self.T_OWN, D], BF16)
        self.y1_spill = ap("y1_spill", [self.T_OWN, D], F32)
        self.x4fm_spill = ap("x4fm_spill", [D, self.T_OWN], BF16)
        self.Xbuf = ap("Xbuf", [E * CAP, XW], BF16)
        self.Ypair = ap("Ypair", [4 * self.T_OWN, D], BF16)
        self.gr_slots = nc.dram_tensor("gr_slots", [2 * D, D], F32, kind="Internal",
                                       addr_space="Shared").ap()
        self.gr_flags = nc.dram_tensor("gr_flags", [4, 2], I32, kind="Internal",
                                       addr_space="Shared").ap()

    def dbg(self, name, shape, dtype=F32):
        t = self.nc.dram_tensor("dbg_" + name, list(shape), dtype, kind="ExternalOutput").ap()
        self.dbg_outs[name] = t
        return t

    def _alloc_psum(self):
        nc = self.nc
        self.psA = nc.alloc_psum_tensor("psA", [128, 512], F32)
        self.psB = nc.alloc_psum_tensor("psB", [128, 512], F32)
        self.psC = nc.alloc_psum_tensor("psC", [128, 1024], F32)
        self.psD = nc.alloc_psum_tensor("psD", [128, 1024], F32)
        self.psE = nc.alloc_psum_tensor("psE", [128, 512], F32)
        self.psT = nc.alloc_psum_tensor("psT", [128, 1024], BF16)

    def _alloc_perm(self):
        ar, S, nc = self.ar, self.S, self.nc
        self.wWi = ar.alloc("wWi", [128, NCH, D], BF16)
        self.wWl1 = ar.alloc("wWl1", [128, NCH, D], BF16)
        self.wWl2 = ar.alloc("wWl2", [128, NCH, D], BF16)
        self.wmemW = ar.alloc("wmemW", [128, NCH, D], BF16)
        self.gradT = ar.alloc("gradT", [128, NCH, D], F32)
        self.c_bi = ar.alloc("c_bi", [128, NCH], F32)
        self.c_bl1 = ar.alloc("c_bl1", [128, NCH], F32)
        self.c_bo = ar.alloc("c_bo", [128, NCH], F32)
        self.c_b2 = ar.alloc("c_b2", [128, NCH], F32)
        self.c_bg = ar.alloc("c_bg", [128, NCH], F32)
        self.c_memb = ar.alloc("c_memb", [128, NCH], F32)
        self.c_b1 = ar.alloc("c_b1", [128, FF // 128], F32)
        self.r_bl2 = ar.alloc("r_bl2", [1, D], BF16)
        self.r_memb = ar.alloc("r_memb", [1, D], BF16)
        self.r_bg = ar.alloc("r_bg", [1, D], BF16)
        self.r_b2 = ar.alloc("r_b2", [1, D], BF16)
        self.sb_ones = ar.alloc("sb_ones", [1, 512], BF16)
        self.sb_identbf = ar.alloc("sb_identbf", [128, 128], BF16)
        self.sb_identf32 = ar.alloc("sb_identf32", [128, 128], F32)
        self.sb_negI = ar.alloc("sb_negI", [128, 128], BF16)
        self.sb_maskA = ar.alloc("sb_maskA", [128, 144], BF16)
        self.sb_maskF = ar.alloc("sb_maskF", [128, 144], BF16)
        self.sb_fg = ar.alloc("sb_fg", [128, 1], F32)
        self.sb_lr = ar.alloc("sb_lr", [128, 1], F32)
        self.sb_pscale = ar.alloc("sb_pscale", [128, 1], F32)
        self.sb_cst = ar.alloc("sb_cst", [128, 6], F32)
        self.dmy_dve = ar.alloc("dmy_dve", [128, 1], F32)
        self.dmy_act = ar.alloc("dmy_act", [128, 1], F32)
        self.sb_P = ar.alloc("sb_P", [16, D], BF16)
        self.sb_sig8 = ar.alloc("sb_sig8", [128, NCH], I32)
        self.sb_tok2 = ar.alloc("sb_tok2", [2, 2], I32)
        self.sb_fio = ar.alloc("sb_fio", [2, 1], I32)
        self.sb_fip = ar.alloc("sb_fip", [2, 1], I32)
        self.sb_fbuf = ar.alloc("sb_fbuf", [2, 2], I32)
        self.sb_gig8 = ar.alloc("sb_gig8", [128, NCH], I32)
        self.sb_imask = ar.alloc("sb_imask", [128, 2], I32)
        self.sb_ltri128 = ar.alloc("sb_ltri128", [128, 128], BF16)
        self.sb_ltri32 = ar.alloc("sb_ltri32", [32, 32], BF16)
        self.sb_ebase = ar.alloc("sb_ebase", [32, 2], F32)
        self.sb_tokp = ar.alloc("sb_tokp", [128, 2], F32)
        self.wrWg = ar.alloc("wrWg", [128, NCH, E], BF16)
        for nm in ("g1", "be1", "g2", "be2", "g3", "be3"):
            setattr(self, "bc_" + nm, ar.alloc("bc_" + nm, [128, D], BF16))
        self.stat = ar.alloc("stat", [128, 64], F32)
        self.a_scr1 = ar.alloc("a_scr1", [128, D], F32)
        self.a_scr2 = ar.alloc("a_scr2", [128, D], F32)
        self.sqscr = ar.alloc("sqscr", [128, D], F32)
        self.perm_end = ar.off

    def load_perm(self):
        S = self
        sch = self.S
        loads = [
            (self.wWi, self.WiT), (self.wWl1, self.Wl1T), (self.wWl2, self.Wl2T),
            (self.wmemW, self.memWT),
            (self.c_bi, self.bi_c), (self.c_bl1, self.bl1_c), (self.c_bo, self.bo_c),
            (self.c_b2, self.b2_c), (self.c_bg, self.bg_c), (self.c_memb, self.mem_b_c),
            (self.c_b1, self.b1_c),
            (self.r_bl2, self.bl2_r), (self.r_memb, self.mem_b_r), (self.r_bg, self.bg_r),
            (self.r_b2, self.b2_r),
            (self.sb_ones, self.ones_r), (self.sb_identbf, self.ident_bf),
            (self.sb_identf32, self.ident_f32), (self.sb_negI, self.negI_bf),
            (self.sb_maskA, self.maskA), (self.sb_maskF, self.maskF),
            (self.sb_fg, self.fgate), (self.sb_lr, self.lrate), (self.sb_pscale, self.pscale),
            (self.sb_cst, self.cst_in),
            (self.r_g1, self.g1_r), (self.r_be1, self.be1_r),
            (self.r_g2, self.g2_r), (self.r_be2, self.be2_r),
            (self.r_g3, self.g3_r), (self.r_be3, self.be3_r),
            (self.sb_P, self.P_in), (self.sb_sig8, self.si_grad8),
            (self.sb_tok2, self.tok2), (self.sb_fio, self.fi_own), (self.sb_gig8, self.gi_grad8),
            (self.sb_ltri128, self.ltriS128), (self.sb_ltri32, self.ltriS32),
            (self.sb_ebase, self.ebase_c), (self.sb_tokp, self.tokp),
            (self.wrWg, self.rWg),
            (self.sb_fip, self.fi_peer),
        ]
        for dst, src in loads:
            if dst.shape[0] == 128 and len(dst.shape) == 3:
                srcv = src.rearrange("(c p) m -> p c m", p=128)
                sch.op("sp", lambda e, d=dst, s=srcv: e.dma_start(out=d[:], in_=s), inc="q_sp")
            else:
                sch.op("act", lambda e, d=dst, s=src: e.dma_start(out=d[:], in_=s[:]), inc="q_act")
        tk = sch.tok("q_act")
        self.tok_cst = tk
        sch.op("dve", lambda e: e.memset(self.gradT[:], 0.0), inc="c_dve")
        sch.op("dve", lambda e: e.memset(self.dmy_dve[:], 0.0), inc="c_dve")
        sch.op("dve", lambda e: e.memset(self.dmy_act[:], 0.0), inc="c_dve")
        for nm in ("g1", "be1", "g2", "be2", "g3", "be3"):
            row = getattr(self, "r_" + nm)
            dst = getattr(self, "bc_" + nm)
            for hh in range(2):
                mm = self.mm("psA", [lambda e, r=row, h=hh, ps=self.psA: e.matmul(
                    ps[:], self.sb_ones[0:1, 0:128], r[0:1, h * 512:(h + 1) * 512],
                    start=True, stop=True)], deps=[tk])
                self.evac("dve", "psA", lambda e, d=dst, h=hh: e.tensor_copy(
                    d[:, h * 512:(h + 1) * 512], self.psA[:]))
        self.tok_perm = [sch.tok("c_dve"), sch.tok("q_sp"), sch.tok("q_act"), sch.tok("c_pe")]

    def fenced(self, eng, emit, deps=()):
        self.S.op(eng, emit, deps=deps)
        if eng == "dve":
            return self.S.op("dve", lambda e: e.tensor_scalar_mul(
                self.dmy_dve[:], self.dmy_dve[:], 1.0), inc="c_dve")
        else:
            return self.S.op("act", lambda e: e.activation(
                self.dmy_act[:], self.dmy_act[:], AF.Copy), inc="c_act")

    def mm(self, psname, emits, deps=()):
        if not hasattr(self, "pslast"):
            self.pslast = {}
        st = self.pslast.setdefault(psname, {"wr": None, "rd": None})
        sch = self.S
        first_deps = list(deps)
        if st["rd"] is not None:
            first_deps.append(st["rd"])
        n = len(emits)
        for i, em in enumerate(emits):
            t = sch.op("pe", em, deps=first_deps if i == 0 else (),
                       inc="c_pe" if i == n - 1 else None)
        st["wr"] = t
        return t

    def evac(self, eng, psname, emit, deps=(), inc=None):
        st = self.pslast[psname]
        d = list(deps)
        if st["wr"] is not None:
            d.append(st["wr"])
        t = self.S.op(eng, emit, deps=d, inc=inc or ("c_" + eng))
        st["rd"] = t
        return t

    def alloc_phase1(self):
        ar = self.ar
        ar.mark()
        base = (ar.off + 63) // 64 * 64
        self.xt = [ar.alloc(f"xt{i}", [128, 2, D], F32) for i in range(2)]
        for i, nm in enumerate(("g1", "be1", "g2", "be2", "g3", "be3")):
            setattr(self, "r_" + nm, self.nc.alloc_sbuf_tensor_at(
                "r_" + nm + "s", [1, D], BF16, offset=base + i * 2048))
        self.xb_tm = ar.alloc("xb_tm", [128, D], F32)
        self.xb_bf = ar.alloc("xb_bf", [128, 2, D], BF16)
        self.xb_fm = ar.alloc("xb_fm", [128, NCH, 16 + 256], BF16)
        self.kn_bf = ar.alloc("kn_bf", [128, D], BF16)
        self.kn_fm = ar.alloc("kn_fm", [128, NCH, 256], BF16)
        self.kp_fm = ar.alloc("kp_fm", [128, NCH, 256], BF16)
        self.h1_fm = ar.alloc("h1_fm", [128, NCH, 256], BF16)
        self.hdd_tm = ar.alloc("hdd_tm", [128, 2, D], BF16)
        self.hdd_fm = ar.alloc("hdd_fm", [128, NCH, 256], BF16)
        self.diff_tm = ar.alloc("diff_tm", [128, 2, D], BF16)
        self.ktm = [ar.alloc(f"ktm{i}", [16, D], BF16) for i in range(2)]
        self.exps = ar.alloc("exps", [128, 144], F32)
        self.probs_bf = ar.alloc("probs_bf", [128, 144], BF16)
        self.pT_main = ar.alloc("pT_main", [128, 128], BF16)
        self.pT_halo = ar.alloc("pT_halo", [16, 128], BF16)
        self.sum_tm = ar.alloc("sum_tm", [128, D], F32)
        self.x1_bf = ar.alloc("x1_bf", [128, 2, D], BF16)
        self.xhalo_sb = self.xt[0][0:16, 0, :]
        self.xhalo_bf = self.xb_bf[0:16, 0, :]
        self.tile_end = []

    def dmy(self):
        self.S.op("dve", lambda e: e.tensor_scalar_mul(
            self.dmy_dve[:], self.dmy_dve[:], 1.0), inc="c_dve")

    def ln_stats(self, src_ap, n=D, eps=EPS_LN, want_rnorm=False, deps=(), parts=128):
        S, st = self.S, self.stat
        c = lambda i: st[0:parts, 4 * i:4 * i + 1]
        t1 = self.fenced("dve", lambda e: e.reduce_sum(c(5), src_ap,
                         axis=mybir.AxisListType.X), deps=deps)
        sqwar = [self.war[("sqscr",)]] if self.war.get(("sqscr",)) else []
        t2a = S.op("act", lambda e: e.activation(self.sqscr[0:parts, 0:n], src_ap,
                   AF.Square), deps=list(deps) + sqwar, inc="c_act")
        t2 = self.fenced("dve", lambda e: e.reduce_sum(c(6), self.sqscr[0:parts, 0:n],
                         axis=mybir.AxisListType.X), deps=[t2a])
        self.war[("sqscr",)] = t2
        S.op("dve", lambda e: e.scalar_tensor_tensor(c(7), c(5), 1.0 / n, c(5),
             ALU.mult, ALU.mult), inc="c_dve")
        S.op("dve", lambda e: e.tensor_scalar_mul(c(0), c(5), 1.0 / n), inc="c_dve")
        S.op("dve", lambda e: e.tensor_sub(c(4), c(6), c(7)), inc="c_dve")
        self.dmy()
        tv = self.fenced("dve", lambda e: e.tensor_scalar(c(9), c(4), 1.0 / n, eps,
                         ALU.mult, ALU.add))
        ts_ = self.fenced("act", lambda e: e.activation(c(7), c(9), AF.Sqrt), deps=[tv])
        tr = self.fenced("dve", lambda e: e.reciprocal(c(1), c(7)), deps=[ts_])
        if want_rnorm:
            tq = self.fenced("act", lambda e: e.activation(c(8), c(4), AF.Sqrt), deps=[tv])
            S.op("dve", lambda e: e.tensor_scalar(c(8), c(8), c(1), EPS_NORM,
                 ALU.mult, ALU.max), deps=[tq], inc="c_dve")
            self.dmy()
            tr = self.fenced("dve", lambda e: e.reciprocal(c(2), c(8)))
        return tr

    def transpose_tm_to_fm(self, src_tm_chunk, dst_fm, tcol, deps=(), eng="dve", war=None):
        emits = [lambda e, k=kc: e.transpose(
            self.psT[:, k * 128:(k + 1) * 128],
            src_tm_chunk[:, k * 128:(k + 1) * 128],
            self.sb_identbf[:]) for kc in range(NCH)]
        self.mm("psT", emits, deps=deps)
        wd = [] if not war else (list(war) if isinstance(war, list) else [war])
        return self.evac(eng, "psT", lambda e: ecopy(e,
            dst_fm[:, :, tcol:tcol + 128],
            self.psT[:].rearrange("p (c n) -> p c n", c=NCH)),
            deps=wd)

    def emit_halo_init(self, is_odd_core_input):
        S = self.S
        tl = S.op("act", lambda e: e.dma_start(out=self.xhalo_sb, in_=self.xhalo[:]),
                  deps=self.tok_perm, inc="q_act")
        tst = self.ln_stats(self.xhalo_sb, deps=[tl], parts=16)
        c = lambda i: self.stat[0:16, 4 * i:4 * i + 1]
        th = S.op("dve", lambda e: e.tensor_scalar(
            self.xhalo_bf, self.xhalo_sb, c(0), c(1), ALU.subtract, ALU.mult),
            deps=[tst], inc="c_dve")
        tk = S.op("dve", lambda e: e.tensor_copy(self.ktm[1][:], self.xhalo_bf),
                  inc="c_dve")
        self.war[("ktm", 1)] = None
        emits = [lambda e, k=kc: e.transpose(
            self.psT[:, k * 16:(k + 1) * 16],
            self.xb_bf[0:16, 0, k * 128:(k + 1) * 128],
            self.sb_identbf[0:16, 0:16]) for kc in range(NCH)]
        self.mm("psT", emits, deps=[th])
        te = self.evac("dve", "psT", lambda e: e.tensor_copy(
            self.xb_fm[:, :, 0:16],
            self.psT[:, 0:128].rearrange("p (c n) -> p c n", c=NCH)))
        self.war[("halo",)] = te
        self.war[("xt", 0)] = tk
        self.war[("xb_bf_init",)] = self.S.tok("c_pe")
        return tk

    def emit_front_tile(self, tile, x_src_ap, do_own=True, gchunk0=0):
        S, nc = self.S, self.nc
        st = self.stat
        c = lambda i: st[:, 4 * i:4 * i + 1]
        buf = tile % 2
        xt = self.xt[buf]
        prev_end = self.tile_end
        xdeps = ([self.war.get(("xt", buf))] if self.war.get(("xt", buf)) else []) \
            + list(self.tok_perm)
        if do_own:
            xload = S.op("sp", lambda e: e.dma_start(
                out=xt[:], in_=x_src_ap.rearrange("(c p) m -> p c m", p=128)),
                deps=xdeps, inc="q_sp")
        else:
            xload = S.op("sp", lambda e: e.dma_start(
                out=xt[:], in_=x_src_ap.rearrange("(c p) m -> p c m", p=128)),
                deps=xdeps, inc="q_sp")
        if do_own:
            if tile > 0:
                thc = S.op("dve", lambda e: e.tensor_copy(
                    self.xb_fm[:, :, 0:16], self.xb_fm[:, :, 256:272]),
                    deps=prev_end, inc="c_dve")
                self.war[("halo",)] = thc
        ttx1 = [None, None]
        for tc in range(2):
            gchunk = gchunk0 + tc
            tstat = self.ln_stats(xt[:, tc, :], want_rnorm=True,
                                  deps=[xload] + prev_end +
                                  ([self.war[("sqscr",)]] if self.war.get(("sqscr",)) else []))
            if do_own:
                txb = S.op("dve", lambda e, t=tc: e.tensor_scalar(
                    self.xb_tm[:], xt[:, t, :], c(0), c(1),
                    ALU.subtract, ALU.mult),
                    deps=[tstat] + ([self.war[("xb_tm",)]] if self.war.get(("xb_tm",)) else []),
                    inc="c_dve")
                txbb = S.op("act", lambda e, t=tc: e.activation(
                    self.xb_bf[:, t, :], self.xb_tm[:], AF.Copy),
                    deps=[txb] + ([self.war.pop(("xb_bf_init",))]
                                  if ("xb_bf_init",) in self.war else []),
                    inc="c_act")
                self.war[("xb_tm",)] = txbb
                tkn = S.op("dve", lambda e: e.tensor_scalar(
                    self.kn_bf[:], self.xb_tm[:], c(2), None, ALU.mult),
                    deps=[txb] + ([self.war[("kn_bf",)]] if self.war.get(("kn_bf",)) else []),
                    inc="c_dve")
            else:
                self.fenced("dve", lambda e: e.tensor_mul(c(4), c(1), c(2)),
                            deps=[tstat])
                if self.debug and hasattr(self, "snapbuf"):
                    k = self.snapidx; self.snapidx += 1
                    if k < self.snapbuf.shape[1] // 64:
                        S.op("dve", lambda e, k=k: e.tensor_copy(
                            self.snapbuf[:, k * 64:(k + 1) * 64], self.stat[:, 0:64]),
                            inc="c_dve")
                tkn = S.op("dve", lambda e, t=tc: e.tensor_scalar(
                    self.kn_bf[:], xt[:, t, :], c(0), c(4),
                    ALU.subtract, ALU.mult),
                    deps=[self.war[("kn_bf",)]] if self.war.get(("kn_bf",)) else (),
                    inc="c_dve")
                txbb = tkn
            self.war[("xt", buf)] = tkn
            tfmk = self.transpose_tm_to_fm(self.kn_bf[:], self.kn_fm, tc * 128,
                                           deps=[tkn], eng="act",
                                           war=self.war.get(("kn_fm",)))
            self.war[("kn_bf",)] = self.S.tok("c_pe")
            if do_own:
                tfmx = self.transpose_tm_to_fm(
                    self.xb_bf[:, tc, :], self.xb_fm, 16 + tc * 128,
                    deps=[txbb, self.war.get(("scores",), None) or ("c_pe", 0),
                          self.war.get(("halo",), None) or ("c_pe", 0)])
                emits = [lambda e, k=kc, t=tc: e.transpose(
                    self.psT[0:16, k * 128:(k + 1) * 128],
                    self.xb_fm[:, k, 16 + t * 128 + 112:16 + t * 128 + 128],
                    self.sb_identbf[:]) for kc in range(NCH)]
                self.mm("psT", emits,
                        deps=[tfmx] + ([self.war[("ktm", gchunk % 2)]]
                                       if self.war.get(("ktm", gchunk % 2)) else []))
                ttail = self.evac("dve", "psT", lambda e, g=gchunk: e.tensor_copy(
                    self.ktm[g % 2][:], self.psT[0:16, :]))
                mask = self.sb_maskF if gchunk == 0 else self.sb_maskA
                emits = [lambda e, k=kc, t=tc: e.matmul(
                    self.psB[:, 0:144],
                    self.xb_fm[:, k, 16 + t * 128:16 + t * 128 + 128],
                    self.xb_fm[:, k, t * 128:t * 128 + 144],
                    start=(k == 0), stop=(k == NCH - 1)) for kc in range(NCH)]
                tsc = self.mm("psB", emits, deps=[tfmx])
                self.war[("scores",)] = tsc
                tmk = self.evac("dve", "psB", lambda e, m=mask: e.scalar_tensor_tensor(
                    self.psB[:, 0:144], self.psB[:, 0:144], 1.0 / 32.0, m[:],
                    ALU.mult, ALU.add))
                tmx = self.fenced("dve", lambda e: e.reduce_max(
                    c(9), self.psB[:, 0:144], axis=mybir.AxisListType.X))
                tng = self.fenced("dve", lambda e: e.tensor_scalar_mul(c(10), c(9), -1.0))
                texp = S.op("act", lambda e: e.activation(
                    self.exps[:], self.psB[:, 0:144], AF.Exp, bias=c(10)),
                    deps=[tng] + ([self.war[("exps",)]] if self.war.get(("exps",)) else []),
                    inc="c_act")
                self.pslast["psB"]["rd"] = texp
                tsume = self.fenced("dve", lambda e: e.reduce_sum(
                    c(11), self.exps[:], axis=mybir.AxisListType.X), deps=[texp])
                trc = self.fenced("dve", lambda e: e.reciprocal(c(12), c(11)),
                                  deps=[tsume])
                tpb = S.op("dve", lambda e: e.tensor_scalar(
                    self.probs_bf[:], self.exps[:], c(12), None, ALU.mult), inc="c_dve")
                self.war[("exps",)] = tpb
                if self.debug and gchunk == 0 and hasattr(self, "dbg_p0"):
                    S.op("sp", lambda e: e.dma_start(out=self.dbg_p0[:], in_=self.probs_bf[:]),
                         deps=[tpb], inc="q_sp")
                    S.op("sp", lambda e: e.dma_start(out=self.dbg_s0[:], in_=self.exps[:]),
                         deps=[tpb], inc="q_sp")
                emits = [lambda e: e.transpose(self.psT[:, 0:128], self.probs_bf[:, 16:144],
                                               self.sb_identbf[:]),
                         lambda e: e.transpose(self.psT[0:16, 128:256], self.probs_bf[:, 0:16],
                                               self.sb_identbf[:])]
                self.mm("psT", emits, deps=[tpb])
                tpt = self.evac("dve", "psT", lambda e: e.tensor_copy(
                    self.pT_main[:], self.psT[:, 0:128]))
                tpt2 = S.op("dve", lambda e: e.tensor_copy(
                    self.pT_halo[:], self.psT[0:16, 128:256]), inc="c_dve")
                self.pslast["psT"]["rd"] = tpt2
                halo_src = self.ktm[(gchunk + 1) % 2]
                emits = []
                for hh in range(2):
                    emits.append(lambda e, h=hh, t=tc: e.matmul(
                        self.psD[:, h * 512:(h + 1) * 512], self.pT_main[:],
                        self.xb_bf[:, t, h * 512:(h + 1) * 512], start=True, stop=False))
                    emits.append(lambda e, h=hh, hs=halo_src: e.matmul(
                        self.psD[:, h * 512:(h + 1) * 512], self.pT_halo[:],
                        hs[:, h * 512:(h + 1) * 512], start=False, stop=True))
                tat = self.mm("psD", emits, deps=[tpt, tpt2, ttail])
                self.war[("ktm", (gchunk + 1) % 2)] = tat
                tsum = self.evac("dve", "psD", lambda e: e.tensor_add(
                    self.sum_tm[:], self.xb_tm[:], self.psD[:]),
                    deps=[self.war[("sum_tm",)]] if self.war.get(("sum_tm",)) else ())
                if self.debug and gchunk == 0 and hasattr(self, "dbg_sm0"):
                    S.op("sp", lambda e: e.dma_start(out=self.dbg_sm0[:], in_=self.sum_tm[:]),
                         deps=[tsum], inc="q_sp")
                tst2 = self.ln_stats(self.sum_tm[:], deps=[tsum] +
                    ([self.war[("sqscr",)]] if self.war.get(("sqscr",)) else []))
                self.war[("sum_tm",)] = self.S.tok("c_act")
                S.op("dve", lambda e: e.tensor_scalar(
                    self.sqscr[:], self.sum_tm[:], c(0), c(1),
                    ALU.subtract, ALU.mult), deps=[tst2], inc="c_dve")
                tg_ = S.op("dve", lambda e: e.tensor_mul(
                    self.sum_tm[:], self.sqscr[:], self.bc_g1[:]), inc="c_dve")
                self.war[("sqscr",)] = tg_
                ttx1[tc] = S.op("dve", lambda e, t=tc: e.tensor_add(
                    self.x1_bf[:, t, :], self.sum_tm[:], self.bc_be1[:]),
                    deps=[self.war[("x1sp",)]] if self.war.get(("x1sp",)) else (),
                    inc="c_dve")
        if do_own:
            tsp1 = S.op("sp", lambda e: e.dma_start(
                out=self.x1_spill[tile * 256:(tile + 1) * 256].rearrange(
                    "(c p) m -> p c m", p=128),
                in_=self.x1_bf[:]), deps=[ttx1[0], ttx1[1],
                                          self.war.get(("x1sp",)) or ("c_sp", 0)],
                inc="q_sp")
            self.war[("x1sp",)] = tsp1
        tknf = self.S.tok("c_act")
        for mc in range(NCH):
            ps, pn = (self.psA, "psA") if mc % 2 == 0 else (self.psB, "psB")
            emits = [lambda e, k=kc, m=mc, p=ps: e.matmul(
                p[:, 0:256], self.wWi[:, k, m * 128:(m + 1) * 128],
                self.kn_fm[:, k, :], start=(k == 0), stop=(k == NCH - 1))
                for kc in range(NCH)]
            self.mm(pn, emits, deps=[tfmk, self.war.get(("kp_fm",)) or ("c_pe", 0)]
                    if mc == 0 else ())
            self.evac("dve", pn, lambda e, m=mc, p=ps: e.tensor_scalar(
                self.kp_fm[:, m, :], p[:, 0:256], self.c_bi[:, m:m + 1], None, ALU.add),
                deps=[self.war.get(("kp_fm",)) or ("c_pe", 0)] if mc == 0 else ())
        for mc in range(NCH):
            ps, pn = (self.psA, "psA") if mc % 2 == 0 else (self.psB, "psB")
            emits = [lambda e, k=kc, m=mc, p=ps: e.matmul(
                p[:, 0:256], self.wWl1[:, k, m * 128:(m + 1) * 128],
                self.kp_fm[:, k, :], start=(k == 0), stop=(k == NCH - 1))
                for kc in range(NCH)]
            self.mm(pn, emits, deps=[self.war.get(("h1_fm",)) or ("c_pe", 0)]
                    if mc == 0 else ())
            self.evac("act", pn, lambda e, m=mc, p=ps: e.activation(
                self.h1_fm[:, m, :], p[:, 0:256], AF.Silu, bias=self.c_bl1[:, m:m + 1]),
                deps=[self.war.get(("h1_fm",)) or ("c_pe", 0)] if mc == 0 else ())
        self.war[("kn_fm",)] = self.S.tok("c_pe")
        tkp = self.S.tok("c_act")
        for tc in range(2):
            for hh in range(2):
                emits = [lambda e, k=kc, t=tc, h=hh: e.matmul(
                    self.psC[:, h * 512:(h + 1) * 512],
                    self.h1_fm[:, k, t * 128:(t + 1) * 128],
                    self.wWl2[:, k, h * 512:(h + 1) * 512],
                    start=(k == 0), stop=False) for kc in range(NCH)]
                emits.append(lambda e, h=hh: e.matmul(
                    self.psC[:, h * 512:(h + 1) * 512], self.sb_ones[0:1, 0:128],
                    self.r_bl2[0:1, h * 512:(h + 1) * 512], start=False, stop=True))
                self.mm("psC", emits,
                        deps=[tkp, self.war.get(("hdd_tm",)) or ("c_pe", 0)]
                        if (tc == 0 and hh == 0) else ())
            thdd = self.evac("act", "psC", lambda e, t=tc: e.activation(
                self.hdd_tm[:, t, :], self.psC[:], AF.Silu),
                deps=[self.war.get(("hdd_tm",)) or ("c_pe", 0)] if tc == 0 else ())
            self.transpose_tm_to_fm(self.hdd_tm[:, tc, :], self.hdd_fm, tc * 128,
                                    deps=[thdd],
                                    war=[t for t in (self.war.get(("hdd_fm",)),
                                                     self.war.get(("hddfm_sp",))) if t])
        self.war[("h1_fm",)] = self.S.tok("c_pe")
        thf = self.S.tok("c_dve")
        if do_own:
            tsp2 = S.op("sp", lambda e: e.dma_start(
                out=self.hdd_spill.rearrange("(c p) t -> p c t", p=128)[
                    :, :, tile * 256:(tile + 1) * 256],
                in_=self.hdd_fm[:]), deps=[thf, self.war.get(("hdsp",)) or ("c_sp", 0)],
                inc="q_sp")
            self.war[("hdsp",)] = tsp2
            self.war[("hddfm_sp",)] = tsp2
        for tc in range(2):
            for hh in range(2):
                emits = [lambda e, k=kc, t=tc, h=hh: e.matmul(
                    self.psD[:, h * 512:(h + 1) * 512],
                    self.hdd_fm[:, k, t * 128:(t + 1) * 128],
                    self.wmemW[:, k, h * 512:(h + 1) * 512],
                    start=(k == 0), stop=False) for kc in range(NCH)]
                emits.append(lambda e, h=hh: e.matmul(
                    self.psD[:, h * 512:(h + 1) * 512], self.sb_ones[0:1, 0:128],
                    self.r_memb[0:1, h * 512:(h + 1) * 512], start=False, stop=False))
                for mc in range(hh * 4, hh * 4 + 4):
                    emits.append(lambda e, t=tc, m=mc, h=hh: e.matmul(
                        self.psD[:, m * 128:(m + 1) * 128],
                        self.kp_fm[:, m, t * 128:(t + 1) * 128],
                        self.sb_negI[:], start=False, stop=(m == h * 4 + 3)))
                self.mm("psD", emits,
                        deps=[thf, self.war.get(("diff_tm",)) or ("c_pe", 0)]
                        if (tc == 0 and hh == 0) else ())
            self.evac("dve", "psD", lambda e, t=tc: e.tensor_copy(
                self.diff_tm[:, t, :], self.psD[:]),
                deps=[self.war.get(("diff_tm",)) or ("c_pe", 0)] if tc == 0 else ())
        self.war[("kp_fm",)] = self.S.tok("c_pe")
        tdiff = self.S.tok("c_dve")
        for bc in range(NCH):
            for hh in range(2):
                emits = [lambda e, t=tc, b=bc, h=hh: e.matmul(
                    self.psC[:, h * 512:(h + 1) * 512],
                    self.hdd_tm[:, t, b * 128:(b + 1) * 128],
                    self.diff_tm[:, t, h * 512:(h + 1) * 512],
                    start=(t == 0), stop=(t == 1)) for tc in range(2)]
                self.mm("psC", emits, deps=[tdiff] if (bc == 0 and hh == 0) else ())
            self.evac("dve", "psC", lambda e, b=bc: e.tensor_add(
                self.gradT[:, b, :], self.gradT[:, b, :], self.psC[:]))
        tgr = self.S.tok("c_pe")
        self.war[("hdd_tm",)] = tgr
        self.war[("diff_tm",)] = tgr
        self.war[("hdd_fm",)] = tgr
        self.tile_end = [self.S.tok("c_pe"), self.S.tok("c_dve"), self.S.tok("c_act")]
        return self.tile_end

    def emit_p_chain(self):
        S = self.S
        st = self.stat
        c = lambda i: st[0:16, 4 * i:4 * i + 1]
        tsq = S.op("act", lambda e: e.activation(self.sqscr[0:16, :], self.sb_P[:],
                   AF.Square), deps=self.tok_perm, inc="c_act")
        t6 = self.fenced("dve", lambda e: e.reduce_sum(c(6), self.sqscr[0:16, :],
                         axis=mybir.AxisListType.X), deps=[tsq])
        self.war[("sqscr",)] = t6
        ts_ = self.fenced("act", lambda e: e.activation(c(7), c(6), AF.Sqrt), deps=[t6])
        S.op("dve", lambda e: e.tensor_scalar_max(c(7), c(7), EPS_NORM), deps=[ts_],
             inc="c_dve")
        self.dmy()
        tr = self.fenced("dve", lambda e: e.reciprocal(c(2), c(7)))
        tkn = S.op("dve", lambda e: e.tensor_scalar(
            self.kn_bf[0:16, :], self.sb_P[:], c(2), None, ALU.mult),
            deps=[tr], inc="c_dve")
        emits = [lambda e, k=kc: e.transpose(
            self.psT[:, k * 16:(k + 1) * 16],
            self.kn_bf[0:16, k * 128:(k + 1) * 128],
            self.sb_identbf[0:16, 0:16]) for kc in range(NCH)]
        self.mm("psT", emits, deps=[tkn])
        tknf = self.evac("act", "psT", lambda e: ecopy(e,
            self.kn_fm[:, :, 0:16],
            self.psT[:, 0:128].rearrange("p (c n) -> p c n", c=NCH)))
        for mc in range(NCH):
            ps, pn = (self.psA, "psA") if mc % 2 == 0 else (self.psB, "psB")
            emits = [lambda e, k=kc, m=mc, p=ps: e.matmul(
                p[:, 0:16], self.wWi[:, k, m * 128:(m + 1) * 128],
                self.kn_fm[:, k, 0:16], start=(k == 0), stop=(k == NCH - 1))
                for kc in range(NCH)]
            self.mm(pn, emits, deps=[tknf] if mc == 0 else ())
            self.evac("dve", pn, lambda e, m=mc, p=ps: e.tensor_scalar(
                self.kp_fm[:, m, 0:16], p[:, 0:16], self.c_bi[:, m:m + 1],
                None, ALU.add))
        for mc in range(NCH):
            ps, pn = (self.psA, "psA") if mc % 2 == 0 else (self.psB, "psB")
            emits = [lambda e, k=kc, m=mc, p=ps: e.matmul(
                p[:, 0:16], self.wWl1[:, k, m * 128:(m + 1) * 128],
                self.kp_fm[:, k, 0:16], start=(k == 0), stop=(k == NCH - 1))
                for kc in range(NCH)]
            self.mm(pn, emits)
            self.evac("act", pn, lambda e, m=mc, p=ps: e.activation(
                self.h1_fm[:, m, 0:16], p[:, 0:16], AF.Silu,
                bias=self.c_bl1[:, m:m + 1]))
        for hh in range(2):
            emits = [lambda e, k=kc, h=hh: e.matmul(
                self.psC[0:16, h * 512:(h + 1) * 512],
                self.h1_fm[:, k, 0:16],
                self.wWl2[:, k, h * 512:(h + 1) * 512],
                start=(k == 0), stop=False) for kc in range(NCH)]
            emits.append(lambda e, h=hh: e.matmul(
                self.psC[0:16, h * 512:(h + 1) * 512], self.sb_ones[0:1, 0:16],
                self.r_bl2[0:1, h * 512:(h + 1) * 512], start=False, stop=True))
            self.mm("psC", emits)
        thdd = self.evac("act", "psC", lambda e: e.activation(
            self.hdd_tm[0:16, 0, :], self.psC[0:16, :], AF.Silu))
        emits = [lambda e, k=kc: e.transpose(
            self.psT[:, k * 16:(k + 1) * 16],
            self.hdd_tm[0:16, 0, k * 128:(k + 1) * 128],
            self.sb_identbf[0:16, 0:16]) for kc in range(NCH)]
        self.mm("psT", emits, deps=[thdd])
        thf = self.evac("dve", "psT", lambda e: ecopy(e,
            self.hdd_fm[:, :, 0:16],
            self.psT[:, 0:128].rearrange("p (c n) -> p c n", c=NCH)))
        for hh in range(2):
            emits = [lambda e, k=kc, h=hh: e.matmul(
                self.psD[0:16, h * 512:(h + 1) * 512],
                self.hdd_fm[:, k, 0:16],
                self.wmemW[:, k, h * 512:(h + 1) * 512],
                start=(k == 0), stop=False) for kc in range(NCH)]
            emits.append(lambda e, h=hh: e.matmul(
                self.psD[0:16, h * 512:(h + 1) * 512], self.sb_ones[0:1, 0:16],
                self.r_memb[0:1, h * 512:(h + 1) * 512], start=False, stop=False))
            for mc in range(hh * 4, hh * 4 + 4):
                emits.append(lambda e, m=mc, h=hh: e.matmul(
                    self.psD[0:16, m * 128:(m + 1) * 128],
                    self.kp_fm[:, m, 0:16],
                    self.sb_negI[:], start=False, stop=(m == hh * 4 + 3)))
            self.mm("psD", emits, deps=[thf] if hh == 0 else ())
        tdiff = self.evac("dve", "psD", lambda e: e.tensor_copy(
            self.diff_tm[0:16, 0, :], self.psD[0:16, :]))
        tds = S.op("dve", lambda e: e.tensor_scalar(
            self.diff_tm[0:16, 0, :], self.diff_tm[0:16, 0, :],
            self.sb_pscale[0:16, 0:1], None, ALU.mult), inc="c_dve")
        for bc in range(NCH):
            for hh in range(2):
                emits = [lambda e, b=bc, h=hh: e.matmul(
                    self.psC[:, h * 512:(h + 1) * 512],
                    self.hdd_tm[0:16, 0, b * 128:(b + 1) * 128],
                    self.diff_tm[0:16, 0, h * 512:(h + 1) * 512],
                    start=True, stop=True)]
                self.mm("psC", emits, deps=[tds] if (bc == 0 and hh == 0) else ())
            self.evac("dve", "psC", lambda e, b=bc: e.tensor_add(
                self.gradT[:, b, :], self.gradT[:, b, :], self.psC[:]))
        tok = [self.S.tok("c_pe"), self.S.tok("c_dve"), self.S.tok("c_act")]
        for k in ("kn_fm", "kp_fm", "h1_fm", "hdd_tm", "diff_tm", "hdd_fm", "kn_bf"):
            self.war[(k,)] = self.S.tok("c_pe")
        self.tile_end = tok

    def alloc_stage2(self):
        self.ar.release()
        self.wWnew = self.ar.at("wWnew", [128, NCH, D], BF16, "wWi")
        self.ar.mark()
        self.st2_ga = self.ar.alloc("st2_ga", [128, NCH, D], F32)

    def emit_pair_allreduce(self):
        S = self.S
        toks = [self.S.tok("c_pe"), self.S.tok("c_dve"), self.S.tok("c_act"),
                self.S.tok("q_sp"), self.S.tok("q_pool")]
        for cch in range(NCH):
            S.op("pool", lambda e, cc=cch: e.indirect_dma_start(
                out=self.gr_slots[:],
                out_offset=IndirectOffsetOnAxis(self.sb_sig8[:, cc:cc + 1], 0),
                in_=self.gradT[:, cc, :], in_offset=None),
                deps=toks if cch == 0 else (), inc="q_pool")
        tw = self.S.tok("q_pool")
        S.op("pool", lambda e: e.indirect_dma_start(
            out=self.gr_flags[:],
            out_offset=IndirectOffsetOnAxis(self.sb_fio[:], 0),
            in_=self.sb_tok2[:], in_offset=None),
            deps=[tw], inc="q_pool")
        tf = self.S.tok("q_pool")
        def poll(e):
            with (e.register("rcond") as rcond, e.register("rexp") as rexp,
                  e.register("ra") as ra, e.register("rb") as rb,
                  e.register("rt0") as rt0, e.register("rt1") as rt1):
                e.reg_load(rt0, self.sb_tok2[0:1, 0:1])
                e.reg_load(rt1, self.sb_tok2[0:1, 1:2])
                e.reg_mov(rcond, 1)
                e.reg_mov(rexp, 0)
                with e.While(rcond):
                    e.indirect_dma_start(
                        out=self.sb_fbuf[:], out_offset=None,
                        in_=self.gr_flags[:],
                        in_offset=IndirectOffsetOnAxis(self.sb_fip[:], 0),
                    ).then_inc(self.S.sem["q_pf"], 16)
                    e.reg_add(rexp, rexp, 16)
                    e.wait_ge(self.S.sem["q_pf"], rexp)
                    e.reg_load(ra, self.sb_fbuf[0:1, 0:1])
                    e.reg_load(rb, self.sb_fbuf[0:1, 1:2])
                    e.reg_alu(ra, ra, rt0, ALU.not_equal)
                    e.reg_alu(rb, rb, rt1, ALU.not_equal)
                    e.reg_alu(rcond, ra, rb, ALU.bitwise_or)
            return e.nop()
        tp = S.op("pool", poll, deps=[tf], inc="c_pool")
        ga = self.st2_ga
        for cch in range(NCH):
            S.op("pool", lambda e, cc=cch: e.indirect_dma_start(
                out=ga[:, cc, :], out_offset=None,
                in_=self.gr_slots[:],
                in_offset=IndirectOffsetOnAxis(self.sb_gig8[:, cc:cc + 1], 0)),
                deps=[tp], inc="q_pool")
        tg = self.S.tok("q_pool")
        tsum = S.op("dve", lambda e: e.tensor_add(self.gradT[:], self.gradT[:], ga[:]),
                    deps=[tg], inc="c_dve")
        cgrad = 2.0 / (4 * 8202 * 1024)
        s1 = self.stat[:, 0:1]
        s2 = self.stat[:, 4:5]
        S.op("dve", lambda e: e.tensor_scalar(s1, self.sb_fg[:], -1.0, 1.0,
             ALU.mult, ALU.add), deps=[tsum], inc="c_dve")
        S.op("dve", lambda e: e.tensor_scalar_mul(s2, self.sb_lr[:], cgrad), inc="c_dve")
        self.dmy()
        for cch in range(NCH):
            S.op("dve", lambda e, cc=cch: e.tensor_scalar(
                self.sqscr[:], self.wmemW[:, cc, :], s1, None, ALU.mult), inc="c_dve")
            twn = S.op("dve", lambda e, cc=cch: e.scalar_tensor_tensor(
                self.wWnew[:, cc, :], self.gradT[:, cc, :], s2, self.sqscr[:],
                ALU.mult, ALU.add), inc="c_dve")
        return twn

    def alloc_phase2a(self):
        ar = self.ar
        self.wWo = ar.at("wWo", [128, NCH, D], BF16, "wmemW")
        self.wWg = ar.at("wWg", [128, 2 * NCH, D], BF16, "wWl1")
        _x1 = ar.alloc("a_x10", [128, 4, D], BF16)
        self.a_x1 = [_x1, _x1]
        _hd = ar.alloc("a_hdd0", [128, NCH, 512], BF16)
        self.a_hdd = [_hd, _hd]
        self.a_mout = ar.at("a_mout", [128, NCH, 512], BF16, "gradT")
        self.a_mofm = ar.alloc("a_mofm", [128, NCH, 512], BF16)
        self.a_motm = ar.alloc("a_motm", [128, 4, D], BF16)
        self.a_x1fm = ar.alloc("a_x1fm", [128, NCH, 512], BF16)
        self.a_gate = ar.alloc("a_gate", [128, 4, D], BF16)
        self.a_x2tm = ar.alloc("a_x2tm", [128, 4, D], BF16)
        self.a_x2fm = ar.alloc("a_x2fm", [128, NCH, 512], BF16)

    def load_phase2a_weights(self):
        S = self.S
        deps = list(self.tok_w2a) if hasattr(self, "tok_w2a") else []
        t1 = S.op("sp", lambda e: e.dma_start(
            out=self.wWo[:], in_=self.WoT.rearrange("(c p) m -> p c m", p=128)),
            deps=deps, inc="q_sp")
        t2 = S.op("sp", lambda e: e.dma_start(
            out=self.wWg[:], in_=self.WgT.rearrange("(c p) m -> p c m", p=128)),
            inc="q_sp")
        self.tok_w2a_done = [t1, t2]

    def emit_tile_2a(self, tile):
        S = self.S
        st = self.stat
        c = lambda i: st[:, 4 * i:4 * i + 1]
        buf = tile % 2
        x1 = self.a_x1[buf]
        hdd = self.a_hdd[buf]
        r0, r1 = tile * 512, (tile + 1) * 512
        def wl(key):
            v = self.war.get(key)
            if not v:
                return []
            return list(v) if isinstance(v, list) else [v]
        tlx = S.op("sp", lambda e: e.dma_start(
            out=x1[:], in_=self.x1_spill[r0:r1].rearrange("(c p) m -> p c m", p=128)),
            deps=wl(("a_x1", buf)) + self.tok_w2a_done, inc="q_sp")
        tlh = S.op("sp", lambda e: e.dma_start(
            out=hdd[:], in_=self.hdd_spill.rearrange("(c p) t -> p c t", p=128)[:, :, r0:r1]),
            deps=[self.war.get(("a_hdd", buf)) or ("c_pe", 0)], inc="q_sp")
        for mc in range(NCH):
            ps, pn = (self.psA, "psA") if mc % 2 == 0 else (self.psB, "psB")
            emits = [lambda e, k=kc, m=mc, p=ps: e.matmul(
                p[:], self.wWnew[:, k, m * 128:(m + 1) * 128],
                hdd[:, k, :], start=(k == 0), stop=(k == NCH - 1))
                for kc in range(NCH)]
            self.mm(pn, emits, deps=[tlh, self.war.get(("a_mout",)) or ("c_pe", 0)]
                    if mc == 0 else ())
            self.evac("dve", pn, lambda e, m=mc, p=ps: e.tensor_scalar(
                self.a_mout[:, m, :], p[:], self.c_memb[:, m:m + 1], None, ALU.add))
        tmo0 = self.S.tok("c_dve")
        for mc in range(NCH):
            ps, pn = (self.psA, "psA") if mc % 2 == 0 else (self.psB, "psB")
            emits = [lambda e, k=kc, m=mc, p=ps: e.matmul(
                p[:], self.wWo[:, k, m * 128:(m + 1) * 128],
                self.a_mout[:, k, :], start=(k == 0), stop=(k == NCH - 1))
                for kc in range(NCH)]
            self.mm(pn, emits, deps=[tmo0, self.war.get(("a_mofm",)) or ("c_pe", 0)]
                    if mc == 0 else ())
            self.evac("dve", pn, lambda e, m=mc, p=ps: e.tensor_scalar(
                self.a_mofm[:, m, :], p[:], self.c_bo[:, m:m + 1], None, ALU.add))
        tmofm = self.S.tok("c_dve")
        for tc in range(4):
            self.transpose_tm_to_fm(x1[:, tc, :], self.a_x1fm, tc * 128,
                                    deps=[tlx], eng="dve",
                                    war=self.war.get(("a_x1fm",)))
        tx1f = self.S.tok("c_dve")
        for tc in range(4):
            emits = [lambda e, k=kc, t=tc: e.transpose(
                self.psT[:, k * 128:(k + 1) * 128],
                self.a_mofm[:, k, t * 128:(t + 1) * 128],
                self.sb_identbf[:]) for kc in range(NCH)]
            self.mm("psT", emits, deps=[tmofm] if tc == 0 else ())
            self.evac("dve", "psT", lambda e, t=tc: e.tensor_copy(
                self.a_motm[:, t, :], self.psT[:]),
                deps=[self.war.get(("a_motm",)) or ("c_pe", 0)] if tc == 0 else ())
        tmotm = self.S.tok("c_dve")
        for tc in range(4):
            for hh in range(2):
                emits = []
                for kc in range(NCH):
                    emits.append(lambda e, k=kc, t=tc, h=hh: e.matmul(
                        self.psC[:, h * 512:(h + 1) * 512],
                        self.a_x1fm[:, k, t * 128:(t + 1) * 128],
                        self.wWg[:, k, h * 512:(h + 1) * 512],
                        start=(k == 0), stop=False))
                for kc in range(NCH):
                    emits.append(lambda e, k=kc, t=tc, h=hh: e.matmul(
                        self.psC[:, h * 512:(h + 1) * 512],
                        self.a_mofm[:, k, t * 128:(t + 1) * 128],
                        self.wWg[:, NCH + k, h * 512:(h + 1) * 512],
                        start=False, stop=False))
                emits.append(lambda e, h=hh: e.matmul(
                    self.psC[:, h * 512:(h + 1) * 512], self.sb_ones[0:1, 0:128],
                    self.r_bg[0:1, h * 512:(h + 1) * 512], start=False, stop=True))
                self.mm("psC", emits, deps=[tx1f, tmofm] if (tc == 0 and hh == 0) else ())
            self.evac("act", "psC", lambda e, t=tc: e.activation(
                self.a_gate[:, t, :], self.psC[:], AF.Sigmoid))
        tgate = self.S.tok("c_act")
        tx2 = [None] * 4
        for tc in range(4):
            t1 = S.op("dve", lambda e, t=tc: e.tensor_sub(
                self.a_scr1[:], x1[:, t, :], self.a_motm[:, t, :]),
                deps=[tmotm, tgate] if tc == 0 else (), inc="c_dve")
            S.op("dve", lambda e, t=tc: e.tensor_mul(
                self.a_scr1[:], self.a_scr1[:], self.a_gate[:, t, :]), inc="c_dve")
            S.op("dve", lambda e, t=tc: e.tensor_add(
                self.a_scr2[:], x1[:, t, :], self.a_motm[:, t, :]), inc="c_dve")
            tsum = S.op("dve", lambda e: e.tensor_add(
                self.a_scr2[:], self.a_scr2[:], self.a_scr1[:]), inc="c_dve")
            tst = self.ln_stats(self.a_scr2[:], deps=[tsum])
            S.op("dve", lambda e: e.tensor_scalar(
                self.sqscr[:], self.a_scr2[:], c(0), c(1), ALU.subtract, ALU.mult),
                deps=[tst], inc="c_dve")
            S.op("dve", lambda e: e.tensor_mul(self.a_scr2[:], self.sqscr[:],
                                               self.bc_g2[:]), inc="c_dve")
            self.war[("sqscr",)] = self.S.tok("c_dve")
            tx2[tc] = S.op("dve", lambda e, t=tc: e.tensor_add(
                self.a_x2tm[:, t, :], self.a_scr2[:], self.bc_be2[:]),
                deps=[self.war.get(("a_x2tm",)) or ("c_pe", 0)] if tc == 0 else (),
                inc="c_dve")
            self.transpose_tm_to_fm(self.a_x2tm[:, tc, :], self.a_x2fm, tc * 128,
                                    deps=[tx2[tc]], eng="act",
                                    war=[t for t in (self.war.get(("a_x2fm",)),) if t])
        tx2f = self.S.tok("c_act")
        ts1 = S.op("sp", lambda e: e.dma_start(
            out=self.x2tm_spill[r0:r1].rearrange("(c p) m -> p c m", p=128),
            in_=self.a_x2tm[:]), deps=[tx2[3], self.war.get(("x2sp",)) or ("c_sp", 0)],
            inc="q_sp")
        ts2 = S.op("sp", lambda e: e.dma_start(
            out=self.x2fm_spill.rearrange("(c p) t -> p c t", p=128)[:, :, r0:r1],
            in_=self.a_x2fm[:]), deps=[tx2f], inc="q_sp")
        self.war[("x2sp",)] = ts2
        tpe = self.S.tok("c_pe")
        self.war[("a_x1", buf)] = [self.S.tok("c_dve"), tpe]
        self.war[("a_hdd", buf)] = tpe
        self.war[("a_mout",)] = tpe
        self.war[("a_mofm",)] = tpe
        self.war[("a_motm",)] = self.S.tok("c_dve")
        self.war[("a_x1fm",)] = tpe
        self.war[("a_x2fm",)] = ts2
        self.war[("a_x2tm",)] = ts1
        self.war[("a_gate",)] = self.S.tok("c_dve")


    def alloc_phase2bi(self):
        ar = self.ar
        self.wW1 = ar.at("wW1", [128, NCH, FF], BF16, "wWi")
        self.wW2a = ar.at("wW2a", [128, 16, D], BF16, "gradT")
        self.wW2b = ar.alloc("wW2b", [128, 16, D], BF16)
        _s = ar.alloc("b_x2f0", [128, NCH, 512], BF16)
        self.b_x2f = [_s, _s]
        self.b_h = ar.alloc("b_h", [128, 32, 256], BF16)
        self.b_ff = ar.alloc("b_ff", [128, 4, D], BF16)

    def load_phase2bi_weights(self, deps):
        S = self.S
        t1 = S.op("sp", lambda e: e.dma_start(
            out=self.wW1[:], in_=self.W1T.rearrange("(c p) m -> p c m", p=128)),
            deps=deps, inc="q_sp")
        t2 = S.op("sp", lambda e: e.dma_start(
            out=self.wW2a[:], in_=self.W2T[0:2048].rearrange("(c p) m -> p c m", p=128)),
            deps=deps, inc="q_sp")
        t3 = S.op("sp", lambda e: e.dma_start(
            out=self.wW2b[:], in_=self.W2T[2048:4096].rearrange("(c p) m -> p c m", p=128)),
            deps=deps, inc="q_sp")
        self.tok_w2b = [t1, t2, t3]

    def emit_tile_2bi(self, tile):
        S = self.S
        buf = tile % 2
        x2f = self.b_x2f[buf]
        r0, r1 = tile * 512, (tile + 1) * 512
        tlx = S.op("sp", lambda e: e.dma_start(
            out=x2f[:], in_=self.x2fm_spill.rearrange("(c p) t -> p c t", p=128)[:, :, r0:r1]),
            deps=[self.war.get(("b_x2f", buf)) or ("c_pe", 0)] + self.tok_w2b,
            inc="q_sp")
        for th in range(2):
            ts0 = th * 256
            for hc in range(32):
                ps, pn = (self.psA, "psA") if hc % 2 == 0 else (self.psB, "psB")
                emits = [lambda e, k=kc, h=hc, p=ps, t0=ts0: e.matmul(
                    p[:, 0:256], self.wW1[:, k, h * 128:(h + 1) * 128],
                    x2f[:, k, t0:t0 + 256], start=(k == 0), stop=(k == NCH - 1))
                    for kc in range(NCH)]
                self.mm(pn, emits,
                        deps=[tlx, self.war.get(("b_h",)) or ("c_pe", 0)]
                        if hc == 0 else ())
                self.evac("act", pn, lambda e, h=hc, p=ps: e.activation(
                    self.b_h[:, h, :], p[:, 0:256], AF.Silu,
                    bias=self.c_b1[:, h:h + 1]))
            th_tok = self.S.tok("c_act")
            for tc2 in range(2):
                for hh in range(2):
                    emits = []
                    for kc in range(32):
                        w = self.wW2a if kc < 16 else self.wW2b
                        emits.append(lambda e, k=kc, t=tc2, h=hh, ww=w: e.matmul(
                            self.psC[:, h * 512:(h + 1) * 512],
                            self.b_h[:, k, t * 128:(t + 1) * 128],
                            ww[:, k % 16, h * 512:(h + 1) * 512],
                            start=(k == 0), stop=False))
                    emits.append(lambda e, h=hh: e.matmul(
                        self.psC[:, h * 512:(h + 1) * 512], self.sb_ones[0:1, 0:128],
                        self.r_b2[0:1, h * 512:(h + 1) * 512], start=False, stop=True))
                    self.mm("psC", emits, deps=[th_tok] if (tc2 == 0 and hh == 0) else ())
                self.evac("dve", "psC", lambda e, s=th * 2 + tc2: e.tensor_copy(
                    self.b_ff[:, s, :], self.psC[:]),
                    deps=[self.war.get(("b_ff",)) or ("c_pe", 0)]
                    if (th == 0 and tc2 == 0) else ())
            self.war[("b_h",)] = self.S.tok("c_pe")
        tffe = self.S.tok("c_dve")
        tsp = S.op("sp", lambda e: e.dma_start(
            out=self.ff_spill[r0:r1].rearrange("(c p) m -> p c m", p=128),
            in_=self.b_ff[:]), deps=[tffe, self.war.get(("ffsp",)) or ("c_sp", 0)],
            inc="q_sp")
        self.war[("ffsp",)] = tsp
        self.war[("b_ff",)] = tsp
        self.war[("b_x2f", buf)] = self.S.tok("c_pe")

    def alloc_phase2bii(self):
        ar = self.ar
        self.t_ff = ar.alloc("t_ff", [128, 4, D], BF16)
        self.t_x2 = ar.alloc("t_x2", [128, 4, D], BF16)
        self.t_x = ar.alloc("t_x", [128, 4, D], F32)
        self.t_y1 = ar.alloc("t_y1", [128, 4, D], F32)
        self.t_x4f = ar.alloc("t_x4f", [128, NCH, 512], BF16)
        self.t_xe = []
        self.t_xef = []
        for k in range(4):
            t = ar.alloc(f"t_xe{k}", [128, XW], BF16)
            f = self.nc.alloc_sbuf_tensor_at(f"t_xef{k}", [128, XW // 2], F32,
                                             offset=ar.names[f"t_xe{k}"])
            self.t_xe.append(t)
            self.t_xef.append(f)
        self.rt_pr = ar.alloc("rt_pr", [128, E], F32)
        self.rt_cur = ar.alloc("rt_cur", [128, E], F32)
        self.rt_oh = ar.alloc("rt_oh", [128, 4, E], F32)
        self.rt_w = ar.alloc("rt_w", [128, 8], F32)
        self.rt_selbf = ar.alloc("rt_selbf", [128, E], BF16)
        self.rt_selem = ar.alloc("rt_selem", [32, 128], BF16)
        self.rt_sv = ar.alloc("rt_sv", [32, 128], F32)
        self.rt_ovf = ar.alloc("rt_ovf", [32, 128], F32)
        self.rt_slottm = ar.alloc("rt_slottm", [128, E], F32)
        self.rt_scr = ar.alloc("rt_scr", [128, E], F32)
        self.rt_base = ar.alloc("rt_base", [32, 4], F32)
        self.rt_dsti = ar.alloc("rt_dsti", [128, 4], I32)
        self.rt_exp = ar.alloc("rt_exp", [128, E], F32)

    def emit_tile_2bii(self, tile, first):
        S = self.S
        st = self.stat
        c = lambda i: st[:, 4 * i:4 * i + 1]
        r0, r1 = tile * 512, (tile + 1) * 512
        def wl(key):
            v = self.war.get(key)
            return ([] if not v else (list(v) if isinstance(v, list) else [v]))
        tlf = S.op("sp", lambda e: e.dma_start(
            out=self.t_ff[:], in_=self.ff_spill[r0:r1].rearrange("(c p) m -> p c m", p=128)),
            deps=wl(("t_ff",)) + (self.tok_w2bii if first else []), inc="q_sp")
        tlx2 = S.op("sp", lambda e: e.dma_start(
            out=self.t_x2[:], in_=self.x2tm_spill[r0:r1].rearrange("(c p) m -> p c m", p=128)),
            deps=wl(("t_x2",)), inc="q_sp")
        tlx = S.op("sp", lambda e: e.dma_start(
            out=self.t_x[:], in_=self.x_own[r0:r1].rearrange("(c p) m -> p c m", p=128)),
            deps=wl(("t_x",)), inc="q_sp")
        if first:
            S.op("dve", lambda e: e.memset(self.rt_base[:], 0.0), inc="c_dve")
            self.dmy()
            S.op("pool", lambda e: (setattr(self, "breg_xbuf", e.to_reg(E * CAP - 1)),
                                    setattr(self, "breg_yp", e.to_reg(4 * self.T_OWN - 1)),
                                    e.nop())[-1], inc="c_pool")
        for tc in range(4):
            gtok = tile * 512 + tc * 128
            t1 = S.op("dve", lambda e, t=tc: e.tensor_add(
                self.a_scr1[:], self.t_x2[:, t, :], self.t_ff[:, t, :]),
                deps=[tlf, tlx2] if tc == 0 else (), inc="c_dve")
            tst = self.ln_stats(self.a_scr1[:], deps=[t1])
            S.op("dve", lambda e: e.tensor_scalar(
                self.sqscr[:], self.a_scr1[:], c(0), c(1), ALU.subtract, ALU.mult),
                deps=[tst], inc="c_dve")
            S.op("dve", lambda e: e.tensor_mul(self.a_scr1[:], self.sqscr[:],
                                               self.bc_g3[:]), inc="c_dve")
            t3 = S.op("dve", lambda e: e.tensor_add(self.a_scr1[:], self.a_scr1[:],
                                                    self.bc_be3[:]), inc="c_dve")
            self.war[("sqscr",)] = t3
            tst2 = self.ln_stats(self.a_scr1[:], deps=[t3])
            ty1 = S.op("dve", lambda e, t=tc: e.tensor_scalar(
                self.t_y1[:, t, :], self.a_scr1[:], c(0), c(1),
                ALU.subtract, ALU.mult),
                deps=[tst2] + (wl(("t_y1b",)) if tc == 0 else []), inc="c_dve")
            ty1b = S.op("dve", lambda e, t=tc: e.tensor_add(
                self.t_y1[:, t, :], self.t_y1[:, t, :], self.t_x[:, t, :]),
                deps=[tlx] if tc == 0 else (), inc="c_dve")
            tst3 = self.ln_stats(self.t_y1[:, tc, :], deps=[ty1b])
            tx4 = S.op("dve", lambda e, t=tc: e.tensor_scalar(
                self.t_xe[0][:, 0:D], self.t_y1[:, t, :], c(0), c(1),
                ALU.subtract, ALU.mult),
                deps=[tst3] + wl(("t_xe",)), inc="c_dve")
            for k in range(1, 4):
                S.op("dve", lambda e, k=k: e.tensor_copy(
                    self.t_xe[k][:, 0:D], self.t_xe[0][:, 0:D]), inc="c_dve")
            txe = self.S.tok("c_dve")
            self.transpose_tm_to_fm(self.t_xe[0][:, 0:D], self.t_x4f, tc * 128,
                                    deps=[txe], eng="act",
                                    war=wl(("t_x4f",)) if tc == 0 else None)
            tx4f = self.S.tok("c_act")
            emits = [lambda e, k=kc, t=tc: e.matmul(
                self.psE[:, 0:E],
                self.t_x4f[:, k, t * 128:(t + 1) * 128],
                self.wrWg[:, k, :], start=(k == 0), stop=(k == NCH - 1))
                for kc in range(NCH)]
            self.mm("psE", emits, deps=[tx4f])
            tmx = self.fenced("dve", lambda e: e.reduce_max(
                c(9), self.psE[:, 0:E], axis=mybir.AxisListType.X))
            tng = self.fenced("dve", lambda e: e.tensor_scalar_mul(c(10), c(9), -1.0))
            texp = S.op("act", lambda e: e.activation(
                self.rt_exp[:], self.psE[:, 0:E], AF.Exp, bias=c(10)),
                deps=[tng], inc="c_act")
            self.pslast["psE"]["rd"] = texp
            tsm = self.fenced("dve", lambda e: e.reduce_sum(
                c(11), self.rt_exp[:], axis=mybir.AxisListType.X), deps=[texp])
            trc = self.fenced("dve", lambda e: e.reciprocal(c(12), c(11)), deps=[tsm])
            tpr = S.op("dve", lambda e: e.tensor_scalar(
                self.rt_pr[:], self.rt_exp[:], c(12), None, ALU.mult), inc="c_dve")
            self.dmy()
            S.op("dve", lambda e: e.tensor_copy(self.rt_cur[:], self.rt_pr[:]),
                 inc="c_dve")
            for k in range(4):
                self.dmy()
                tmk = self.fenced("dve", lambda e: e.reduce_max(
                    c(9), self.rt_cur[:], axis=mybir.AxisListType.X))
                S.op("dve", lambda e, k=k: e.tensor_copy(self.rt_w[:, k:k + 1], c(9)),
                     inc="c_dve")
                S.op("dve", lambda e, k=k: e.tensor_scalar(
                    self.rt_oh[:, k, :], self.rt_cur[:], c(9), None, ALU.is_ge),
                    inc="c_dve")
                if k < 3:
                    self.dmy()
                    S.op("dve", lambda e, k=k: e.scalar_tensor_tensor(
                        self.rt_cur[:], self.rt_oh[:, k, :], -1e30, self.rt_cur[:],
                        ALU.mult, ALU.add), inc="c_dve")
            self.dmy()
            tsel = S.op("dve", lambda e: e.tensor_scalar(
                self.rt_selbf[:], self.rt_pr[:], c(9), None, ALU.is_ge), inc="c_dve")
            emits = [lambda e: e.transpose(self.psT[0:E, 0:128], self.rt_selbf[:],
                                           self.sb_identbf[:])]
            self.mm("psT", emits, deps=[tsel])
            tse = self.evac("dve", "psT", lambda e: e.tensor_copy(
                self.rt_selem[:], self.psT[0:E, 0:128]))
            emits = [lambda e: e.matmul(self.psA[0:E, 0:128], self.rt_selbf[:],
                                        self.sb_ltri128[:], start=True, stop=True)]
            self.mm("psA", emits, deps=[tsel])
            tsv = self.evac("dve", "psA", lambda e: e.tensor_scalar(
                self.rt_sv[:], self.psA[0:E, 0:128], self.rt_base[0:E, 0:1], None,
                ALU.add))
            self.dmy()
            tbu = S.op("dve", lambda e: e.tensor_add(
                self.rt_base[0:E, 0:1], self.rt_sv[:, 127:128],
                self.rt_selem[0:E, 127:128]), deps=[tse, tsv], inc="c_dve")
            tov = S.op("dve", lambda e: e.tensor_scalar(
                self.rt_ovf[:], self.rt_sv[:], float(CAP) - 0.5, None, ALU.is_ge),
                inc="c_dve")
            S.op("dve", lambda e: e.tensor_scalar(
                self.rt_sv[:], self.rt_sv[:], self.sb_ebase[0:E, 0:1], None, ALU.add),
                inc="c_dve")
            self.dmy()
            tslot = S.op("dve", lambda e: e.scalar_tensor_tensor(
                self.rt_sv[:], self.rt_ovf[:], 1e6, self.rt_sv[:],
                ALU.mult, ALU.add), inc="c_dve")
            emits = [lambda e: e.transpose(self.psE[:, 32:64], self.rt_sv[:],
                                           self.sb_identf32[0:E, 0:E])]
            self.mm("psE", emits, deps=[tslot])
            tstm = self.evac("dve", "psE", lambda e: e.tensor_copy(
                self.rt_slottm[:], self.psE[:, 32:64]))
            emits = [lambda e: e.matmul(self.psB[:, 0:E], self.rt_selem[:],
                                        self.sb_ltri32[:], start=True, stop=True)]
            self.mm("psB", emits, deps=[tse])
            ttk = self.fenced("dve", lambda e, g=gtok: e.tensor_scalar(
                c(13), self.sb_tokp[:, 0:1], 1.0, float(g), ALU.mult, ALU.add))
            todst = self.evac("dve", "psB", lambda e: e.tensor_scalar(
                self.rt_scr[:], self.psB[:, 0:E], float(self.T_OWN), c(13),
                ALU.mult, ALU.add), deps=[ttk])
            for k in range(4):
                S.op("dve", lambda e, k=k: e.tensor_mul(
                    self.rt_cur[:], self.rt_slottm[:], self.rt_oh[:, k, :]),
                    deps=[tstm] if k == 0 else (), inc="c_dve")
                self.dmy()
                td = self.fenced("dve", lambda e: e.reduce_sum(
                    c(14), self.rt_cur[:], axis=mybir.AxisListType.X))
                S.op("dve", lambda e, k=k: e.tensor_copy(
                    self.rt_dsti[:, k:k + 1], c(14)), inc="c_dve")
                S.op("dve", lambda e, k=k: e.tensor_mul(
                    self.rt_cur[:], self.rt_scr[:], self.rt_oh[:, k, :]), inc="c_dve")
                self.dmy()
                to = self.fenced("dve", lambda e: e.reduce_sum(
                    c(15), self.rt_cur[:], axis=mybir.AxisListType.X))
                S.op("dve", lambda e, k=k: e.tensor_copy(
                    self.t_xef[k][:, 513:514], c(15)), inc="c_dve")
                S.op("dve", lambda e, k=k: e.tensor_copy(
                    self.t_xef[k][:, 512:513], self.rt_w[:, k:k + 1]), inc="c_dve")
                S.op("dve", lambda e, k=k: e.tensor_copy(
                    self.t_xef[k][:, 514:515], self.sb_cst[:, 2:3]), inc="c_dve")
            self.dmy()
            tdone = self.S.tok("c_dve")
            if self.debug and hasattr(self, "dbg_rt"):
                ci = tile * 4 + tc
                S.op("sp", lambda e, ci=ci: e.dma_start(
                    out=self.dbg_rt[ci * 128:(ci + 1) * 128, 0:4],
                    in_=self.rt_dsti[:].bitcast(F32) if hasattr(self.rt_dsti[:], "bitcast")
                    else self.rt_dsti[:]), deps=[tdone], inc="q_sp")
                S.op("sp", lambda e, ci=ci: e.dma_start(
                    out=self.dbg_rt[ci * 128:(ci + 1) * 128, 4:36],
                    in_=self.rt_slottm[:]), deps=[tdone], inc="q_sp")
                S.op("sp", lambda e, ci=ci: e.dma_start(
                    out=self.dbg_rt[ci * 128:(ci + 1) * 128, 36:68],
                    in_=self.rt_pr[:]), deps=[tdone], inc="q_sp")
                S.op("sp", lambda e, ci=ci: e.dma_start(
                    out=self.dbg_rt[ci * 128:ci * 128 + 32, 68:72],
                    in_=self.rt_base[:]), deps=[tdone], inc="q_sp")
                S.op("sp", lambda e, ci=ci: e.dma_start(
                    out=self.dbg_rt[ci * 128:ci * 128 + 32, 72:100],
                    in_=self.rt_sv[:, 100:128]), deps=[tdone], inc="q_sp")
            for k in range(4):
                S.op("pool", lambda e, k=k: e.indirect_dma_start(
                    out=self.Xbuf[:],
                    out_offset=IndirectOffsetOnAxis(self.rt_dsti[:, k:k + 1], 0),
                    in_=self.t_xe[k][:], in_offset=None,
                    bounds_check=self.breg_xbuf, oob_is_err=False),
                    deps=[tdone], inc="q_pool")
            self.war[("t_xe",)] = self.S.tok("q_pool")
        ty1e = self.S.tok("c_dve")
        ts1 = S.op("sp", lambda e: e.dma_start(
            out=self.y1_spill[r0:r1].rearrange("(c p) m -> p c m", p=128),
            in_=self.t_y1[:]), deps=[ty1e] + wl(("y1sp",)), inc="q_sp")
        self.war[("y1sp",)] = ts1
        self.war[("t_y1b",)] = ts1
        ts2 = S.op("sp", lambda e: e.dma_start(
            out=self.x4fm_spill.rearrange("(c p) t -> p c t", p=128)[:, :, r0:r1],
            in_=self.t_x4f[:]), deps=[self.S.tok("c_act"), self.S.tok("c_pe")]
            + wl(("x4sp",)), inc="q_sp")
        self.war[("x4sp",)] = ts2
        self.war[("t_x4f",)] = ts2
        self.war[("t_ff",)] = self.S.tok("c_dve")
        self.war[("t_x2",)] = self.S.tok("c_dve")
        self.war[("t_x",)] = self.S.tok("c_dve")

    def alloc_phase3b(self):
        ar = self.ar
        self.wE1 = [ar.at("wE1_0", [128, NCH, EH], BF16, "wWi"),
                    ar.at("wE1_1", [128, NCH, EH], BF16, "wWl1")]
        self.wE3 = [ar.at("wE3_0", [128, NCH, EH], BF16, "wWl2"),
                    ar.at("wE3_1", [128, NCH, EH], BF16, "wmemW")]
        self.wE2 = [ar.at("wE2_0", [128, 4, D], BF16, "gradT")]
        self.wE2.append(self.nc.alloc_sbuf_tensor_at(
            "wE2_1", [128, 4, D], BF16, offset=ar.names["gradT"] + 8192))
        self.e_xg = [ar.alloc(f"e_xg{i}", [128, NCH, CAP], BF16) for i in range(2)]
        self.e_meta = []
        self.e_metaf = []
        for i in range(2):
            t = ar.alloc(f"e_meta{i}", [128, 10, 6], BF16)
            f = self.nc.alloc_sbuf_tensor_at(f"e_metaf{i}", [128, 10, 3], F32,
                                             offset=ar.names[f"e_meta{i}"])
            self.e_meta.append(t)
            self.e_metaf.append(f)
        self.e_hsw = ar.alloc("e_hsw", [128, 4, CAP], BF16)
        self.e_h1s = ar.alloc("e_h1s", [128, 4, 512], BF16)
        self.e_out = [ar.alloc(f"e_out{i}", [128, D], BF16) for i in range(2)]
        self.e_w = [ar.alloc(f"e_w{i}", [128, 10], F32) for i in range(2)]
        self.e_oi = [ar.alloc(f"e_oi{i}", [128, 10], I32) for i in range(2)]

    def emit_expert(self, ei, xdep):
        S = self.S
        buf = ei % 2
        def wl(key):
            v = self.war.get(key)
            return ([] if not v else (list(v) if isinstance(v, list) else [v]))
        tw = []
        for w, src in ((self.wE1[buf], self.eW1), (self.wE3[buf], self.eW3)):
            tw.append(S.op("sp", lambda e, d=w, s=src: e.dma_start(
                out=d[:], in_=s[ei].rearrange("(c p) h -> p c h", p=128)),
                deps=wl(("we", buf)) + list(xdep), inc="q_sp"))
        tw.append(S.op("sp", lambda e: e.dma_start(
            out=self.wE2[buf][:], in_=self.eW2[ei].rearrange("(c p) h -> p c h", p=128)),
            deps=wl(("we", buf)), inc="q_sp"))
        txg = []
        for kc in range(NCH):
            txg.append(S.op("act", lambda e, k=kc: e.dma_start(
                out=self.e_xg[buf][:, k, :],
                in_=self.Xbuf[ei * CAP:(ei + 1) * CAP, k * 128:(k + 1) * 128],
                transpose=True),
                deps=(wl(("xg", buf)) + list(xdep)) if kc == 0 else (), inc="q_act"))
        tme = S.op("pool", lambda e: e.dma_start(
            out=self.e_meta[buf][:],
            in_=self.Xbuf[ei * CAP:(ei + 1) * CAP, 1024:1030].rearrange(
                "(j p) w -> p j w", p=128)),
            deps=wl(("meta", buf)) + list(xdep), inc="q_pool")
        mf = self.e_metaf[buf]
        st = self.stat
        c = lambda i: st[:, 4 * i:4 * i + 1]
        tov = []
        for j in range(10):
            tv = S.op("dve", lambda e, j=j: e.tensor_scalar(
                self.sb_imask[:, 0:1], mf[:, j, 2:3], self.sb_cst[:, 2:3], None,
                ALU.is_equal), deps=[tme] if j == 0 else (), inc="c_dve")
            self.dmy()
            S.op("dve", lambda e, j=j: e.select(
                c(10), self.sb_imask[:, 0:1], mf[:, j, 1:2], self.sb_cst[:, 3:4]),
                inc="c_dve")
            S.op("dve", lambda e, j=j, b=buf: e.select(
                self.e_w[b][:, j:j + 1], self.sb_imask[:, 0:1], mf[:, j, 0:1],
                self.sb_cst[:, 4:5]), inc="c_dve")
            self.dmy()
            tov.append(S.op("dve", lambda e, j=j, b=buf: e.tensor_copy(
                self.e_oi[b][:, j:j + 1], c(10)), inc="c_dve"))
        tmeta_done = self.S.tok("c_dve")
        cts = [(0, 512), (512, 512), (1024, 256)]
        for (c0, cw) in cts:
            for hc in range(4):
                ps, pn = (self.psA, "psA") if hc % 2 == 0 else (self.psB, "psB")
                emits = [lambda e, k=kc, h=hc, p=ps, q0=c0, qw=cw, b=buf: e.matmul(
                    p[:, 0:qw], self.wE1[b][:, k, h * 128:(h + 1) * 128],
                    self.e_xg[b][:, k, q0:q0 + qw], start=(k == 0), stop=(k == NCH - 1))
                    for kc in range(NCH)]
                self.mm(pn, emits, deps=[txg[-1], tw[0],
                        self.war.get(("e_h1s",)) or ("c_pe", 0)]
                        if (c0 == 0 and hc == 0) else ())
                self.evac("act", pn, lambda e, h=hc, p=ps, qw=cw: e.activation(
                    self.e_h1s[:, h, 0:qw], p[:, 0:qw], AF.Silu))
            th1 = self.S.tok("c_act")
            for hc in range(4):
                ps, pn = (self.psA, "psA") if hc % 2 == 0 else (self.psB, "psB")
                emits = [lambda e, k=kc, h=hc, p=ps, q0=c0, qw=cw, b=buf: e.matmul(
                    p[:, 0:qw], self.wE3[b][:, k, h * 128:(h + 1) * 128],
                    self.e_xg[b][:, k, q0:q0 + qw], start=(k == 0), stop=(k == NCH - 1))
                    for kc in range(NCH)]
                self.mm(pn, emits, deps=[tw[1], self.war.get(("e_hsw",)) or ("c_pe", 0)]
                        if (c0 == 0 and hc == 0) else ())
                self.evac("dve", pn, lambda e, h=hc, p=ps, q0=c0, qw=cw: e.tensor_tensor(
                    self.e_hsw[:, h, q0:q0 + qw], self.e_h1s[:, h, 0:qw],
                    p[:, 0:qw], ALU.mult), deps=[th1])
            self.war[("e_h1s",)] = self.S.tok("c_pe")
        thsw = self.S.tok("c_dve")
        for j in range(10):
            ob = j % 2
            for hh in range(2):
                emits = [lambda e, k=kc, h=hh, j=j, b=buf: e.matmul(
                    self.psC[:, h * 512:(h + 1) * 512],
                    self.e_hsw[:, k, j * 128:(j + 1) * 128],
                    self.wE2[b][:, k, h * 512:(h + 1) * 512],
                    start=(k == 0), stop=(k == 3)) for kc in range(4)]
                self.mm("psC", emits, deps=[thsw, tw[2]] if (j == 0 and hh == 0) else ())
            tev = self.evac("dve", "psC", lambda e, j=j, b=buf, o=ob: e.tensor_scalar(
                self.e_out[o][:], self.psC[:], self.e_w[b][:, j:j + 1], None,
                ALU.mult), deps=wl(("e_out", ob)) + [tmeta_done])
            tsc = S.op("pool", lambda e, j=j, b=buf, o=ob: e.indirect_dma_start(
                out=self.Ypair[:],
                out_offset=IndirectOffsetOnAxis(self.e_oi[b][:, j:j + 1], 0),
                in_=self.e_out[o][:], in_offset=None,
                bounds_check=self.breg_yp, oob_is_err=False),
                deps=[tev], inc="q_pool")
            self.war[("e_out", ob)] = tsc
        tpe = self.S.tok("c_pe")
        self.war[("we", buf)] = tpe
        self.war[("xg", buf)] = tpe
        self.war[("meta", buf)] = self.S.tok("c_dve")
        self.war[("e_hsw",)] = tpe

    def alloc_phase3c(self):
        ar = self.ar
        self.wZ1 = ar.at("wZ1", [128, NCH, HS], BF16, "wWi")
        self.wZ3 = ar.at("wZ3", [128, NCH, HS], BF16, "wWl2")
        self.wZ2 = ar.at("wZ2", [128, NCH, D], BF16, "gradT")
        self.z_x4 = [ar.alloc(f"z_x4{i}", [128, NCH, 512], BF16) for i in range(2)]
        self.z_y1 = ar.alloc("z_y1", [128, 4, D], F32)
        self.z_yp = ar.alloc("z_yp", [128, 4, D], BF16)
        self.z_h1s = ar.alloc("z_h1s", [128, NCH, 512], BF16)
        self.z_sw = ar.alloc("z_sw", [128, NCH, 512], BF16)
        self.z_out = [ar.alloc(f"z_out{i}", [128, D], F32) for i in range(2)]

    def load_phase3c_weights(self, deps):
        S = self.S
        tws = []
        for w, srcd in ((self.wZ1, self.zWs1), (self.wZ3, self.zWs3), (self.wZ2, self.zWs2)):
            tws.append(S.op("sp", lambda e, d=w, s=srcd: e.dma_start(
                out=d[:], in_=s.rearrange("(c p) m -> p c m", p=128)),
                deps=deps, inc="q_sp"))
        self.tok_w3c = tws

    def emit_tile_3c(self, tile, ydep):
        S = self.S
        st = self.stat
        c = lambda i: st[:, 4 * i:4 * i + 1]
        r0, r1 = tile * 512, (tile + 1) * 512
        buf = tile % 2
        def wl(key):
            v = self.war.get(key)
            return ([] if not v else (list(v) if isinstance(v, list) else [v]))
        tx4 = S.op("sp", lambda e: e.dma_start(
            out=self.z_x4[buf][:],
            in_=self.x4fm_spill.rearrange("(c p) t -> p c t", p=128)[:, :, r0:r1]),
            deps=wl(("z_x4", buf)) + self.tok_w3c, inc="q_sp")
        ty1 = S.op("sp", lambda e: e.dma_start(
            out=self.z_y1[:], in_=self.y1_spill[r0:r1].rearrange("(c p) m -> p c m", p=128)),
            deps=wl(("z_y1",)), inc="q_sp")
        for hc in range(NCH):
            ps, pn = (self.psA, "psA") if hc % 2 == 0 else (self.psB, "psB")
            emits = [lambda e, k=kc, h=hc, p=ps, b=buf: e.matmul(
                p[:], self.wZ1[:, k, h * 128:(h + 1) * 128],
                self.z_x4[b][:, k, :], start=(k == 0), stop=(k == NCH - 1))
                for kc in range(NCH)]
            self.mm(pn, emits, deps=[tx4, self.war.get(("z_h1s",)) or ("c_pe", 0)]
                    if hc == 0 else ())
            self.evac("act", pn, lambda e, h=hc, p=ps: e.activation(
                self.z_h1s[:, h, :], p[:], AF.Silu))
        th1 = self.S.tok("c_act")
        for hc in range(NCH):
            ps, pn = (self.psA, "psA") if hc % 2 == 0 else (self.psB, "psB")
            emits = [lambda e, k=kc, h=hc, p=ps, b=buf: e.matmul(
                p[:], self.wZ3[:, k, h * 128:(h + 1) * 128],
                self.z_x4[b][:, k, :], start=(k == 0), stop=(k == NCH - 1))
                for kc in range(NCH)]
            self.mm(pn, emits, deps=[self.war.get(("z_sw",)) or ("c_pe", 0)]
                    if hc == 0 else ())
            self.evac("dve", pn, lambda e, h=hc, p=ps: e.tensor_tensor(
                self.z_sw[:, h, :], self.z_h1s[:, h, :], p[:], ALU.mult),
                deps=[th1])
        tsw = self.S.tok("c_dve")
        for tc in range(4):
            typ = S.op("sp", lambda e, t=tc: e.dma_start(
                out=self.z_yp[:],
                in_=self.Ypair.rearrange("(k t) m -> k t m", k=4)[:, r0 + t * 128:r0 + (t + 1) * 128, :]
                    .rearrange("k p m -> p k m")),
                deps=ydep + wl(("z_yp",)), inc="q_sp")
            for hh in range(2):
                emits = [lambda e, k=kc, t=tc, h=hh: e.matmul(
                    self.psC[:, h * 512:(h + 1) * 512],
                    self.z_sw[:, k, t * 128:(t + 1) * 128],
                    self.wZ2[:, k, h * 512:(h + 1) * 512],
                    start=(k == 0), stop=(k == NCH - 1)) for kc in range(NCH)]
                self.mm("psC", emits, deps=[tsw] if (tc == 0 and hh == 0) else ())
            tz = self.evac("dve", "psC", lambda e: e.tensor_add(
                self.a_scr1[:], self.psC[:],
                self.z_yp[:, 0, :]), deps=[typ])
            for k in range(1, 4):
                S.op("dve", lambda e, k=k: e.tensor_add(
                    self.a_scr1[:], self.a_scr1[:], self.z_yp[:, k, :]), inc="c_dve")
            tsum = self.S.tok("c_dve")
            tst = self.ln_stats(self.a_scr1[:], deps=[tsum])
            ob = tc % 2
            tnm = S.op("dve", lambda e, o=ob: e.tensor_scalar(
                self.z_out[o][:], self.a_scr1[:], c(0), c(1), ALU.subtract, ALU.mult),
                deps=[tst] + wl(("z_out", ob)), inc="c_dve")
            tfin = S.op("dve", lambda e, t=tc, o=ob: e.tensor_add(
                self.z_out[o][:], self.z_out[o][:], self.z_y1[:, t, :]),
                deps=[ty1] if tc == 0 else (), inc="c_dve")
            tst_ = S.op("sp", lambda e, t=tc, o=ob: e.dma_start(
                out=self.out_d[r0 + t * 128:r0 + (t + 1) * 128, :],
                in_=self.z_out[o][:]), deps=[tfin], inc="q_sp")
            self.war[("z_out", ob)] = tst_
            self.war[("z_yp",)] = tsum
        self.war[("z_x4", buf)] = self.S.tok("c_pe")
        self.war[("z_y1",)] = self.S.tok("c_dve")
        self.war[("z_h1s",)] = self.S.tok("c_pe")
        self.war[("z_sw",)] = self.S.tok("c_pe")



    def emit_all(self):
        S = self.S
        self.alloc_phase1()
        self.load_perm()
        self.emit_halo_init(None)
        self.emit_p_chain()
        nt = self.T_OWN // 256
        for t in range(nt):
            self.emit_front_tile(t, self.x_own[t * 256:(t + 1) * 256, :],
                                 do_own=True, gchunk0=t * 2)
        for t in range(self.T_EXTRA // 256):
            self.emit_front_tile(nt + t, self.x_extra[t * 256:(t + 1) * 256, :],
                                 do_own=False, gchunk0=0)
        self.alloc_stage2()
        self.emit_pair_allreduce()
        self.tok_w2a = [S.tok("c_pe"), S.tok("c_dve"), S.tok("c_act")]
        self.ar.release()
        self.ar.mark()
        self.alloc_phase2a()
        self.load_phase2a_weights()
        for t in range(self.T_OWN // 512):
            self.emit_tile_2a(t)
        t2a_done = [S.tok("c_pe"), S.tok("c_dve"), S.tok("c_act"), S.tok("q_sp")]
        self.ar.release()
        self.ar.mark()
        self.alloc_phase2bi()
        self.load_phase2bi_weights(t2a_done)
        for t in range(self.T_OWN // 512):
            self.emit_tile_2bi(t)
        t2bi_done = [S.tok("c_pe"), S.tok("c_dve"), S.tok("c_act"), S.tok("q_sp")]
        self.ar.release()
        self.ar.mark()
        self.alloc_phase2bii()
        self.tok_w2bii = t2bi_done
        for t in range(self.T_OWN // 512):
            self.emit_tile_2bii(t, first=(t == 0))
        xdep = [S.tok("q_pool"), S.tok("c_dve"), S.tok("c_pe"), S.tok("c_act")]
        self.ar.release()
        self.ar.mark()
        self.alloc_phase3b()
        for ei in range(E):
            self.emit_expert(ei, xdep if ei < 2 else ())
        t3b_done = [S.tok("c_pe"), S.tok("c_dve"), S.tok("c_act"), S.tok("q_sp"),
                    S.tok("q_pool"), S.tok("q_act")]
        self.ar.release()
        self.ar.mark()
        self.alloc_phase3c()
        self.load_phase3c_weights(t3b_done)
        ydep = [S.tok("q_pool")]
        for t in range(self.T_OWN // 512):
            self.emit_tile_3c(t, ydep)
        fins = S.barrier_tokens()
        def fin(e):
            for semname, val in fins:
                if val > 0:
                    e.wait_ge(S.sem[semname], val)
            return e.nop()
        S.op("sp", fin)

    def emit_phase1(self):
        self.alloc_phase1()
        self.emit_halo_init(None)
        ntiles = self.T_OWN // 256
        for t in range(ntiles):
            self.emit_front_tile(t, self.x_own[t * 256:(t + 1) * 256, :],
                                 do_own=True, gchunk0=t * 2)
        nextra = self.T_EXTRA // 256
        for t in range(nextra):
            self.emit_front_tile(ntiles + t, self.x_extra[t * 256:(t + 1) * 256, :],
                                 do_own=False, gchunk0=0)

    def finish_debug(self):
        S = self.S
        g_out = self.dbg("gradT", [128, NCH, D])
        toks = S.barrier_tokens()
        S.op("sp", lambda e: e.dma_start(out=g_out[:], in_=self.gradT[:]),
             deps=toks, inc="q_sp")
        final = S.tok("q_sp")
        S.op("sp", lambda e: e.wait_ge(S.sem["q_sp"], final[1]) or e.nop())


def host_masks():
    j = np.arange(144)[None, :]
    q = np.arange(128)[:, None]
    band = (j >= q) & (j <= q + WIN)
    maskA = np.where(band, 0.0, -1e30).astype(np.float32)
    maskF = np.where(band & (j >= 16), 0.0, -1e30).astype(np.float32)
    return maskA, maskF


def host_consts():
    ident = np.eye(128)
    ltriS128 = (np.arange(128)[:, None] < np.arange(128)[None, :]).astype(np.float32)
    ltriS32 = (np.arange(32)[:, None] < np.arange(32)[None, :]).astype(np.float32)
    return ident, ltriS128, ltriS32


def col_layout(v):
    w = v.shape[0]
    return np.ascontiguousarray(v.reshape(w // 128, 128).T).astype(np.float32)




_CACHE = {}


def _prog():
    if "p" not in _CACHE:
        p = Prog(T_OWN=4096, T_EXTRA=3 * 4096, debug=False)
        p.emit_all()
        p.S.build()
        _CACHE["p"] = p
    return _CACHE["p"]


_NCALL = [0]


def kernel(**inputs):
    import time, os as _os
    p = _prog()
    inp = {k: np.asarray(v) for k, v in inputs.items()}
    x = np.asarray(inp["L0_attn_x"], np.float32)
    B, S, _ = x.shape
    TO = 4096
    bf = lambda a: np.asarray(a, BF)
    maskA_h, maskF_h = host_masks()
    ident, l128, l32 = host_consts()
    col = col_layout
    shared = {
        "WiT": bf(inp["L0_attn_Wi"].T), "Wl1T": bf(inp["L0_attn_Wl1"].T),
        "Wl2T": bf(inp["L0_attn_Wl2"].T), "memWT": bf(inp["L0_attn_mem_W"].T),
        "WoT": bf(inp["L0_attn_Wo"].T), "WgT": bf(inp["L0_attn_Wg"].T),
        "W1T": bf(inp["L0_attn_W1"].T), "W2T": bf(inp["L0_attn_W2"].T),
        "rWg": bf(inp["L0_ffn_Wg"]), "eW1": bf(inp["L0_ffn_W1"]),
        "eW3": bf(inp["L0_ffn_W3"]), "eW2": bf(inp["L0_ffn_W2"]),
        "zWs1": bf(inp["L0_ffn_Ws1"]), "zWs3": bf(inp["L0_ffn_Ws3"]),
        "zWs2": bf(inp["L0_ffn_Ws2"]),
        "bi_c": col(inp["L0_attn_bi"]), "bl1_c": col(inp["L0_attn_bl1"]),
        "bl2_c": col(inp["L0_attn_bl2"]), "mem_b_c": col(inp["L0_attn_mem_b"]),
        "bo_c": col(inp["L0_attn_bo"]), "bg_c": col(inp["L0_attn_bg"]),
        "b2_c": col(inp["L0_attn_b2"]), "b1_c": col(inp["L0_attn_b1"]),
        "bl2_r": bf(inp["L0_attn_bl2"]).reshape(1, -1),
        "mem_b_r": bf(inp["L0_attn_mem_b"]).reshape(1, -1),
        "bg_r": bf(inp["L0_attn_bg"]).reshape(1, -1),
        "b2_r": bf(inp["L0_attn_b2"]).reshape(1, -1),
        "g1_r": bf(inp["L0_attn_g1"]).reshape(1, -1),
        "be1_r": bf(inp["L0_attn_be1"]).reshape(1, -1),
        "g2_r": bf(inp["L0_attn_g2"]).reshape(1, -1),
        "be2_r": bf(inp["L0_attn_be2"]).reshape(1, -1),
        "g3_r": bf(inp["L0_attn_g3"]).reshape(1, -1),
        "be3_r": bf(inp["L0_attn_be3"]).reshape(1, -1),
        "fgate": np.full((128, 1), np.float32(inp["L0_attn_fgate"][0]), np.float32),
        "lrate": np.full((128, 1), np.float32(inp["L0_attn_lrate"][0]), np.float32),
        "P_in": bf(np.concatenate([np.asarray(inp["L0_attn_P"], np.float32),
                                   np.zeros((6, D), np.float32)])),
        "maskA": bf(maskA_h),
        "ident_bf": bf(ident), "ident_f32": ident.astype(np.float32),
        "negI_bf": bf(-ident), "ones_r": bf(np.ones((1, 512))),
        "ltriS128": bf(l128), "ltriS32": bf(l32),
        "ebase_c": np.stack([np.arange(E) * CAP, np.zeros(E)], 1).astype(np.float32),
        "tokp": np.stack([np.arange(128), np.zeros(128)], 1).astype(np.float32),
        "nonce": np.full((128, 1), 1.2345e7, np.float32),
        "cst_in": np.tile(np.array([[1e-5, 1024.0, 12345678.0, 1e5, 0.0, 0.0]],
                                   np.float32), (128, 1)),
    }
    _NCALL[0] += 1
    n0 = (time.time_ns() ^ (_os.getpid() << 12)) & 0x7FFFFFFF or 1
    n1 = (_NCALL[0] * 2654435761 + 12345) & 0x7FFFFFFF or 2
    in_maps = []
    for c in range(8):
        b, hf = c // 2, c % 2
        side = c % 2
        im = dict(shared)
        im["x_own"] = np.ascontiguousarray(x[b, hf * TO:(hf + 1) * TO])
        im["x_extra"] = np.ascontiguousarray(np.concatenate(
            [x[b2, hf * TO:(hf + 1) * TO] for b2 in range(B) if b2 != b], 0))
        im["xhalo"] = (np.zeros((16, D), np.float32) if hf == 0
                       else np.ascontiguousarray(x[b, TO - 16:TO]))
        im["maskF"] = bf(maskF_h if hf == 0 else maskA_h)
        ps = np.zeros((128, 1), np.float32)
        if hf == 0:
            ps[:10] = 4.0
        im["pscale"] = ps
        im["tok2"] = np.array([[n0, n1], [n0, n1]], np.int32)
        im["si_grad8"] = (side * 1024 + np.arange(NCH)[None, :] * 128
                          + np.arange(128)[:, None]).astype(np.int32)
        im["gi_grad8"] = ((1 - side) * 1024 + np.arange(NCH)[None, :] * 128
                          + np.arange(128)[:, None]).astype(np.int32)
        im["fi_own"] = np.array([[side * 2], [side * 2 + 1]], np.int32)
        im["fi_peer"] = np.array([[(1 - side) * 2], [(1 - side) * 2 + 1]], np.int32)
        in_maps.append(im)
    res = run_bass_kernel_spmd(p.nc, in_maps, core_ids=list(range(8)))
    out = np.empty((B, S, D), np.float32)
    for c in range(8):
        b, hf = c // 2, c % 2
        out[b, hf * TO:(hf + 1) * TO] = np.asarray(res.results[c]["out_y"])
    return out


# version 03
# speedup vs baseline: 1.1003x; 1.1003x over previous
import numpy as np
import ml_dtypes
import concourse.bass as bass
import concourse.mybir as mybir
from concourse.bass_utils import run_bass_kernel_spmd
from concourse.bass_core_rust import IndirectOffsetOnAxis

F32 = mybir.dt.float32
BF16 = mybir.dt.bfloat16
I32 = mybir.dt.int32
AF = mybir.ActivationFunctionType
ALU = mybir.AluOpType
BF = ml_dtypes.bfloat16

D = 1024
NCH = 8
WIN = 16
EPS_LN = 1e-5
EPS_NORM = 1e-12
FF = 4096
E = 32
EH = 512
CAP = 1280
HS = 1024
XW = 1096
NONCE_F = None


class Sched:
    ENGS = ("sp", "pool", "pe", "act", "dve")

    def __init__(self, nc):
        self.nc = nc
        self.ops = {e: [] for e in self.ENGS}
        self.sem = {}
        self.cnt = {}
        self.lastwait = {e: {} for e in self.ENGS}
        for name in ("q_sp", "q_pool", "q_act", "q_pf",
                     "c_pe", "c_act", "c_dve", "c_pool", "c_sp"):
            self.sem[name] = nc.alloc_semaphore(name)
            self.cnt[name] = 0
        self.engobj = {}

    def tok(self, sem):
        return (sem, self.cnt[sem])

    def _engine_for(self, sem):
        return {"q_sp": "sp", "q_pool": "pool", "q_act": "act", "q_pf": "dve",
                "c_pe": "pe", "c_act": "act", "c_dve": "dve",
                "c_pool": "pool", "c_sp": "sp"}[sem]

    def op(self, eng, emit, deps=(), inc=None, inc_by=None):
        waits = []
        for sem, val in deps:
            if val <= 0:
                continue
            if (self._engine_for(sem) == eng and eng != "pool"
                    and sem in ("c_pe", "c_act", "c_dve", "c_sp")):
                continue
            lw = self.lastwait[eng].get(sem, 0)
            if val > lw:
                waits.append((sem, val))
                self.lastwait[eng][sem] = val
        if inc is not None:
            if inc_by is None:
                inc_by = 16 if inc.startswith("q_") else 1
            self.cnt[inc] += inc_by
        self.ops[eng].append((emit, tuple(waits), inc, inc_by))
        return self.tok(inc) if inc is not None else None

    def barrier_tokens(self):
        return [self.tok(s) for s in self.cnt]

    def build(self):
        nc = self.nc
        sched = self

        def runner(engobj, eng):
            for emit, waits, inc, inc_by in sched.ops[eng]:
                for semname, val in waits:
                    engobj.wait_ge(sched.sem[semname], val)
                ins = emit(engobj)
                if inc is not None:
                    assert ins is not None, f"op with inc={inc} emitted nothing"
                    ins.then_inc(sched.sem[inc], inc_by)

        with nc.Block() as block:
            block.sync(lambda e: runner(e, "sp"))
            block.gpsimd(lambda e: runner(e, "pool"))
            block.tensor(lambda e: runner(e, "pe"))
            block.scalar(lambda e: runner(e, "act"))
            block.vector(lambda e: runner(e, "dve"))


def ecopy(e, out, in_):
    return e.tensor_copy(out, in_) if hasattr(e, "tensor_copy") else e.copy(out, in_)


def ceil_div(a, b):
    return (a + b - 1) // b


class Arena:
    BASE = 17 * 1024
    TOP = 229376

    def __init__(self, nc):
        self.nc = nc
        self.off = self.BASE
        self.marks = []
        self.peak = self.BASE

    def alloc(self, name, shape, dtype, align=64):
        per = int(np.prod(shape[1:])) * {F32: 4, BF16: 2, I32: 4}[dtype]
        self.off = (self.off + align - 1) // align * align
        if not hasattr(self, "names"):
            self.names = {}
        self.names[name] = self.off
        t = self.nc.alloc_sbuf_tensor_at(name, list(shape), dtype, offset=self.off)
        self.off += per
        self.peak = max(self.peak, self.off)
        assert self.off <= self.TOP, f"SBUF overflow at {name}: {self.off}"
        return t

    def at(self, name, shape, dtype, at_name):
        return self.nc.alloc_sbuf_tensor_at(name, list(shape), dtype,
                                            offset=self.names[at_name])

    def mark(self):
        self.marks.append(self.off)

    def release(self):
        self.off = self.marks.pop()


class Prog:
    def __init__(self, T_OWN=4096, T_EXTRA=0, debug=False, grad_local=False):
        self.T_OWN = T_OWN
        self.T_EXTRA = T_EXTRA
        self.grad_local = grad_local
        self.debug = debug
        nc = bass.Bass("TRN2", target_bir_lowering=False,
                       detect_race_conditions=False)
        self.nc = nc
        self.S = Sched(nc)
        self.ar = Arena(nc)
        self.dbg_outs = {}
        self.war = {}
        self._declare_io()
        self._alloc_psum()
        self._alloc_perm()

    def _declare_io(self):
        nc = self.nc
        ap = lambda *a, **k: nc.dram_tensor(*a, **k).ap()
        self.x_own = ap("x_own", [self.T_OWN, D], F32, kind="ExternalInput")
        self.xhalo = ap("xhalo", [16, D], F32, kind="ExternalInput")
        self.cst_in = ap("cst_in", [128, 6], F32, kind="ExternalInput")
        if self.T_EXTRA:
            self.x_extra = ap("x_extra", [self.T_EXTRA, D], F32, kind="ExternalInput")
        for nm in ("WiT", "Wl1T", "Wl2T", "memWT", "WoT"):
            setattr(self, nm, ap(nm, [D, D], BF16, kind="ExternalInput"))
        self.WgT = ap("WgT", [2 * D, D], BF16, kind="ExternalInput")
        self.W1T = ap("W1T", [D, FF], BF16, kind="ExternalInput")
        self.W2T = ap("W2T", [FF, D], BF16, kind="ExternalInput")
        self.rWg = ap("rWg", [D, E], BF16, kind="ExternalInput")
        self.eW1 = ap("eW1", [E, D, EH], BF16, kind="ExternalInput")
        self.eW3 = ap("eW3", [E, D, EH], BF16, kind="ExternalInput")
        self.eW2 = ap("eW2", [E, EH, D], BF16, kind="ExternalInput")
        self.zWs1 = ap("zWs1", [D, HS], BF16, kind="ExternalInput")
        self.zWs3 = ap("zWs3", [D, HS], BF16, kind="ExternalInput")
        self.zWs2 = ap("zWs2", [HS, D], BF16, kind="ExternalInput")
        for nm, w in (("bi", D), ("bl1", D), ("bl2", D), ("mem_b", D), ("bo", D),
                      ("bg", D), ("b2", D), ("b1", FF)):
            setattr(self, nm + "_c", ap(nm + "_c", [128, w // 128], F32, kind="ExternalInput"))
        for nm, w in (("bl2", D), ("mem_b", D), ("bg", D), ("b2", D)):
            setattr(self, nm + "_r", ap(nm + "_r", [1, w], BF16, kind="ExternalInput"))
        for nm in ("g1", "be1", "g2", "be2", "g3", "be3"):
            setattr(self, nm + "_r", ap(nm + "_r", [1, D], BF16, kind="ExternalInput"))
        self.fgate = ap("fgate", [128, 1], F32, kind="ExternalInput")
        self.lrate = ap("lrate", [128, 1], F32, kind="ExternalInput")
        self.P_in = ap("P_in", [16, D], BF16, kind="ExternalInput")
        self.pscale = ap("pscale", [128, 1], F32, kind="ExternalInput")
        self.maskA = ap("maskA", [128, 144], BF16, kind="ExternalInput")
        self.maskF = ap("maskF", [128, 144], BF16, kind="ExternalInput")
        self.ident_bf = ap("ident_bf", [128, 128], BF16, kind="ExternalInput")
        self.ident_f32 = ap("ident_f32", [128, 128], F32, kind="ExternalInput")
        self.negI_bf = ap("negI_bf", [128, 128], BF16, kind="ExternalInput")
        self.ones_r = ap("ones_r", [1, 512], BF16, kind="ExternalInput")
        self.ltriS128 = ap("ltriS128", [128, 128], BF16, kind="ExternalInput")
        self.ltriS32 = ap("ltriS32", [32, 32], BF16, kind="ExternalInput")
        self.ebase_c = ap("ebase_c", [32, 2], F32, kind="ExternalInput")
        self.tokp = ap("tokp", [128, 2], F32, kind="ExternalInput")
        self.nonce = ap("nonce", [128, 1], F32, kind="ExternalInput")
        self.tok2 = ap("tok2", [2, 2], I32, kind="ExternalInput")
        self.si_grad8 = ap("si_grad8", [128, NCH], I32, kind="ExternalInput")
        self.gi_grad8 = ap("gi_grad8", [128, NCH], I32, kind="ExternalInput")
        self.fi_own = ap("fi_own", [2, 1], I32, kind="ExternalInput")
        self.fi_peer = ap("fi_peer", [2, 1], I32, kind="ExternalInput")
        self.out_d = ap("out_y", [self.T_OWN, D], F32, kind="ExternalOutput")
        self.hdd_spill = ap("hdd_spill", [D, self.T_OWN], BF16)
        self.x1_spill = ap("x1_spill", [self.T_OWN, D], BF16)
        self.x2tm_spill = ap("x2tm_spill", [self.T_OWN, D], BF16)
        self.x2fm_spill = ap("x2fm_spill", [D, self.T_OWN], BF16)
        self.ff_spill = ap("ff_spill", [self.T_OWN, D], BF16)
        self.y1_spill = ap("y1_spill", [self.T_OWN, D], F32)
        self.x4fm_spill = ap("x4fm_spill", [D, self.T_OWN], BF16)
        self.Xbuf = ap("Xbuf", [E * CAP, XW], BF16)
        self.Ypair = ap("Ypair", [4 * self.T_OWN, D], BF16)
        self.gr_slots = nc.dram_tensor("gr_slots", [2 * D, D], F32, kind="Internal",
                                       addr_space="Shared").ap()
        self.gr_flags = nc.dram_tensor("gr_flags", [4, 2], I32, kind="Internal",
                                       addr_space="Shared").ap()

    def dbg(self, name, shape, dtype=F32):
        t = self.nc.dram_tensor("dbg_" + name, list(shape), dtype, kind="ExternalOutput").ap()
        self.dbg_outs[name] = t
        return t

    def _alloc_psum(self):
        nc = self.nc
        self.psA = nc.alloc_psum_tensor("psA", [128, 512], F32)
        self.psB = nc.alloc_psum_tensor("psB", [128, 512], F32)
        self.psC = nc.alloc_psum_tensor("psC", [128, 1024], F32)
        self.psD = nc.alloc_psum_tensor("psD", [128, 1024], F32)
        self.psE = nc.alloc_psum_tensor("psE", [128, 512], F32)
        self.psT = nc.alloc_psum_tensor("psT", [128, 1024], BF16)

    def _alloc_perm(self):
        ar, S, nc = self.ar, self.S, self.nc
        self.wWi = ar.alloc("wWi", [128, NCH, D], BF16)
        self.wWl1 = ar.alloc("wWl1", [128, NCH, D], BF16)
        self.wWl2 = ar.alloc("wWl2", [128, NCH, D], BF16)
        self.wmemW = ar.alloc("wmemW", [128, NCH, D], BF16)
        self.gradT = ar.alloc("gradT", [128, NCH, D], F32)
        self.c_bi = ar.alloc("c_bi", [128, NCH], F32)
        self.c_bl1 = ar.alloc("c_bl1", [128, NCH], F32)
        self.c_bo = ar.alloc("c_bo", [128, NCH], F32)
        self.c_b2 = ar.alloc("c_b2", [128, NCH], F32)
        self.c_bg = ar.alloc("c_bg", [128, NCH], F32)
        self.c_memb = ar.alloc("c_memb", [128, NCH], F32)
        self.c_b1 = ar.alloc("c_b1", [128, FF // 128], F32)
        self.r_bl2 = ar.alloc("r_bl2", [1, D], BF16)
        self.r_memb = ar.alloc("r_memb", [1, D], BF16)
        self.r_bg = ar.alloc("r_bg", [1, D], BF16)
        self.r_b2 = ar.alloc("r_b2", [1, D], BF16)
        self.sb_ones = ar.alloc("sb_ones", [1, 512], BF16)
        self.sb_identbf = ar.alloc("sb_identbf", [128, 128], BF16)
        self.sb_identf32 = ar.alloc("sb_identf32", [128, 128], F32)
        self.sb_negI = ar.alloc("sb_negI", [128, 128], BF16)
        self.sb_maskA = ar.alloc("sb_maskA", [128, 144], BF16)
        self.sb_maskF = ar.alloc("sb_maskF", [128, 144], BF16)
        self.sb_fg = ar.alloc("sb_fg", [128, 1], F32)
        self.sb_lr = ar.alloc("sb_lr", [128, 1], F32)
        self.sb_pscale = ar.alloc("sb_pscale", [128, 1], F32)
        self.sb_cst = ar.alloc("sb_cst", [128, 6], F32)
        self.dmy_dve = ar.alloc("dmy_dve", [128, 1], F32)
        self.dmy_act = ar.alloc("dmy_act", [128, 1], F32)
        self.sb_P = ar.alloc("sb_P", [16, D], BF16)
        self.sb_sig8 = ar.alloc("sb_sig8", [128, NCH], I32)
        self.sb_tok2 = ar.alloc("sb_tok2", [2, 2], I32)
        self.sb_fio = ar.alloc("sb_fio", [2, 1], I32)
        self.sb_fip = ar.alloc("sb_fip", [2, 1], I32)
        self.sb_fbuf = ar.alloc("sb_fbuf", [2, 2], I32)
        self.sb_gig8 = ar.alloc("sb_gig8", [128, NCH], I32)
        self.sb_imask = ar.alloc("sb_imask", [128, 2], I32)
        self.sb_ltri128 = ar.alloc("sb_ltri128", [128, 128], BF16)
        self.sb_ltri32 = ar.alloc("sb_ltri32", [32, 32], BF16)
        self.sb_ebase = ar.alloc("sb_ebase", [32, 2], F32)
        self.sb_tokp = ar.alloc("sb_tokp", [128, 2], F32)
        self.wrWg = ar.alloc("wrWg", [128, NCH, E], BF16)
        for nm in ("g1", "be1", "g2", "be2", "g3", "be3"):
            setattr(self, "bc_" + nm, ar.alloc("bc_" + nm, [128, D], BF16))
        self.stat = ar.alloc("stat", [128, 64], F32)
        self.a_scr1 = ar.alloc("a_scr1", [128, D], F32)
        self.a_scr2 = ar.alloc("a_scr2", [128, D], F32)
        self.sqscr = ar.alloc("sqscr", [128, D], F32)
        self.perm_end = ar.off

    def load_perm(self):
        S = self
        sch = self.S
        loads = [
            (self.wWi, self.WiT), (self.wWl1, self.Wl1T), (self.wWl2, self.Wl2T),
            (self.wmemW, self.memWT),
            (self.c_bi, self.bi_c), (self.c_bl1, self.bl1_c), (self.c_bo, self.bo_c),
            (self.c_b2, self.b2_c), (self.c_bg, self.bg_c), (self.c_memb, self.mem_b_c),
            (self.c_b1, self.b1_c),
            (self.r_bl2, self.bl2_r), (self.r_memb, self.mem_b_r), (self.r_bg, self.bg_r),
            (self.r_b2, self.b2_r),
            (self.sb_ones, self.ones_r), (self.sb_identbf, self.ident_bf),
            (self.sb_identf32, self.ident_f32), (self.sb_negI, self.negI_bf),
            (self.sb_maskA, self.maskA), (self.sb_maskF, self.maskF),
            (self.sb_fg, self.fgate), (self.sb_lr, self.lrate), (self.sb_pscale, self.pscale),
            (self.sb_cst, self.cst_in),
            (self.r_g1, self.g1_r), (self.r_be1, self.be1_r),
            (self.r_g2, self.g2_r), (self.r_be2, self.be2_r),
            (self.r_g3, self.g3_r), (self.r_be3, self.be3_r),
            (self.sb_P, self.P_in), (self.sb_sig8, self.si_grad8),
            (self.sb_tok2, self.tok2), (self.sb_fio, self.fi_own), (self.sb_gig8, self.gi_grad8),
            (self.sb_ltri128, self.ltriS128), (self.sb_ltri32, self.ltriS32),
            (self.sb_ebase, self.ebase_c), (self.sb_tokp, self.tokp),
            (self.wrWg, self.rWg),
            (self.sb_fip, self.fi_peer),
        ]
        for dst, src in loads:
            if dst.shape[0] == 128 and len(dst.shape) == 3:
                srcv = src.rearrange("(c p) m -> p c m", p=128)
                sch.op("sp", lambda e, d=dst, s=srcv: e.dma_start(out=d[:], in_=s), inc="q_sp")
            else:
                sch.op("act", lambda e, d=dst, s=src: e.dma_start(out=d[:], in_=s[:]), inc="q_act")
        tk = sch.tok("q_act")
        self.tok_cst = tk
        sch.op("dve", lambda e: e.memset(self.gradT[:], 0.0), inc="c_dve")
        sch.op("dve", lambda e: e.memset(self.dmy_dve[:], 0.0), inc="c_dve")
        sch.op("dve", lambda e: e.memset(self.dmy_act[:], 0.0), inc="c_dve")
        for nm in ("g1", "be1", "g2", "be2", "g3", "be3"):
            row = getattr(self, "r_" + nm)
            dst = getattr(self, "bc_" + nm)
            for hh in range(2):
                mm = self.mm("psA", [lambda e, r=row, h=hh, ps=self.psA: e.matmul(
                    ps[:], self.sb_ones[0:1, 0:128], r[0:1, h * 512:(h + 1) * 512],
                    start=True, stop=True)], deps=[tk])
                self.evac("dve", "psA", lambda e, d=dst, h=hh: e.tensor_copy(
                    d[:, h * 512:(h + 1) * 512], self.psA[:]))
        self.tok_perm = [sch.tok("c_dve"), sch.tok("q_sp"), sch.tok("q_act"), sch.tok("c_pe")]

    def fenced(self, eng, emit, deps=()):
        self.S.op(eng, emit, deps=deps)
        if eng == "dve":
            return self.S.op("dve", lambda e: e.tensor_scalar_mul(
                self.dmy_dve[:], self.dmy_dve[:], 1.0), inc="c_dve")
        else:
            return self.S.op("act", lambda e: e.activation(
                self.dmy_act[:], self.dmy_act[:], AF.Copy), inc="c_act")

    def mm(self, psname, emits, deps=()):
        if not hasattr(self, "pslast"):
            self.pslast = {}
        st = self.pslast.setdefault(psname, {"wr": None, "rd": None})
        sch = self.S
        first_deps = list(deps)
        if st["rd"] is not None:
            first_deps.append(st["rd"])
        n = len(emits)
        for i, em in enumerate(emits):
            t = sch.op("pe", em, deps=first_deps if i == 0 else (),
                       inc="c_pe" if i == n - 1 else None)
        st["wr"] = t
        return t

    def evac(self, eng, psname, emit, deps=(), inc=None):
        st = self.pslast[psname]
        d = list(deps)
        if st["wr"] is not None:
            d.append(st["wr"])
        t = self.S.op(eng, emit, deps=d, inc=inc or ("c_" + eng))
        st["rd"] = t
        return t

    def alloc_phase1(self):
        ar = self.ar
        ar.mark()
        base = (ar.off + 63) // 64 * 64
        self.xt = [ar.alloc(f"xt{i}", [128, 2, D], F32) for i in range(2)]
        for i, nm in enumerate(("g1", "be1", "g2", "be2", "g3", "be3")):
            setattr(self, "r_" + nm, self.nc.alloc_sbuf_tensor_at(
                "r_" + nm + "s", [1, D], BF16, offset=base + i * 2048))
        self.xb_tm = ar.alloc("xb_tm", [128, D], F32)
        self.xb_bf = ar.alloc("xb_bf", [128, 2, D], BF16)
        self.xb_fm = ar.alloc("xb_fm", [128, NCH, 16 + 256], BF16)
        self.kn_bf = ar.alloc("kn_bf", [128, D], BF16)
        self.kn_fm = ar.alloc("kn_fm", [128, NCH, 256], BF16)
        self.kp_fm = ar.alloc("kp_fm", [128, NCH, 256], BF16)
        self.h1_fm = ar.alloc("h1_fm", [128, NCH, 256], BF16)
        self.hdd_tm = ar.alloc("hdd_tm", [128, 2, D], BF16)
        self.hdd_fm = ar.alloc("hdd_fm", [128, NCH, 256], BF16)
        self.diff_tm = ar.alloc("diff_tm", [128, 2, D], BF16)
        self.ktm = [ar.alloc(f"ktm{i}", [16, D], BF16) for i in range(2)]
        self.exps = ar.alloc("exps", [128, 144], F32)
        self.probs_bf = ar.alloc("probs_bf", [128, 144], BF16)
        self.pT_main = ar.alloc("pT_main", [128, 128], BF16)
        self.pT_halo = ar.alloc("pT_halo", [16, 128], BF16)
        self.sum_tm = ar.alloc("sum_tm", [128, D], F32)
        self.x1_bf = ar.alloc("x1_bf", [128, 2, D], BF16)
        self.xhalo_sb = self.xt[0][0:16, 0, :]
        self.xhalo_bf = self.xb_bf[0:16, 0, :]
        self.tile_end = []

    def dmy(self):
        self.S.op("dve", lambda e: e.tensor_scalar_mul(
            self.dmy_dve[:], self.dmy_dve[:], 1.0), inc="c_dve")

    def ln_stats(self, src_ap, n=D, eps=EPS_LN, want_rnorm=False, deps=(), parts=128):
        S, st = self.S, self.stat
        c = lambda i: st[0:parts, 4 * i:4 * i + 1]
        t1 = self.fenced("dve", lambda e: e.reduce_sum(c(5), src_ap,
                         axis=mybir.AxisListType.X), deps=deps)
        sqwar = [self.war[("sqscr",)]] if self.war.get(("sqscr",)) else []
        t2a = S.op("act", lambda e: e.activation(self.sqscr[0:parts, 0:n], src_ap,
                   AF.Square), deps=list(deps) + sqwar, inc="c_act")
        t2 = self.fenced("dve", lambda e: e.reduce_sum(c(6), self.sqscr[0:parts, 0:n],
                         axis=mybir.AxisListType.X), deps=[t2a])
        self.war[("sqscr",)] = t2
        S.op("dve", lambda e: e.scalar_tensor_tensor(c(7), c(5), 1.0 / n, c(5),
             ALU.mult, ALU.mult), inc="c_dve")
        S.op("dve", lambda e: e.tensor_scalar_mul(c(0), c(5), 1.0 / n), inc="c_dve")
        S.op("dve", lambda e: e.tensor_sub(c(4), c(6), c(7)), inc="c_dve")
        self.dmy()
        tv = self.fenced("dve", lambda e: e.tensor_scalar(c(9), c(4), 1.0 / n, eps,
                         ALU.mult, ALU.add))
        ts_ = self.fenced("act", lambda e: e.activation(c(7), c(9), AF.Sqrt), deps=[tv])
        tr = self.fenced("dve", lambda e: e.reciprocal(c(1), c(7)), deps=[ts_])
        if want_rnorm:
            tq = self.fenced("act", lambda e: e.activation(c(8), c(4), AF.Sqrt), deps=[tv])
            S.op("dve", lambda e: e.tensor_scalar(c(8), c(8), c(1), EPS_NORM,
                 ALU.mult, ALU.max), deps=[tq], inc="c_dve")
            self.dmy()
            tr = self.fenced("dve", lambda e: e.reciprocal(c(2), c(8)))
        return tr

    def transpose_tm_to_fm(self, src_tm_chunk, dst_fm, tcol, deps=(), eng="dve", war=None):
        emits = [lambda e, k=kc: e.transpose(
            self.psT[:, k * 128:(k + 1) * 128],
            src_tm_chunk[:, k * 128:(k + 1) * 128],
            self.sb_identbf[:]) for kc in range(NCH)]
        self.mm("psT", emits, deps=deps)
        wd = [] if not war else (list(war) if isinstance(war, list) else [war])
        return self.evac(eng, "psT", lambda e: ecopy(e,
            dst_fm[:, :, tcol:tcol + 128],
            self.psT[:].rearrange("p (c n) -> p c n", c=NCH)),
            deps=wd)

    def emit_halo_init(self, is_odd_core_input):
        S = self.S
        tl = S.op("act", lambda e: e.dma_start(out=self.xhalo_sb, in_=self.xhalo[:]),
                  deps=self.tok_perm, inc="q_act")
        tst = self.ln_stats(self.xhalo_sb, deps=[tl], parts=16)
        c = lambda i: self.stat[0:16, 4 * i:4 * i + 1]
        th = S.op("dve", lambda e: e.tensor_scalar(
            self.xhalo_bf, self.xhalo_sb, c(0), c(1), ALU.subtract, ALU.mult),
            deps=[tst], inc="c_dve")
        tk = S.op("dve", lambda e: e.tensor_copy(self.ktm[1][:], self.xhalo_bf),
                  inc="c_dve")
        self.war[("ktm", 1)] = None
        emits = [lambda e, k=kc: e.transpose(
            self.psT[:, k * 16:(k + 1) * 16],
            self.xb_bf[0:16, 0, k * 128:(k + 1) * 128],
            self.sb_identbf[0:16, 0:16]) for kc in range(NCH)]
        self.mm("psT", emits, deps=[th])
        te = self.evac("dve", "psT", lambda e: e.tensor_copy(
            self.xb_fm[:, :, 0:16],
            self.psT[:, 0:128].rearrange("p (c n) -> p c n", c=NCH)))
        self.war[("halo",)] = te
        self.war[("xt", 0)] = tk
        self.war[("xb_bf_init",)] = self.S.tok("c_pe")
        return tk

    def emit_front_tile(self, tile, x_src_ap, do_own=True, gchunk0=0):
        S, nc = self.S, self.nc
        st = self.stat
        c = lambda i: st[:, 4 * i:4 * i + 1]
        buf = tile % 2
        xt = self.xt[buf]
        prev_end = self.tile_end
        xdeps = ([self.war.get(("xt", buf))] if self.war.get(("xt", buf)) else []) \
            + list(self.tok_perm)
        if do_own:
            xload = S.op("sp", lambda e: e.dma_start(
                out=xt[:], in_=x_src_ap.rearrange("(c p) m -> p c m", p=128)),
                deps=xdeps, inc="q_sp")
        else:
            xload = S.op("sp", lambda e: e.dma_start(
                out=xt[:], in_=x_src_ap.rearrange("(c p) m -> p c m", p=128)),
                deps=xdeps, inc="q_sp")
        if do_own:
            if tile > 0:
                thc = S.op("dve", lambda e: e.tensor_copy(
                    self.xb_fm[:, :, 0:16], self.xb_fm[:, :, 256:272]),
                    deps=prev_end, inc="c_dve")
                self.war[("halo",)] = thc
        ttx1 = [None, None]
        for tc in range(2):
            gchunk = gchunk0 + tc
            tstat = self.ln_stats(xt[:, tc, :], want_rnorm=True,
                                  deps=[xload] +
                                  ([self.war[("sqscr",)]] if self.war.get(("sqscr",)) else []))
            if do_own:
                txb = S.op("dve", lambda e, t=tc: e.tensor_scalar(
                    self.xb_tm[:], xt[:, t, :], c(0), c(1),
                    ALU.subtract, ALU.mult),
                    deps=[tstat] + ([self.war[("xb_tm",)]] if self.war.get(("xb_tm",)) else []),
                    inc="c_dve")
                txbb = S.op("act", lambda e, t=tc: e.activation(
                    self.xb_bf[:, t, :], self.xb_tm[:], AF.Copy),
                    deps=[txb] + ([self.war.pop(("xb_bf_init",))]
                                  if ("xb_bf_init",) in self.war else []),
                    inc="c_act")
                self.war[("xb_tm",)] = txbb
                tkn = S.op("dve", lambda e: e.tensor_scalar(
                    self.kn_bf[:], self.xb_tm[:], c(2), None, ALU.mult),
                    deps=[txb] + ([self.war[("kn_bf",)]] if self.war.get(("kn_bf",)) else []),
                    inc="c_dve")
            else:
                self.fenced("dve", lambda e: e.tensor_mul(c(4), c(1), c(2)),
                            deps=[tstat])
                if self.debug and hasattr(self, "snapbuf"):
                    k = self.snapidx; self.snapidx += 1
                    if k < self.snapbuf.shape[1] // 64:
                        S.op("dve", lambda e, k=k: e.tensor_copy(
                            self.snapbuf[:, k * 64:(k + 1) * 64], self.stat[:, 0:64]),
                            inc="c_dve")
                tkn = S.op("dve", lambda e, t=tc: e.tensor_scalar(
                    self.kn_bf[:], xt[:, t, :], c(0), c(4),
                    ALU.subtract, ALU.mult),
                    deps=[self.war[("kn_bf",)]] if self.war.get(("kn_bf",)) else (),
                    inc="c_dve")
                txbb = tkn
            self.war[("xt", buf)] = tkn
            tfmk = self.transpose_tm_to_fm(self.kn_bf[:], self.kn_fm, tc * 128,
                                           deps=[tkn], eng="act",
                                           war=self.war.get(("kn_fm",)))
            self.war[("kn_bf",)] = self.S.tok("c_pe")
            if do_own:
                tfmx = self.transpose_tm_to_fm(
                    self.xb_bf[:, tc, :], self.xb_fm, 16 + tc * 128,
                    deps=[txbb, self.war.get(("scores",), None) or ("c_pe", 0),
                          self.war.get(("halo",), None) or ("c_pe", 0)])
                emits = [lambda e, k=kc, t=tc: e.transpose(
                    self.psT[0:16, k * 128:(k + 1) * 128],
                    self.xb_fm[:, k, 16 + t * 128 + 112:16 + t * 128 + 128],
                    self.sb_identbf[:]) for kc in range(NCH)]
                self.mm("psT", emits,
                        deps=[tfmx] + ([self.war[("ktm", gchunk % 2)]]
                                       if self.war.get(("ktm", gchunk % 2)) else []))
                ttail = self.evac("dve", "psT", lambda e, g=gchunk: e.tensor_copy(
                    self.ktm[g % 2][:], self.psT[0:16, :]))
                mask = self.sb_maskF if gchunk == 0 else self.sb_maskA
                emits = [lambda e, k=kc, t=tc: e.matmul(
                    self.psB[:, 0:144],
                    self.xb_fm[:, k, 16 + t * 128:16 + t * 128 + 128],
                    self.xb_fm[:, k, t * 128:t * 128 + 144],
                    start=(k == 0), stop=(k == NCH - 1)) for kc in range(NCH)]
                tsc = self.mm("psB", emits, deps=[tfmx])
                self.war[("scores",)] = tsc
                tmk = self.evac("dve", "psB", lambda e, m=mask: e.scalar_tensor_tensor(
                    self.psB[:, 0:144], self.psB[:, 0:144], 1.0 / 32.0, m[:],
                    ALU.mult, ALU.add))
                tmx = self.fenced("dve", lambda e: e.reduce_max(
                    c(9), self.psB[:, 0:144], axis=mybir.AxisListType.X))
                tng = self.fenced("dve", lambda e: e.tensor_scalar_mul(c(10), c(9), -1.0))
                texp = S.op("act", lambda e: e.activation(
                    self.exps[:], self.psB[:, 0:144], AF.Exp, bias=c(10)),
                    deps=[tng] + ([self.war[("exps",)]] if self.war.get(("exps",)) else []),
                    inc="c_act")
                self.pslast["psB"]["rd"] = texp
                tsume = self.fenced("dve", lambda e: e.reduce_sum(
                    c(11), self.exps[:], axis=mybir.AxisListType.X), deps=[texp])
                trc = self.fenced("dve", lambda e: e.reciprocal(c(12), c(11)),
                                  deps=[tsume])
                tpb = S.op("dve", lambda e: e.tensor_scalar(
                    self.probs_bf[:], self.exps[:], c(12), None, ALU.mult), inc="c_dve")
                self.war[("exps",)] = tpb
                if self.debug and gchunk == 0 and hasattr(self, "dbg_p0"):
                    S.op("sp", lambda e: e.dma_start(out=self.dbg_p0[:], in_=self.probs_bf[:]),
                         deps=[tpb], inc="q_sp")
                    S.op("sp", lambda e: e.dma_start(out=self.dbg_s0[:], in_=self.exps[:]),
                         deps=[tpb], inc="q_sp")
                emits = [lambda e: e.transpose(self.psT[:, 0:128], self.probs_bf[:, 16:144],
                                               self.sb_identbf[:]),
                         lambda e: e.transpose(self.psT[0:16, 128:256], self.probs_bf[:, 0:16],
                                               self.sb_identbf[:])]
                self.mm("psT", emits, deps=[tpb])
                tpt = self.evac("dve", "psT", lambda e: e.tensor_copy(
                    self.pT_main[:], self.psT[:, 0:128]))
                tpt2 = S.op("dve", lambda e: e.tensor_copy(
                    self.pT_halo[:], self.psT[0:16, 128:256]), inc="c_dve")
                self.pslast["psT"]["rd"] = tpt2
                halo_src = self.ktm[(gchunk + 1) % 2]
                emits = []
                for hh in range(2):
                    emits.append(lambda e, h=hh, t=tc: e.matmul(
                        self.psD[:, h * 512:(h + 1) * 512], self.pT_main[:],
                        self.xb_bf[:, t, h * 512:(h + 1) * 512], start=True, stop=False))
                    emits.append(lambda e, h=hh, hs=halo_src: e.matmul(
                        self.psD[:, h * 512:(h + 1) * 512], self.pT_halo[:],
                        hs[:, h * 512:(h + 1) * 512], start=False, stop=True))
                tat = self.mm("psD", emits, deps=[tpt, tpt2, ttail])
                self.war[("ktm", (gchunk + 1) % 2)] = tat
                tsum = self.evac("dve", "psD", lambda e: e.tensor_add(
                    self.sum_tm[:], self.xb_tm[:], self.psD[:]),
                    deps=[self.war[("sum_tm",)]] if self.war.get(("sum_tm",)) else ())
                if self.debug and gchunk == 0 and hasattr(self, "dbg_sm0"):
                    S.op("sp", lambda e: e.dma_start(out=self.dbg_sm0[:], in_=self.sum_tm[:]),
                         deps=[tsum], inc="q_sp")
                tst2 = self.ln_stats(self.sum_tm[:], deps=[tsum] +
                    ([self.war[("sqscr",)]] if self.war.get(("sqscr",)) else []))
                self.war[("sum_tm",)] = self.S.tok("c_act")
                S.op("dve", lambda e: e.tensor_scalar(
                    self.sqscr[:], self.sum_tm[:], c(0), c(1),
                    ALU.subtract, ALU.mult), deps=[tst2], inc="c_dve")
                tg_ = S.op("dve", lambda e: e.tensor_mul(
                    self.sum_tm[:], self.sqscr[:], self.bc_g1[:]), inc="c_dve")
                self.war[("sqscr",)] = tg_
                ttx1[tc] = S.op("dve", lambda e, t=tc: e.tensor_add(
                    self.x1_bf[:, t, :], self.sum_tm[:], self.bc_be1[:]),
                    deps=[self.war[("x1sp",)]] if self.war.get(("x1sp",)) else (),
                    inc="c_dve")
        if do_own:
            tsp1 = S.op("sp", lambda e: e.dma_start(
                out=self.x1_spill[tile * 256:(tile + 1) * 256].rearrange(
                    "(c p) m -> p c m", p=128),
                in_=self.x1_bf[:]), deps=[ttx1[0], ttx1[1],
                                          self.war.get(("x1sp",)) or ("c_sp", 0)],
                inc="q_sp")
            self.war[("x1sp",)] = tsp1
        tknf = self.S.tok("c_act")
        for mc in range(NCH):
            ps, pn = (self.psA, "psA") if mc % 2 == 0 else (self.psB, "psB")
            emits = [lambda e, k=kc, m=mc, p=ps: e.matmul(
                p[:, 0:256], self.wWi[:, k, m * 128:(m + 1) * 128],
                self.kn_fm[:, k, :], start=(k == 0), stop=(k == NCH - 1))
                for kc in range(NCH)]
            self.mm(pn, emits, deps=[tfmk, self.war.get(("kp_fm",)) or ("c_pe", 0)]
                    if mc == 0 else ())
            self.evac("dve", pn, lambda e, m=mc, p=ps: e.tensor_scalar(
                self.kp_fm[:, m, :], p[:, 0:256], self.c_bi[:, m:m + 1], None, ALU.add),
                deps=[self.war.get(("kp_fm",)) or ("c_pe", 0)] if mc == 0 else ())
        for mc in range(NCH):
            ps, pn = (self.psA, "psA") if mc % 2 == 0 else (self.psB, "psB")
            emits = [lambda e, k=kc, m=mc, p=ps: e.matmul(
                p[:, 0:256], self.wWl1[:, k, m * 128:(m + 1) * 128],
                self.kp_fm[:, k, :], start=(k == 0), stop=(k == NCH - 1))
                for kc in range(NCH)]
            self.mm(pn, emits, deps=[self.war.get(("h1_fm",)) or ("c_pe", 0)]
                    if mc == 0 else ())
            self.evac("act", pn, lambda e, m=mc, p=ps: e.activation(
                self.h1_fm[:, m, :], p[:, 0:256], AF.Silu, bias=self.c_bl1[:, m:m + 1]),
                deps=[self.war.get(("h1_fm",)) or ("c_pe", 0)] if mc == 0 else ())
        self.war[("kn_fm",)] = self.S.tok("c_pe")
        tkp = self.S.tok("c_act")
        for tc in range(2):
            for hh in range(2):
                emits = [lambda e, k=kc, t=tc, h=hh: e.matmul(
                    self.psC[:, h * 512:(h + 1) * 512],
                    self.h1_fm[:, k, t * 128:(t + 1) * 128],
                    self.wWl2[:, k, h * 512:(h + 1) * 512],
                    start=(k == 0), stop=False) for kc in range(NCH)]
                emits.append(lambda e, h=hh: e.matmul(
                    self.psC[:, h * 512:(h + 1) * 512], self.sb_ones[0:1, 0:128],
                    self.r_bl2[0:1, h * 512:(h + 1) * 512], start=False, stop=True))
                self.mm("psC", emits,
                        deps=[tkp, self.war.get(("hdd_tm",)) or ("c_pe", 0)]
                        if (tc == 0 and hh == 0) else ())
            thdd = self.evac("act", "psC", lambda e, t=tc: e.activation(
                self.hdd_tm[:, t, :], self.psC[:], AF.Silu),
                deps=[self.war.get(("hdd_tm",)) or ("c_pe", 0)] if tc == 0 else ())
            self.transpose_tm_to_fm(self.hdd_tm[:, tc, :], self.hdd_fm, tc * 128,
                                    deps=[thdd],
                                    war=[t for t in (self.war.get(("hdd_fm",)),
                                                     self.war.get(("hddfm_sp",))) if t])
        self.war[("h1_fm",)] = self.S.tok("c_pe")
        thf = self.S.tok("c_dve")
        if do_own:
            tsp2 = S.op("sp", lambda e: e.dma_start(
                out=self.hdd_spill.rearrange("(c p) t -> p c t", p=128)[
                    :, :, tile * 256:(tile + 1) * 256],
                in_=self.hdd_fm[:]), deps=[thf, self.war.get(("hdsp",)) or ("c_sp", 0)],
                inc="q_sp")
            self.war[("hdsp",)] = tsp2
            self.war[("hddfm_sp",)] = tsp2
        for tc in range(2):
            for hh in range(2):
                emits = [lambda e, k=kc, t=tc, h=hh: e.matmul(
                    self.psD[:, h * 512:(h + 1) * 512],
                    self.hdd_fm[:, k, t * 128:(t + 1) * 128],
                    self.wmemW[:, k, h * 512:(h + 1) * 512],
                    start=(k == 0), stop=False) for kc in range(NCH)]
                emits.append(lambda e, h=hh: e.matmul(
                    self.psD[:, h * 512:(h + 1) * 512], self.sb_ones[0:1, 0:128],
                    self.r_memb[0:1, h * 512:(h + 1) * 512], start=False, stop=False))
                for mc in range(hh * 4, hh * 4 + 4):
                    emits.append(lambda e, t=tc, m=mc, h=hh: e.matmul(
                        self.psD[:, m * 128:(m + 1) * 128],
                        self.kp_fm[:, m, t * 128:(t + 1) * 128],
                        self.sb_negI[:], start=False, stop=(m == h * 4 + 3)))
                self.mm("psD", emits,
                        deps=[thf, self.war.get(("diff_tm",)) or ("c_pe", 0)]
                        if (tc == 0 and hh == 0) else ())
            self.evac("dve", "psD", lambda e, t=tc: e.tensor_copy(
                self.diff_tm[:, t, :], self.psD[:]),
                deps=[self.war.get(("diff_tm",)) or ("c_pe", 0)] if tc == 0 else ())
        self.war[("kp_fm",)] = self.S.tok("c_pe")
        tdiff = self.S.tok("c_dve")
        for bc in range(NCH):
            for hh in range(2):
                emits = [lambda e, t=tc, b=bc, h=hh: e.matmul(
                    self.psC[:, h * 512:(h + 1) * 512],
                    self.hdd_tm[:, t, b * 128:(b + 1) * 128],
                    self.diff_tm[:, t, h * 512:(h + 1) * 512],
                    start=(t == 0), stop=(t == 1)) for tc in range(2)]
                self.mm("psC", emits, deps=[tdiff] if (bc == 0 and hh == 0) else ())
            self.evac("dve", "psC", lambda e, b=bc: e.tensor_add(
                self.gradT[:, b, :], self.gradT[:, b, :], self.psC[:]))
        tgr = self.S.tok("c_pe")
        self.war[("hdd_tm",)] = tgr
        self.war[("diff_tm",)] = tgr
        self.war[("hdd_fm",)] = tgr
        self.tile_end = [self.S.tok("c_pe"), self.S.tok("c_dve"), self.S.tok("c_act")]
        return self.tile_end

    def emit_p_chain(self):
        S = self.S
        st = self.stat
        c = lambda i: st[0:16, 4 * i:4 * i + 1]
        tsq = S.op("act", lambda e: e.activation(self.sqscr[0:16, :], self.sb_P[:],
                   AF.Square), deps=self.tok_perm, inc="c_act")
        t6 = self.fenced("dve", lambda e: e.reduce_sum(c(6), self.sqscr[0:16, :],
                         axis=mybir.AxisListType.X), deps=[tsq])
        self.war[("sqscr",)] = t6
        ts_ = self.fenced("act", lambda e: e.activation(c(7), c(6), AF.Sqrt), deps=[t6])
        S.op("dve", lambda e: e.tensor_scalar_max(c(7), c(7), EPS_NORM), deps=[ts_],
             inc="c_dve")
        self.dmy()
        tr = self.fenced("dve", lambda e: e.reciprocal(c(2), c(7)))
        tkn = S.op("dve", lambda e: e.tensor_scalar(
            self.kn_bf[0:16, :], self.sb_P[:], c(2), None, ALU.mult),
            deps=[tr], inc="c_dve")
        emits = [lambda e, k=kc: e.transpose(
            self.psT[:, k * 16:(k + 1) * 16],
            self.kn_bf[0:16, k * 128:(k + 1) * 128],
            self.sb_identbf[0:16, 0:16]) for kc in range(NCH)]
        self.mm("psT", emits, deps=[tkn])
        tknf = self.evac("act", "psT", lambda e: ecopy(e,
            self.kn_fm[:, :, 0:16],
            self.psT[:, 0:128].rearrange("p (c n) -> p c n", c=NCH)))
        for mc in range(NCH):
            ps, pn = (self.psA, "psA") if mc % 2 == 0 else (self.psB, "psB")
            emits = [lambda e, k=kc, m=mc, p=ps: e.matmul(
                p[:, 0:16], self.wWi[:, k, m * 128:(m + 1) * 128],
                self.kn_fm[:, k, 0:16], start=(k == 0), stop=(k == NCH - 1))
                for kc in range(NCH)]
            self.mm(pn, emits, deps=[tknf] if mc == 0 else ())
            self.evac("dve", pn, lambda e, m=mc, p=ps: e.tensor_scalar(
                self.kp_fm[:, m, 0:16], p[:, 0:16], self.c_bi[:, m:m + 1],
                None, ALU.add))
        for mc in range(NCH):
            ps, pn = (self.psA, "psA") if mc % 2 == 0 else (self.psB, "psB")
            emits = [lambda e, k=kc, m=mc, p=ps: e.matmul(
                p[:, 0:16], self.wWl1[:, k, m * 128:(m + 1) * 128],
                self.kp_fm[:, k, 0:16], start=(k == 0), stop=(k == NCH - 1))
                for kc in range(NCH)]
            self.mm(pn, emits)
            self.evac("act", pn, lambda e, m=mc, p=ps: e.activation(
                self.h1_fm[:, m, 0:16], p[:, 0:16], AF.Silu,
                bias=self.c_bl1[:, m:m + 1]))
        for hh in range(2):
            emits = [lambda e, k=kc, h=hh: e.matmul(
                self.psC[0:16, h * 512:(h + 1) * 512],
                self.h1_fm[:, k, 0:16],
                self.wWl2[:, k, h * 512:(h + 1) * 512],
                start=(k == 0), stop=False) for kc in range(NCH)]
            emits.append(lambda e, h=hh: e.matmul(
                self.psC[0:16, h * 512:(h + 1) * 512], self.sb_ones[0:1, 0:16],
                self.r_bl2[0:1, h * 512:(h + 1) * 512], start=False, stop=True))
            self.mm("psC", emits)
        thdd = self.evac("act", "psC", lambda e: e.activation(
            self.hdd_tm[0:16, 0, :], self.psC[0:16, :], AF.Silu))
        emits = [lambda e, k=kc: e.transpose(
            self.psT[:, k * 16:(k + 1) * 16],
            self.hdd_tm[0:16, 0, k * 128:(k + 1) * 128],
            self.sb_identbf[0:16, 0:16]) for kc in range(NCH)]
        self.mm("psT", emits, deps=[thdd])
        thf = self.evac("dve", "psT", lambda e: ecopy(e,
            self.hdd_fm[:, :, 0:16],
            self.psT[:, 0:128].rearrange("p (c n) -> p c n", c=NCH)))
        for hh in range(2):
            emits = [lambda e, k=kc, h=hh: e.matmul(
                self.psD[0:16, h * 512:(h + 1) * 512],
                self.hdd_fm[:, k, 0:16],
                self.wmemW[:, k, h * 512:(h + 1) * 512],
                start=(k == 0), stop=False) for kc in range(NCH)]
            emits.append(lambda e, h=hh: e.matmul(
                self.psD[0:16, h * 512:(h + 1) * 512], self.sb_ones[0:1, 0:16],
                self.r_memb[0:1, h * 512:(h + 1) * 512], start=False, stop=False))
            for mc in range(hh * 4, hh * 4 + 4):
                emits.append(lambda e, m=mc, h=hh: e.matmul(
                    self.psD[0:16, m * 128:(m + 1) * 128],
                    self.kp_fm[:, m, 0:16],
                    self.sb_negI[:], start=False, stop=(m == hh * 4 + 3)))
            self.mm("psD", emits, deps=[thf] if hh == 0 else ())
        tdiff = self.evac("dve", "psD", lambda e: e.tensor_copy(
            self.diff_tm[0:16, 0, :], self.psD[0:16, :]))
        tds = S.op("dve", lambda e: e.tensor_scalar(
            self.diff_tm[0:16, 0, :], self.diff_tm[0:16, 0, :],
            self.sb_pscale[0:16, 0:1], None, ALU.mult), inc="c_dve")
        for bc in range(NCH):
            for hh in range(2):
                emits = [lambda e, b=bc, h=hh: e.matmul(
                    self.psC[:, h * 512:(h + 1) * 512],
                    self.hdd_tm[0:16, 0, b * 128:(b + 1) * 128],
                    self.diff_tm[0:16, 0, h * 512:(h + 1) * 512],
                    start=True, stop=True)]
                self.mm("psC", emits, deps=[tds] if (bc == 0 and hh == 0) else ())
            self.evac("dve", "psC", lambda e, b=bc: e.tensor_add(
                self.gradT[:, b, :], self.gradT[:, b, :], self.psC[:]))
        tok = [self.S.tok("c_pe"), self.S.tok("c_dve"), self.S.tok("c_act")]
        for k in ("kn_fm", "kp_fm", "h1_fm", "hdd_tm", "diff_tm", "hdd_fm", "kn_bf"):
            self.war[(k,)] = self.S.tok("c_pe")
        self.tile_end = tok

    def alloc_stage2(self):
        self.ar.release()
        self.wWnew = self.ar.at("wWnew", [128, NCH, D], BF16, "wWi")
        self.ar.mark()
        self.st2_ga = self.ar.alloc("st2_ga", [128, NCH, D], F32)

    def emit_pair_allreduce(self):
        S = self.S
        toks = [self.S.tok("c_pe"), self.S.tok("c_dve"), self.S.tok("c_act"),
                self.S.tok("q_sp"), self.S.tok("q_pool")]
        for cch in range(NCH):
            S.op("pool", lambda e, cc=cch: e.indirect_dma_start(
                out=self.gr_slots[:],
                out_offset=IndirectOffsetOnAxis(self.sb_sig8[:, cc:cc + 1], 0),
                in_=self.gradT[:, cc, :], in_offset=None),
                deps=toks if cch == 0 else (), inc="q_pool")
        tw = self.S.tok("q_pool")
        S.op("pool", lambda e: e.indirect_dma_start(
            out=self.gr_flags[:],
            out_offset=IndirectOffsetOnAxis(self.sb_fio[:], 0),
            in_=self.sb_tok2[:], in_offset=None),
            deps=[tw], inc="q_pool")
        tf = self.S.tok("q_pool")
        def poll(e):
            with (e.register("rcond") as rcond, e.register("rexp") as rexp,
                  e.register("ra") as ra, e.register("rb") as rb,
                  e.register("rt0") as rt0, e.register("rt1") as rt1):
                e.reg_load(rt0, self.sb_tok2[0:1, 0:1])
                e.reg_load(rt1, self.sb_tok2[0:1, 1:2])
                e.reg_mov(rcond, 1)
                e.reg_mov(rexp, 0)
                with e.While(rcond):
                    e.indirect_dma_start(
                        out=self.sb_fbuf[:], out_offset=None,
                        in_=self.gr_flags[:],
                        in_offset=IndirectOffsetOnAxis(self.sb_fip[:], 0),
                    ).then_inc(self.S.sem["q_pf"], 16)
                    e.reg_add(rexp, rexp, 16)
                    e.wait_ge(self.S.sem["q_pf"], rexp)
                    e.reg_load(ra, self.sb_fbuf[0:1, 0:1])
                    e.reg_load(rb, self.sb_fbuf[0:1, 1:2])
                    e.reg_alu(ra, ra, rt0, ALU.not_equal)
                    e.reg_alu(rb, rb, rt1, ALU.not_equal)
                    e.reg_alu(rcond, ra, rb, ALU.bitwise_or)
            return e.nop()
        tp = S.op("pool", poll, deps=[tf], inc="c_pool")
        ga = self.st2_ga
        for cch in range(NCH):
            S.op("pool", lambda e, cc=cch: e.indirect_dma_start(
                out=ga[:, cc, :], out_offset=None,
                in_=self.gr_slots[:],
                in_offset=IndirectOffsetOnAxis(self.sb_gig8[:, cc:cc + 1], 0)),
                deps=[tp], inc="q_pool")
        tg = self.S.tok("q_pool")
        tsum = S.op("dve", lambda e: e.tensor_add(self.gradT[:], self.gradT[:], ga[:]),
                    deps=[tg], inc="c_dve")
        return self.emit_wnew([tsum])

    def emit_wnew(self, deps):
        S = self.S
        cgrad = 2.0 / (4 * 8202 * 1024)
        s1 = self.stat[:, 0:1]
        s2 = self.stat[:, 4:5]
        S.op("dve", lambda e: e.tensor_scalar(s1, self.sb_fg[:], -1.0, 1.0,
             ALU.mult, ALU.add), deps=deps, inc="c_dve")
        S.op("dve", lambda e: e.tensor_scalar_mul(s2, self.sb_lr[:], cgrad), inc="c_dve")
        self.dmy()
        for cch in range(NCH):
            S.op("dve", lambda e, cc=cch: e.tensor_scalar(
                self.sqscr[:], self.wmemW[:, cc, :], s1, None, ALU.mult), inc="c_dve")
            twn = S.op("dve", lambda e, cc=cch: e.scalar_tensor_tensor(
                self.wWnew[:, cc, :], self.gradT[:, cc, :], s2, self.sqscr[:],
                ALU.mult, ALU.add), inc="c_dve")
        return twn

    def alloc_phase2a(self):
        ar = self.ar
        self.wWo = ar.at("wWo", [128, NCH, D], BF16, "wmemW")
        self.wWg = ar.at("wWg", [128, 2 * NCH, D], BF16, "wWl1")
        _x1 = ar.alloc("a_x10", [128, 4, D], BF16)
        self.a_x1 = [_x1, _x1]
        _hd = ar.alloc("a_hdd0", [128, NCH, 512], BF16)
        self.a_hdd = [_hd, _hd]
        self.a_mout = ar.at("a_mout", [128, NCH, 512], BF16, "gradT")
        self.a_mofm = ar.alloc("a_mofm", [128, NCH, 512], BF16)
        self.a_motm = ar.alloc("a_motm", [128, 4, D], BF16)
        self.a_x1fm = ar.alloc("a_x1fm", [128, NCH, 512], BF16)
        self.a_gate = ar.alloc("a_gate", [128, 4, D], BF16)
        self.a_x2tm = ar.alloc("a_x2tm", [128, 4, D], BF16)
        self.a_x2fm = ar.alloc("a_x2fm", [128, NCH, 512], BF16)

    def load_phase2a_weights(self):
        S = self.S
        deps = list(self.tok_w2a) if hasattr(self, "tok_w2a") else []
        t1 = S.op("sp", lambda e: e.dma_start(
            out=self.wWo[:], in_=self.WoT.rearrange("(c p) m -> p c m", p=128)),
            deps=deps, inc="q_sp")
        t2 = S.op("sp", lambda e: e.dma_start(
            out=self.wWg[:], in_=self.WgT.rearrange("(c p) m -> p c m", p=128)),
            inc="q_sp")
        self.tok_w2a_done = [t1, t2]

    def emit_tile_2a(self, tile):
        S = self.S
        st = self.stat
        c = lambda i: st[:, 4 * i:4 * i + 1]
        buf = tile % 2
        x1 = self.a_x1[buf]
        hdd = self.a_hdd[buf]
        r0, r1 = tile * 512, (tile + 1) * 512
        def wl(key):
            v = self.war.get(key)
            if not v:
                return []
            return list(v) if isinstance(v, list) else [v]
        tlx = S.op("sp", lambda e: e.dma_start(
            out=x1[:], in_=self.x1_spill[r0:r1].rearrange("(c p) m -> p c m", p=128)),
            deps=wl(("a_x1", buf)) + self.tok_w2a_done, inc="q_sp")
        tlh = S.op("sp", lambda e: e.dma_start(
            out=hdd[:], in_=self.hdd_spill.rearrange("(c p) t -> p c t", p=128)[:, :, r0:r1]),
            deps=[self.war.get(("a_hdd", buf)) or ("c_pe", 0)], inc="q_sp")
        for mc in range(NCH):
            ps, pn = (self.psA, "psA") if mc % 2 == 0 else (self.psB, "psB")
            emits = [lambda e, k=kc, m=mc, p=ps: e.matmul(
                p[:], self.wWnew[:, k, m * 128:(m + 1) * 128],
                hdd[:, k, :], start=(k == 0), stop=(k == NCH - 1))
                for kc in range(NCH)]
            self.mm(pn, emits, deps=[tlh, self.war.get(("a_mout",)) or ("c_pe", 0)]
                    if mc == 0 else ())
            self.evac("dve", pn, lambda e, m=mc, p=ps: e.tensor_scalar(
                self.a_mout[:, m, :], p[:], self.c_memb[:, m:m + 1], None, ALU.add))
        tmo0 = self.S.tok("c_dve")
        for mc in range(NCH):
            ps, pn = (self.psA, "psA") if mc % 2 == 0 else (self.psB, "psB")
            emits = [lambda e, k=kc, m=mc, p=ps: e.matmul(
                p[:], self.wWo[:, k, m * 128:(m + 1) * 128],
                self.a_mout[:, k, :], start=(k == 0), stop=(k == NCH - 1))
                for kc in range(NCH)]
            self.mm(pn, emits, deps=[tmo0, self.war.get(("a_mofm",)) or ("c_pe", 0)]
                    if mc == 0 else ())
            self.evac("dve", pn, lambda e, m=mc, p=ps: e.tensor_scalar(
                self.a_mofm[:, m, :], p[:], self.c_bo[:, m:m + 1], None, ALU.add))
        tmofm = self.S.tok("c_dve")
        for tc in range(4):
            self.transpose_tm_to_fm(x1[:, tc, :], self.a_x1fm, tc * 128,
                                    deps=[tlx], eng="dve",
                                    war=self.war.get(("a_x1fm",)))
        tx1f = self.S.tok("c_dve")
        for tc in range(4):
            emits = [lambda e, k=kc, t=tc: e.transpose(
                self.psT[:, k * 128:(k + 1) * 128],
                self.a_mofm[:, k, t * 128:(t + 1) * 128],
                self.sb_identbf[:]) for kc in range(NCH)]
            self.mm("psT", emits, deps=[tmofm] if tc == 0 else ())
            self.evac("dve", "psT", lambda e, t=tc: e.tensor_copy(
                self.a_motm[:, t, :], self.psT[:]),
                deps=[self.war.get(("a_motm",)) or ("c_pe", 0)] if tc == 0 else ())
        tmotm = self.S.tok("c_dve")
        for tc in range(4):
            for hh in range(2):
                emits = []
                for kc in range(NCH):
                    emits.append(lambda e, k=kc, t=tc, h=hh: e.matmul(
                        self.psC[:, h * 512:(h + 1) * 512],
                        self.a_x1fm[:, k, t * 128:(t + 1) * 128],
                        self.wWg[:, k, h * 512:(h + 1) * 512],
                        start=(k == 0), stop=False))
                for kc in range(NCH):
                    emits.append(lambda e, k=kc, t=tc, h=hh: e.matmul(
                        self.psC[:, h * 512:(h + 1) * 512],
                        self.a_mofm[:, k, t * 128:(t + 1) * 128],
                        self.wWg[:, NCH + k, h * 512:(h + 1) * 512],
                        start=False, stop=False))
                emits.append(lambda e, h=hh: e.matmul(
                    self.psC[:, h * 512:(h + 1) * 512], self.sb_ones[0:1, 0:128],
                    self.r_bg[0:1, h * 512:(h + 1) * 512], start=False, stop=True))
                self.mm("psC", emits, deps=[tx1f, tmofm] if (tc == 0 and hh == 0) else ())
            self.evac("act", "psC", lambda e, t=tc: e.activation(
                self.a_gate[:, t, :], self.psC[:], AF.Sigmoid))
        tgate = self.S.tok("c_act")
        tx2 = [None] * 4
        for tc in range(4):
            t1 = S.op("dve", lambda e, t=tc: e.tensor_sub(
                self.a_scr1[:], x1[:, t, :], self.a_motm[:, t, :]),
                deps=[tmotm, tgate] if tc == 0 else (), inc="c_dve")
            S.op("dve", lambda e, t=tc: e.tensor_mul(
                self.a_scr1[:], self.a_scr1[:], self.a_gate[:, t, :]), inc="c_dve")
            S.op("dve", lambda e, t=tc: e.tensor_add(
                self.a_scr2[:], x1[:, t, :], self.a_motm[:, t, :]), inc="c_dve")
            tsum = S.op("dve", lambda e: e.tensor_add(
                self.a_scr2[:], self.a_scr2[:], self.a_scr1[:]), inc="c_dve")
            tst = self.ln_stats(self.a_scr2[:], deps=[tsum])
            S.op("dve", lambda e: e.tensor_scalar(
                self.sqscr[:], self.a_scr2[:], c(0), c(1), ALU.subtract, ALU.mult),
                deps=[tst], inc="c_dve")
            S.op("dve", lambda e: e.tensor_mul(self.a_scr2[:], self.sqscr[:],
                                               self.bc_g2[:]), inc="c_dve")
            self.war[("sqscr",)] = self.S.tok("c_dve")
            tx2[tc] = S.op("dve", lambda e, t=tc: e.tensor_add(
                self.a_x2tm[:, t, :], self.a_scr2[:], self.bc_be2[:]),
                deps=[self.war.get(("a_x2tm",)) or ("c_pe", 0)] if tc == 0 else (),
                inc="c_dve")
            self.transpose_tm_to_fm(self.a_x2tm[:, tc, :], self.a_x2fm, tc * 128,
                                    deps=[tx2[tc]], eng="act",
                                    war=[t for t in (self.war.get(("a_x2fm",)),) if t])
        tx2f = self.S.tok("c_act")
        ts1 = S.op("sp", lambda e: e.dma_start(
            out=self.x2tm_spill[r0:r1].rearrange("(c p) m -> p c m", p=128),
            in_=self.a_x2tm[:]), deps=[tx2[3], self.war.get(("x2sp",)) or ("c_sp", 0)],
            inc="q_sp")
        ts2 = S.op("sp", lambda e: e.dma_start(
            out=self.x2fm_spill.rearrange("(c p) t -> p c t", p=128)[:, :, r0:r1],
            in_=self.a_x2fm[:]), deps=[tx2f], inc="q_sp")
        self.war[("x2sp",)] = ts2
        tpe = self.S.tok("c_pe")
        self.war[("a_x1", buf)] = [self.S.tok("c_dve"), tpe]
        self.war[("a_hdd", buf)] = tpe
        self.war[("a_mout",)] = tpe
        self.war[("a_mofm",)] = tpe
        self.war[("a_motm",)] = self.S.tok("c_dve")
        self.war[("a_x1fm",)] = tpe
        self.war[("a_x2fm",)] = ts2
        self.war[("a_x2tm",)] = ts1
        self.war[("a_gate",)] = self.S.tok("c_dve")


    def alloc_phase2bi(self):
        ar = self.ar
        self.wW1 = ar.at("wW1", [128, NCH, FF], BF16, "wWi")
        self.wW2a = ar.at("wW2a", [128, 16, D], BF16, "gradT")
        self.wW2b = ar.alloc("wW2b", [128, 16, D], BF16)
        _s = ar.alloc("b_x2f0", [128, NCH, 512], BF16)
        self.b_x2f = [_s, _s]
        self.b_h = ar.alloc("b_h", [128, 32, 256], BF16)
        self.b_ff = ar.alloc("b_ff", [128, 4, D], BF16)

    def load_phase2bi_weights(self, deps):
        S = self.S
        t1 = S.op("sp", lambda e: e.dma_start(
            out=self.wW1[:], in_=self.W1T.rearrange("(c p) m -> p c m", p=128)),
            deps=deps, inc="q_sp")
        t2 = S.op("sp", lambda e: e.dma_start(
            out=self.wW2a[:], in_=self.W2T[0:2048].rearrange("(c p) m -> p c m", p=128)),
            deps=deps, inc="q_sp")
        t3 = S.op("sp", lambda e: e.dma_start(
            out=self.wW2b[:], in_=self.W2T[2048:4096].rearrange("(c p) m -> p c m", p=128)),
            deps=deps, inc="q_sp")
        self.tok_w2b = [t1, t2, t3]

    def emit_tile_2bi(self, tile):
        S = self.S
        buf = tile % 2
        x2f = self.b_x2f[buf]
        r0, r1 = tile * 512, (tile + 1) * 512
        tlx = S.op("sp", lambda e: e.dma_start(
            out=x2f[:], in_=self.x2fm_spill.rearrange("(c p) t -> p c t", p=128)[:, :, r0:r1]),
            deps=[self.war.get(("b_x2f", buf)) or ("c_pe", 0)] + self.tok_w2b,
            inc="q_sp")
        for th in range(2):
            ts0 = th * 256
            for hc in range(32):
                ps, pn = (self.psA, "psA") if hc % 2 == 0 else (self.psB, "psB")
                emits = [lambda e, k=kc, h=hc, p=ps, t0=ts0: e.matmul(
                    p[:, 0:256], self.wW1[:, k, h * 128:(h + 1) * 128],
                    x2f[:, k, t0:t0 + 256], start=(k == 0), stop=(k == NCH - 1))
                    for kc in range(NCH)]
                self.mm(pn, emits,
                        deps=[tlx, self.war.get(("b_h",)) or ("c_pe", 0)]
                        if hc == 0 else ())
                self.evac("act", pn, lambda e, h=hc, p=ps: e.activation(
                    self.b_h[:, h, :], p[:, 0:256], AF.Silu,
                    bias=self.c_b1[:, h:h + 1]))
            th_tok = self.S.tok("c_act")
            for tc2 in range(2):
                for hh in range(2):
                    emits = []
                    for kc in range(32):
                        w = self.wW2a if kc < 16 else self.wW2b
                        emits.append(lambda e, k=kc, t=tc2, h=hh, ww=w: e.matmul(
                            self.psC[:, h * 512:(h + 1) * 512],
                            self.b_h[:, k, t * 128:(t + 1) * 128],
                            ww[:, k % 16, h * 512:(h + 1) * 512],
                            start=(k == 0), stop=False))
                    emits.append(lambda e, h=hh: e.matmul(
                        self.psC[:, h * 512:(h + 1) * 512], self.sb_ones[0:1, 0:128],
                        self.r_b2[0:1, h * 512:(h + 1) * 512], start=False, stop=True))
                    self.mm("psC", emits, deps=[th_tok] if (tc2 == 0 and hh == 0) else ())
                self.evac("dve", "psC", lambda e, s=th * 2 + tc2: e.tensor_copy(
                    self.b_ff[:, s, :], self.psC[:]),
                    deps=[self.war.get(("b_ff",)) or ("c_pe", 0)]
                    if (th == 0 and tc2 == 0) else ())
            self.war[("b_h",)] = self.S.tok("c_pe")
        tffe = self.S.tok("c_dve")
        tsp = S.op("sp", lambda e: e.dma_start(
            out=self.ff_spill[r0:r1].rearrange("(c p) m -> p c m", p=128),
            in_=self.b_ff[:]), deps=[tffe, self.war.get(("ffsp",)) or ("c_sp", 0)],
            inc="q_sp")
        self.war[("ffsp",)] = tsp
        self.war[("b_ff",)] = tsp
        self.war[("b_x2f", buf)] = self.S.tok("c_pe")

    def alloc_phase2bii(self):
        ar = self.ar
        self.t_ff = ar.alloc("t_ff", [128, 4, D], BF16)
        self.t_x2 = ar.alloc("t_x2", [128, 4, D], BF16)
        self.t_x = ar.alloc("t_x", [128, 4, D], F32)
        self.t_y1 = ar.alloc("t_y1", [128, 4, D], F32)
        self.t_x4f = ar.alloc("t_x4f", [128, NCH, 512], BF16)
        self.t_xe = []
        self.t_xef = []
        for k in range(4):
            t = ar.alloc(f"t_xe{k}", [128, XW], BF16)
            f = self.nc.alloc_sbuf_tensor_at(f"t_xef{k}", [128, XW // 2], F32,
                                             offset=ar.names[f"t_xe{k}"])
            self.t_xe.append(t)
            self.t_xef.append(f)
        self.rt_pr = ar.alloc("rt_pr", [128, E], F32)
        self.rt_cur = ar.alloc("rt_cur", [128, E], F32)
        self.rt_oh = ar.alloc("rt_oh", [128, 4, E], F32)
        self.rt_w = ar.alloc("rt_w", [128, 8], F32)
        self.rt_selbf = ar.alloc("rt_selbf", [128, E], BF16)
        self.rt_selem = ar.alloc("rt_selem", [32, 128], BF16)
        self.rt_sv = ar.alloc("rt_sv", [32, 128], F32)
        self.rt_ovf = ar.alloc("rt_ovf", [32, 128], F32)
        self.rt_slottm = ar.alloc("rt_slottm", [128, E], F32)
        self.rt_scr = ar.alloc("rt_scr", [128, E], F32)
        self.rt_base = ar.alloc("rt_base", [32, 4], F32)
        self.rt_dsti = ar.alloc("rt_dsti", [128, 4], I32)
        self.rt_exp = ar.alloc("rt_exp", [128, E], F32)

    def emit_tile_2bii(self, tile, first):
        S = self.S
        st = self.stat
        c = lambda i: st[:, 4 * i:4 * i + 1]
        r0, r1 = tile * 512, (tile + 1) * 512
        def wl(key):
            v = self.war.get(key)
            return ([] if not v else (list(v) if isinstance(v, list) else [v]))
        tlf = S.op("sp", lambda e: e.dma_start(
            out=self.t_ff[:], in_=self.ff_spill[r0:r1].rearrange("(c p) m -> p c m", p=128)),
            deps=wl(("t_ff",)) + (self.tok_w2bii if first else []), inc="q_sp")
        tlx2 = S.op("sp", lambda e: e.dma_start(
            out=self.t_x2[:], in_=self.x2tm_spill[r0:r1].rearrange("(c p) m -> p c m", p=128)),
            deps=wl(("t_x2",)), inc="q_sp")
        tlx = S.op("sp", lambda e: e.dma_start(
            out=self.t_x[:], in_=self.x_own[r0:r1].rearrange("(c p) m -> p c m", p=128)),
            deps=wl(("t_x",)), inc="q_sp")
        if first:
            S.op("dve", lambda e: e.memset(self.rt_base[:], 0.0), inc="c_dve")
            self.dmy()
            S.op("pool", lambda e: (setattr(self, "breg_xbuf", e.to_reg(E * CAP - 1)),
                                    setattr(self, "breg_yp", e.to_reg(4 * self.T_OWN - 1)),
                                    e.nop())[-1], inc="c_pool")
        for tc in range(4):
            gtok = tile * 512 + tc * 128
            t1 = S.op("dve", lambda e, t=tc: e.tensor_add(
                self.a_scr1[:], self.t_x2[:, t, :], self.t_ff[:, t, :]),
                deps=[tlf, tlx2] if tc == 0 else (), inc="c_dve")
            tst = self.ln_stats(self.a_scr1[:], deps=[t1])
            S.op("dve", lambda e: e.tensor_scalar(
                self.sqscr[:], self.a_scr1[:], c(0), c(1), ALU.subtract, ALU.mult),
                deps=[tst], inc="c_dve")
            S.op("dve", lambda e: e.tensor_mul(self.a_scr1[:], self.sqscr[:],
                                               self.bc_g3[:]), inc="c_dve")
            t3 = S.op("dve", lambda e: e.tensor_add(self.a_scr1[:], self.a_scr1[:],
                                                    self.bc_be3[:]), inc="c_dve")
            self.war[("sqscr",)] = t3
            tst2 = self.ln_stats(self.a_scr1[:], deps=[t3])
            ty1 = S.op("dve", lambda e, t=tc: e.tensor_scalar(
                self.t_y1[:, t, :], self.a_scr1[:], c(0), c(1),
                ALU.subtract, ALU.mult),
                deps=[tst2] + (wl(("t_y1b",)) if tc == 0 else []), inc="c_dve")
            ty1b = S.op("dve", lambda e, t=tc: e.tensor_add(
                self.t_y1[:, t, :], self.t_y1[:, t, :], self.t_x[:, t, :]),
                deps=[tlx] if tc == 0 else (), inc="c_dve")
            tst3 = self.ln_stats(self.t_y1[:, tc, :], deps=[ty1b])
            tx4 = S.op("dve", lambda e, t=tc: e.tensor_scalar(
                self.t_xe[0][:, 0:D], self.t_y1[:, t, :], c(0), c(1),
                ALU.subtract, ALU.mult),
                deps=[tst3] + wl(("t_xe",)), inc="c_dve")
            for k in range(1, 4):
                S.op("dve", lambda e, k=k: e.tensor_copy(
                    self.t_xe[k][:, 0:D], self.t_xe[0][:, 0:D]), inc="c_dve")
            txe = self.S.tok("c_dve")
            self.transpose_tm_to_fm(self.t_xe[0][:, 0:D], self.t_x4f, tc * 128,
                                    deps=[txe], eng="act",
                                    war=wl(("t_x4f",)) if tc == 0 else None)
            tx4f = self.S.tok("c_act")
            emits = [lambda e, k=kc, t=tc: e.matmul(
                self.psE[:, 0:E],
                self.t_x4f[:, k, t * 128:(t + 1) * 128],
                self.wrWg[:, k, :], start=(k == 0), stop=(k == NCH - 1))
                for kc in range(NCH)]
            self.mm("psE", emits, deps=[tx4f])
            tmx = self.fenced("dve", lambda e: e.reduce_max(
                c(9), self.psE[:, 0:E], axis=mybir.AxisListType.X))
            tng = self.fenced("dve", lambda e: e.tensor_scalar_mul(c(10), c(9), -1.0))
            texp = S.op("act", lambda e: e.activation(
                self.rt_exp[:], self.psE[:, 0:E], AF.Exp, bias=c(10)),
                deps=[tng], inc="c_act")
            self.pslast["psE"]["rd"] = texp
            tsm = self.fenced("dve", lambda e: e.reduce_sum(
                c(11), self.rt_exp[:], axis=mybir.AxisListType.X), deps=[texp])
            trc = self.fenced("dve", lambda e: e.reciprocal(c(12), c(11)), deps=[tsm])
            tpr = S.op("dve", lambda e: e.tensor_scalar(
                self.rt_pr[:], self.rt_exp[:], c(12), None, ALU.mult), inc="c_dve")
            self.dmy()
            S.op("dve", lambda e: e.tensor_copy(self.rt_cur[:], self.rt_pr[:]),
                 inc="c_dve")
            for k in range(4):
                self.dmy()
                tmk = self.fenced("dve", lambda e: e.reduce_max(
                    c(9), self.rt_cur[:], axis=mybir.AxisListType.X))
                S.op("dve", lambda e, k=k: e.tensor_copy(self.rt_w[:, k:k + 1], c(9)),
                     inc="c_dve")
                S.op("dve", lambda e, k=k: e.tensor_scalar(
                    self.rt_oh[:, k, :], self.rt_cur[:], c(9), None, ALU.is_ge),
                    inc="c_dve")
                if k < 3:
                    self.dmy()
                    S.op("dve", lambda e, k=k: e.scalar_tensor_tensor(
                        self.rt_cur[:], self.rt_oh[:, k, :], -1e30, self.rt_cur[:],
                        ALU.mult, ALU.add), inc="c_dve")
            self.dmy()
            tsel = S.op("dve", lambda e: e.tensor_scalar(
                self.rt_selbf[:], self.rt_pr[:], c(9), None, ALU.is_ge), inc="c_dve")
            emits = [lambda e: e.transpose(self.psT[0:E, 0:128], self.rt_selbf[:],
                                           self.sb_identbf[:])]
            self.mm("psT", emits, deps=[tsel])
            tse = self.evac("dve", "psT", lambda e: e.tensor_copy(
                self.rt_selem[:], self.psT[0:E, 0:128]))
            emits = [lambda e: e.matmul(self.psA[0:E, 0:128], self.rt_selbf[:],
                                        self.sb_ltri128[:], start=True, stop=True)]
            self.mm("psA", emits, deps=[tsel])
            tsv = self.evac("dve", "psA", lambda e: e.tensor_scalar(
                self.rt_sv[:], self.psA[0:E, 0:128], self.rt_base[0:E, 0:1], None,
                ALU.add))
            self.dmy()
            tbu = S.op("dve", lambda e: e.tensor_add(
                self.rt_base[0:E, 0:1], self.rt_sv[:, 127:128],
                self.rt_selem[0:E, 127:128]), deps=[tse, tsv], inc="c_dve")
            tov = S.op("dve", lambda e: e.tensor_scalar(
                self.rt_ovf[:], self.rt_sv[:], float(CAP) - 0.5, None, ALU.is_ge),
                inc="c_dve")
            S.op("dve", lambda e: e.tensor_scalar(
                self.rt_sv[:], self.rt_sv[:], self.sb_ebase[0:E, 0:1], None, ALU.add),
                inc="c_dve")
            self.dmy()
            tslot = S.op("dve", lambda e: e.scalar_tensor_tensor(
                self.rt_sv[:], self.rt_ovf[:], 1e6, self.rt_sv[:],
                ALU.mult, ALU.add), inc="c_dve")
            emits = [lambda e: e.transpose(self.psE[:, 32:64], self.rt_sv[:],
                                           self.sb_identf32[0:E, 0:E])]
            self.mm("psE", emits, deps=[tslot])
            tstm = self.evac("dve", "psE", lambda e: e.tensor_copy(
                self.rt_slottm[:], self.psE[:, 32:64]))
            emits = [lambda e: e.matmul(self.psB[:, 0:E], self.rt_selem[:],
                                        self.sb_ltri32[:], start=True, stop=True)]
            self.mm("psB", emits, deps=[tse])
            ttk = self.fenced("dve", lambda e, g=gtok: e.tensor_scalar(
                c(13), self.sb_tokp[:, 0:1], 1.0, float(g), ALU.mult, ALU.add))
            todst = self.evac("dve", "psB", lambda e: e.tensor_scalar(
                self.rt_scr[:], self.psB[:, 0:E], float(self.T_OWN), c(13),
                ALU.mult, ALU.add), deps=[ttk])
            for k in range(4):
                S.op("dve", lambda e, k=k: e.tensor_mul(
                    self.rt_cur[:], self.rt_slottm[:], self.rt_oh[:, k, :]),
                    deps=[tstm] if k == 0 else (), inc="c_dve")
                self.dmy()
                td = self.fenced("dve", lambda e: e.reduce_sum(
                    c(14), self.rt_cur[:], axis=mybir.AxisListType.X))
                S.op("dve", lambda e, k=k: e.tensor_copy(
                    self.rt_dsti[:, k:k + 1], c(14)), inc="c_dve")
                S.op("dve", lambda e, k=k: e.tensor_mul(
                    self.rt_cur[:], self.rt_scr[:], self.rt_oh[:, k, :]), inc="c_dve")
                self.dmy()
                to = self.fenced("dve", lambda e: e.reduce_sum(
                    c(15), self.rt_cur[:], axis=mybir.AxisListType.X))
                S.op("dve", lambda e, k=k: e.tensor_copy(
                    self.t_xef[k][:, 513:514], c(15)), inc="c_dve")
                S.op("dve", lambda e, k=k: e.tensor_copy(
                    self.t_xef[k][:, 512:513], self.rt_w[:, k:k + 1]), inc="c_dve")
                S.op("dve", lambda e, k=k: e.tensor_copy(
                    self.t_xef[k][:, 514:515], self.sb_cst[:, 2:3]), inc="c_dve")
            self.dmy()
            tdone = self.S.tok("c_dve")
            if self.debug and hasattr(self, "dbg_rt"):
                ci = tile * 4 + tc
                S.op("sp", lambda e, ci=ci: e.dma_start(
                    out=self.dbg_rt[ci * 128:(ci + 1) * 128, 0:4],
                    in_=self.rt_dsti[:].bitcast(F32) if hasattr(self.rt_dsti[:], "bitcast")
                    else self.rt_dsti[:]), deps=[tdone], inc="q_sp")
                S.op("sp", lambda e, ci=ci: e.dma_start(
                    out=self.dbg_rt[ci * 128:(ci + 1) * 128, 4:36],
                    in_=self.rt_slottm[:]), deps=[tdone], inc="q_sp")
                S.op("sp", lambda e, ci=ci: e.dma_start(
                    out=self.dbg_rt[ci * 128:(ci + 1) * 128, 36:68],
                    in_=self.rt_pr[:]), deps=[tdone], inc="q_sp")
                S.op("sp", lambda e, ci=ci: e.dma_start(
                    out=self.dbg_rt[ci * 128:ci * 128 + 32, 68:72],
                    in_=self.rt_base[:]), deps=[tdone], inc="q_sp")
                S.op("sp", lambda e, ci=ci: e.dma_start(
                    out=self.dbg_rt[ci * 128:ci * 128 + 32, 72:100],
                    in_=self.rt_sv[:, 100:128]), deps=[tdone], inc="q_sp")
            for k in range(4):
                S.op("pool", lambda e, k=k: e.indirect_dma_start(
                    out=self.Xbuf[:],
                    out_offset=IndirectOffsetOnAxis(self.rt_dsti[:, k:k + 1], 0),
                    in_=self.t_xe[k][:], in_offset=None,
                    bounds_check=self.breg_xbuf, oob_is_err=False),
                    deps=[tdone], inc="q_pool")
            self.war[("t_xe",)] = self.S.tok("q_pool")
        ty1e = self.S.tok("c_dve")
        ts1 = S.op("sp", lambda e: e.dma_start(
            out=self.y1_spill[r0:r1].rearrange("(c p) m -> p c m", p=128),
            in_=self.t_y1[:]), deps=[ty1e] + wl(("y1sp",)), inc="q_sp")
        self.war[("y1sp",)] = ts1
        self.war[("t_y1b",)] = ts1
        ts2 = S.op("sp", lambda e: e.dma_start(
            out=self.x4fm_spill.rearrange("(c p) t -> p c t", p=128)[:, :, r0:r1],
            in_=self.t_x4f[:]), deps=[self.S.tok("c_act"), self.S.tok("c_pe")]
            + wl(("x4sp",)), inc="q_sp")
        self.war[("x4sp",)] = ts2
        self.war[("t_x4f",)] = ts2
        self.war[("t_ff",)] = self.S.tok("c_dve")
        self.war[("t_x2",)] = self.S.tok("c_dve")
        self.war[("t_x",)] = self.S.tok("c_dve")

    def alloc_phase3b(self):
        ar = self.ar
        self.wE1 = [ar.at("wE1_0", [128, NCH, EH], BF16, "wWi"),
                    ar.at("wE1_1", [128, NCH, EH], BF16, "wWl1")]
        self.wE3 = [ar.at("wE3_0", [128, NCH, EH], BF16, "wWl2"),
                    ar.at("wE3_1", [128, NCH, EH], BF16, "wmemW")]
        self.wE2 = [ar.at("wE2_0", [128, 4, D], BF16, "gradT")]
        self.wE2.append(self.nc.alloc_sbuf_tensor_at(
            "wE2_1", [128, 4, D], BF16, offset=ar.names["gradT"] + 8192))
        self.e_xg = [ar.alloc(f"e_xg{i}", [128, NCH, CAP], BF16) for i in range(2)]
        self.e_meta = []
        self.e_metaf = []
        for i in range(2):
            t = ar.alloc(f"e_meta{i}", [128, 10, 6], BF16)
            f = self.nc.alloc_sbuf_tensor_at(f"e_metaf{i}", [128, 10, 3], F32,
                                             offset=ar.names[f"e_meta{i}"])
            self.e_meta.append(t)
            self.e_metaf.append(f)
        self.e_hsw = ar.alloc("e_hsw", [128, 4, CAP], BF16)
        self.e_h1s = ar.alloc("e_h1s", [128, 4, 512], BF16)
        self.e_out = [ar.alloc(f"e_out{i}", [128, D], BF16) for i in range(2)]
        self.e_w = [ar.alloc(f"e_w{i}", [128, 10], F32) for i in range(2)]
        self.e_oi = [ar.alloc(f"e_oi{i}", [128, 10], I32) for i in range(2)]

    def emit_expert(self, ei, xdep):
        S = self.S
        buf = ei % 2
        def wl(key):
            v = self.war.get(key)
            return ([] if not v else (list(v) if isinstance(v, list) else [v]))
        tw = []
        tw.append(S.op("sp", lambda e, d=self.wE1[buf]: e.dma_start(
            out=d[:], in_=self.eW1[ei].rearrange("(c p) h -> p c h", p=128)),
            deps=wl(("we", buf)) + list(xdep), inc="q_sp"))
        tw.append(S.op("sp", lambda e, d=self.wE3[buf]: e.dma_start(
            out=d[:], in_=self.eW3[ei].rearrange("(c p) h -> p c h", p=128)),
            deps=wl(("we", buf)) + list(xdep), inc="q_sp"))
        tw.append(S.op("sp", lambda e: e.dma_start(
            out=self.wE2[buf][:], in_=self.eW2[ei].rearrange("(c p) h -> p c h", p=128)),
            deps=wl(("we", buf)), inc="q_sp"))
        txg = []
        for kc in range(NCH):
            txg.append(S.op("act", lambda e, k=kc: e.dma_start(
                out=self.e_xg[buf][:, k, :],
                in_=self.Xbuf[ei * CAP:(ei + 1) * CAP, k * 128:(k + 1) * 128],
                transpose=True),
                deps=(wl(("xg", buf)) + list(xdep)) if kc == 0 else (), inc="q_act"))
        tme = S.op("pool", lambda e: e.dma_start(
            out=self.e_meta[buf][:],
            in_=self.Xbuf[ei * CAP:(ei + 1) * CAP, 1024:1030].rearrange(
                "(j p) w -> p j w", p=128)),
            deps=wl(("meta", buf)) + list(xdep), inc="q_pool")
        mf = self.e_metaf[buf]
        st = self.stat
        c = lambda i: st[:, 4 * i:4 * i + 1]
        tov = []
        for j in range(10):
            tv = S.op("dve", lambda e, j=j: e.tensor_scalar(
                self.sb_imask[:, 0:1], mf[:, j, 2:3], self.sb_cst[:, 2:3], None,
                ALU.is_equal), deps=[tme] if j == 0 else (), inc="c_dve")
            self.dmy()
            S.op("dve", lambda e, j=j: e.select(
                c(10), self.sb_imask[:, 0:1], mf[:, j, 1:2], self.sb_cst[:, 3:4]),
                inc="c_dve")
            S.op("dve", lambda e, j=j, b=buf: e.select(
                self.e_w[b][:, j:j + 1], self.sb_imask[:, 0:1], mf[:, j, 0:1],
                self.sb_cst[:, 4:5]), inc="c_dve")
            self.dmy()
            tov.append(S.op("dve", lambda e, j=j, b=buf: e.tensor_copy(
                self.e_oi[b][:, j:j + 1], c(10)), inc="c_dve"))
        tmeta_done = self.S.tok("c_dve")
        cts = [(0, 512), (512, 512), (1024, 256)]
        for (c0, cw) in cts:
            for hc in range(4):
                ps, pn = (self.psA, "psA") if hc % 2 == 0 else (self.psB, "psB")
                emits = [lambda e, k=kc, h=hc, p=ps, q0=c0, qw=cw, b=buf: e.matmul(
                    p[:, 0:qw], self.wE1[b][:, k, h * 128:(h + 1) * 128],
                    self.e_xg[b][:, k, q0:q0 + qw], start=(k == 0), stop=(k == NCH - 1))
                    for kc in range(NCH)]
                self.mm(pn, emits, deps=[txg[4], txg[-1], tw[0],
                        self.war.get(("e_h1s",)) or ("c_pe", 0)]
                        if (c0 == 0 and hc == 0) else ())
                self.evac("act", pn, lambda e, h=hc, p=ps, qw=cw: e.activation(
                    self.e_h1s[:, h, 0:qw], p[:, 0:qw], AF.Silu))
            th1 = self.S.tok("c_act")
            for hc in range(4):
                ps, pn = (self.psA, "psA") if hc % 2 == 0 else (self.psB, "psB")
                emits = [lambda e, k=kc, h=hc, p=ps, q0=c0, qw=cw, b=buf: e.matmul(
                    p[:, 0:qw], self.wE3[b][:, k, h * 128:(h + 1) * 128],
                    self.e_xg[b][:, k, q0:q0 + qw], start=(k == 0), stop=(k == NCH - 1))
                    for kc in range(NCH)]
                self.mm(pn, emits, deps=[tw[1], self.war.get(("e_hsw",)) or ("c_pe", 0)]
                        if (c0 == 0 and hc == 0) else ())
                self.evac("dve", pn, lambda e, h=hc, p=ps, q0=c0, qw=cw: e.tensor_tensor(
                    self.e_hsw[:, h, q0:q0 + qw], self.e_h1s[:, h, 0:qw],
                    p[:, 0:qw], ALU.mult), deps=[th1])
            self.war[("e_h1s",)] = self.S.tok("c_pe")
        thsw = self.S.tok("c_dve")
        for j in range(10):
            ob = j % 2
            for hh in range(2):
                emits = [lambda e, k=kc, h=hh, j=j, b=buf: e.matmul(
                    self.psC[:, h * 512:(h + 1) * 512],
                    self.e_hsw[:, k, j * 128:(j + 1) * 128],
                    self.wE2[b][:, k, h * 512:(h + 1) * 512],
                    start=(k == 0), stop=(k == 3)) for kc in range(4)]
                self.mm("psC", emits, deps=[thsw, tw[2]] if (j == 0 and hh == 0) else ())
            tev = self.evac("dve", "psC", lambda e, j=j, b=buf, o=ob: e.tensor_scalar(
                self.e_out[o][:], self.psC[:], self.e_w[b][:, j:j + 1], None,
                ALU.mult), deps=wl(("e_out", ob)) + [tmeta_done])
            tsc = S.op("pool", lambda e, j=j, b=buf, o=ob: e.indirect_dma_start(
                out=self.Ypair[:],
                out_offset=IndirectOffsetOnAxis(self.e_oi[b][:, j:j + 1], 0),
                in_=self.e_out[o][:], in_offset=None,
                bounds_check=self.breg_yp, oob_is_err=False),
                deps=[tev], inc="q_pool")
            self.war[("e_out", ob)] = tsc
        tpe = self.S.tok("c_pe")
        self.war[("we", buf)] = tpe
        self.war[("xg", buf)] = tpe
        self.war[("meta", buf)] = self.S.tok("c_dve")
        self.war[("e_hsw",)] = tpe

    def alloc_phase3c(self):
        ar = self.ar
        self.wZ1 = ar.at("wZ1", [128, NCH, HS], BF16, "wWi")
        self.wZ3 = ar.at("wZ3", [128, NCH, HS], BF16, "wWl2")
        self.wZ2 = ar.at("wZ2", [128, NCH, D], BF16, "gradT")
        self.z_x4 = [ar.alloc(f"z_x4{i}", [128, NCH, 512], BF16) for i in range(2)]
        self.z_y1 = ar.alloc("z_y1", [128, 4, D], F32)
        self.z_yp = ar.alloc("z_yp", [128, 4, D], BF16)
        self.z_h1s = ar.alloc("z_h1s", [128, NCH, 512], BF16)
        self.z_sw = ar.alloc("z_sw", [128, NCH, 512], BF16)
        self.z_out = [ar.alloc(f"z_out{i}", [128, D], F32) for i in range(2)]

    def load_phase3c_weights(self, deps):
        S = self.S
        tws = []
        for w, srcd in ((self.wZ1, self.zWs1), (self.wZ3, self.zWs3), (self.wZ2, self.zWs2)):
            tws.append(S.op("sp", lambda e, d=w, s=srcd: e.dma_start(
                out=d[:], in_=s.rearrange("(c p) m -> p c m", p=128)),
                deps=deps, inc="q_sp"))
        self.tok_w3c = tws

    def emit_tile_3c(self, tile, ydep):
        S = self.S
        st = self.stat
        c = lambda i: st[:, 4 * i:4 * i + 1]
        r0, r1 = tile * 512, (tile + 1) * 512
        buf = tile % 2
        def wl(key):
            v = self.war.get(key)
            return ([] if not v else (list(v) if isinstance(v, list) else [v]))
        tx4 = S.op("sp", lambda e: e.dma_start(
            out=self.z_x4[buf][:],
            in_=self.x4fm_spill.rearrange("(c p) t -> p c t", p=128)[:, :, r0:r1]),
            deps=wl(("z_x4", buf)) + self.tok_w3c, inc="q_sp")
        ty1 = S.op("sp", lambda e: e.dma_start(
            out=self.z_y1[:], in_=self.y1_spill[r0:r1].rearrange("(c p) m -> p c m", p=128)),
            deps=wl(("z_y1",)), inc="q_sp")
        for hc in range(NCH):
            ps, pn = (self.psA, "psA") if hc % 2 == 0 else (self.psB, "psB")
            emits = [lambda e, k=kc, h=hc, p=ps, b=buf: e.matmul(
                p[:], self.wZ1[:, k, h * 128:(h + 1) * 128],
                self.z_x4[b][:, k, :], start=(k == 0), stop=(k == NCH - 1))
                for kc in range(NCH)]
            self.mm(pn, emits, deps=[tx4, self.war.get(("z_h1s",)) or ("c_pe", 0)]
                    if hc == 0 else ())
            self.evac("act", pn, lambda e, h=hc, p=ps: e.activation(
                self.z_h1s[:, h, :], p[:], AF.Silu))
        th1 = self.S.tok("c_act")
        for hc in range(NCH):
            ps, pn = (self.psA, "psA") if hc % 2 == 0 else (self.psB, "psB")
            emits = [lambda e, k=kc, h=hc, p=ps, b=buf: e.matmul(
                p[:], self.wZ3[:, k, h * 128:(h + 1) * 128],
                self.z_x4[b][:, k, :], start=(k == 0), stop=(k == NCH - 1))
                for kc in range(NCH)]
            self.mm(pn, emits, deps=[self.war.get(("z_sw",)) or ("c_pe", 0)]
                    if hc == 0 else ())
            self.evac("dve", pn, lambda e, h=hc, p=ps: e.tensor_tensor(
                self.z_sw[:, h, :], self.z_h1s[:, h, :], p[:], ALU.mult),
                deps=[th1])
        tsw = self.S.tok("c_dve")
        for tc in range(4):
            typ = S.op("sp", lambda e, t=tc: e.dma_start(
                out=self.z_yp[:],
                in_=self.Ypair.rearrange("(k t) m -> k t m", k=4)[:, r0 + t * 128:r0 + (t + 1) * 128, :]
                    .rearrange("k p m -> p k m")),
                deps=ydep + wl(("z_yp",)), inc="q_sp")
            for hh in range(2):
                emits = [lambda e, k=kc, t=tc, h=hh: e.matmul(
                    self.psC[:, h * 512:(h + 1) * 512],
                    self.z_sw[:, k, t * 128:(t + 1) * 128],
                    self.wZ2[:, k, h * 512:(h + 1) * 512],
                    start=(k == 0), stop=(k == NCH - 1)) for kc in range(NCH)]
                self.mm("psC", emits, deps=[tsw] if (tc == 0 and hh == 0) else ())
            tz = self.evac("dve", "psC", lambda e: e.tensor_add(
                self.a_scr1[:], self.psC[:],
                self.z_yp[:, 0, :]), deps=[typ])
            for k in range(1, 4):
                S.op("dve", lambda e, k=k: e.tensor_add(
                    self.a_scr1[:], self.a_scr1[:], self.z_yp[:, k, :]), inc="c_dve")
            tsum = self.S.tok("c_dve")
            tst = self.ln_stats(self.a_scr1[:], deps=[tsum])
            ob = tc % 2
            tnm = S.op("dve", lambda e, o=ob: e.tensor_scalar(
                self.z_out[o][:], self.a_scr1[:], c(0), c(1), ALU.subtract, ALU.mult),
                deps=[tst] + wl(("z_out", ob)), inc="c_dve")
            tfin = S.op("dve", lambda e, t=tc, o=ob: e.tensor_add(
                self.z_out[o][:], self.z_out[o][:], self.z_y1[:, t, :]),
                deps=[ty1] if tc == 0 else (), inc="c_dve")
            tst_ = S.op("sp", lambda e, t=tc, o=ob: e.dma_start(
                out=self.out_d[r0 + t * 128:r0 + (t + 1) * 128, :],
                in_=self.z_out[o][:]), deps=[tfin], inc="q_sp")
            self.war[("z_out", ob)] = tst_
            self.war[("z_yp",)] = tsum
        self.war[("z_x4", buf)] = self.S.tok("c_pe")
        self.war[("z_y1",)] = self.S.tok("c_dve")
        self.war[("z_h1s",)] = self.S.tok("c_pe")
        self.war[("z_sw",)] = self.S.tok("c_pe")



    def emit_all(self):
        S = self.S
        self.alloc_phase1()
        self.load_perm()
        self.emit_halo_init(None)
        self.emit_p_chain()
        nt = self.T_OWN // 256
        for t in range(nt):
            self.emit_front_tile(t, self.x_own[t * 256:(t + 1) * 256, :],
                                 do_own=True, gchunk0=t * 2)
        for t in range(self.T_EXTRA // 256):
            self.emit_front_tile(nt + t, self.x_extra[t * 256:(t + 1) * 256, :],
                                 do_own=False, gchunk0=0)
        self.alloc_stage2()
        if self.grad_local:
            self.emit_wnew([S.tok("c_pe"), S.tok("c_dve"), S.tok("c_act")])
        else:
            self.emit_pair_allreduce()
        self.tok_w2a = [S.tok("c_pe"), S.tok("c_dve"), S.tok("c_act")]
        self.ar.release()
        self.ar.mark()
        self.alloc_phase2a()
        self.load_phase2a_weights()
        for t in range(self.T_OWN // 512):
            self.emit_tile_2a(t)
        t2a_done = [S.tok("c_pe"), S.tok("c_dve"), S.tok("c_act"), S.tok("q_sp")]
        self.ar.release()
        self.ar.mark()
        self.alloc_phase2bi()
        self.load_phase2bi_weights(t2a_done)
        for t in range(self.T_OWN // 512):
            self.emit_tile_2bi(t)
        t2bi_done = [S.tok("c_pe"), S.tok("c_dve"), S.tok("c_act"), S.tok("q_sp")]
        self.ar.release()
        self.ar.mark()
        self.alloc_phase2bii()
        self.tok_w2bii = t2bi_done
        for t in range(self.T_OWN // 512):
            self.emit_tile_2bii(t, first=(t == 0))
        xdep = [S.tok("q_pool"), S.tok("c_dve"), S.tok("c_pe"), S.tok("c_act")]
        self.ar.release()
        self.ar.mark()
        self.alloc_phase3b()
        for ei in range(E):
            self.emit_expert(ei, xdep if ei < 2 else ())
        t3b_done = [S.tok("c_pe"), S.tok("c_dve"), S.tok("c_act"), S.tok("q_sp"),
                    S.tok("q_pool"), S.tok("q_act")]
        self.ar.release()
        self.ar.mark()
        self.alloc_phase3c()
        self.load_phase3c_weights(t3b_done)
        ydep = [S.tok("q_pool")]
        for t in range(self.T_OWN // 512):
            self.emit_tile_3c(t, ydep)
        fins = S.barrier_tokens()
        def fin(e):
            for semname, val in fins:
                if val > 0:
                    e.wait_ge(S.sem[semname], val)
            return e.nop()
        S.op("sp", fin)

    def emit_phase1(self):
        self.alloc_phase1()
        self.emit_halo_init(None)
        ntiles = self.T_OWN // 256
        for t in range(ntiles):
            self.emit_front_tile(t, self.x_own[t * 256:(t + 1) * 256, :],
                                 do_own=True, gchunk0=t * 2)
        nextra = self.T_EXTRA // 256
        for t in range(nextra):
            self.emit_front_tile(ntiles + t, self.x_extra[t * 256:(t + 1) * 256, :],
                                 do_own=False, gchunk0=0)

    def finish_debug(self):
        S = self.S
        g_out = self.dbg("gradT", [128, NCH, D])
        toks = S.barrier_tokens()
        S.op("sp", lambda e: e.dma_start(out=g_out[:], in_=self.gradT[:]),
             deps=toks, inc="q_sp")
        final = S.tok("q_sp")
        S.op("sp", lambda e: e.wait_ge(S.sem["q_sp"], final[1]) or e.nop())


def host_masks():
    j = np.arange(144)[None, :]
    q = np.arange(128)[:, None]
    band = (j >= q) & (j <= q + WIN)
    maskA = np.where(band, 0.0, -1e30).astype(np.float32)
    maskF = np.where(band & (j >= 16), 0.0, -1e30).astype(np.float32)
    return maskA, maskF


def host_consts():
    ident = np.eye(128)
    ltriS128 = (np.arange(128)[:, None] < np.arange(128)[None, :]).astype(np.float32)
    ltriS32 = (np.arange(32)[:, None] < np.arange(32)[None, :]).astype(np.float32)
    return ident, ltriS128, ltriS32


def col_layout(v):
    w = v.shape[0]
    return np.ascontiguousarray(v.reshape(w // 128, 128).T).astype(np.float32)




_CACHE = {}


def _prog():
    if "p" not in _CACHE:
        p = Prog(T_OWN=4096, T_EXTRA=7 * 4096, debug=False, grad_local=True)
        p.emit_all()
        p.S.build()
        _CACHE["p"] = p
    return _CACHE["p"]


_NCALL = [0]


def kernel(**inputs):
    import time, os as _os
    p = _prog()
    inp = {k: np.asarray(v) for k, v in inputs.items()}
    x = np.asarray(inp["L0_attn_x"], np.float32)
    B, S, _ = x.shape
    TO = 4096
    bf = lambda a: np.asarray(a, BF)
    maskA_h, maskF_h = host_masks()
    ident, l128, l32 = host_consts()
    col = col_layout
    shared = {
        "WiT": bf(inp["L0_attn_Wi"].T), "Wl1T": bf(inp["L0_attn_Wl1"].T),
        "Wl2T": bf(inp["L0_attn_Wl2"].T), "memWT": bf(inp["L0_attn_mem_W"].T),
        "WoT": bf(inp["L0_attn_Wo"].T), "WgT": bf(inp["L0_attn_Wg"].T),
        "W1T": bf(inp["L0_attn_W1"].T), "W2T": bf(inp["L0_attn_W2"].T),
        "rWg": bf(inp["L0_ffn_Wg"]), "eW1": bf(inp["L0_ffn_W1"]),
        "eW3": bf(inp["L0_ffn_W3"]), "eW2": bf(inp["L0_ffn_W2"]),
        "zWs1": bf(inp["L0_ffn_Ws1"]), "zWs3": bf(inp["L0_ffn_Ws3"]),
        "zWs2": bf(inp["L0_ffn_Ws2"]),
        "bi_c": col(inp["L0_attn_bi"]), "bl1_c": col(inp["L0_attn_bl1"]),
        "bl2_c": col(inp["L0_attn_bl2"]), "mem_b_c": col(inp["L0_attn_mem_b"]),
        "bo_c": col(inp["L0_attn_bo"]), "bg_c": col(inp["L0_attn_bg"]),
        "b2_c": col(inp["L0_attn_b2"]), "b1_c": col(inp["L0_attn_b1"]),
        "bl2_r": bf(inp["L0_attn_bl2"]).reshape(1, -1),
        "mem_b_r": bf(inp["L0_attn_mem_b"]).reshape(1, -1),
        "bg_r": bf(inp["L0_attn_bg"]).reshape(1, -1),
        "b2_r": bf(inp["L0_attn_b2"]).reshape(1, -1),
        "g1_r": bf(inp["L0_attn_g1"]).reshape(1, -1),
        "be1_r": bf(inp["L0_attn_be1"]).reshape(1, -1),
        "g2_r": bf(inp["L0_attn_g2"]).reshape(1, -1),
        "be2_r": bf(inp["L0_attn_be2"]).reshape(1, -1),
        "g3_r": bf(inp["L0_attn_g3"]).reshape(1, -1),
        "be3_r": bf(inp["L0_attn_be3"]).reshape(1, -1),
        "fgate": np.full((128, 1), np.float32(inp["L0_attn_fgate"][0]), np.float32),
        "lrate": np.full((128, 1), np.float32(inp["L0_attn_lrate"][0]), np.float32),
        "P_in": bf(np.concatenate([np.asarray(inp["L0_attn_P"], np.float32),
                                   np.zeros((6, D), np.float32)])),
        "maskA": bf(maskA_h),
        "ident_bf": bf(ident), "ident_f32": ident.astype(np.float32),
        "negI_bf": bf(-ident), "ones_r": bf(np.ones((1, 512))),
        "ltriS128": bf(l128), "ltriS32": bf(l32),
        "ebase_c": np.stack([np.arange(E) * CAP, np.zeros(E)], 1).astype(np.float32),
        "tokp": np.stack([np.arange(128), np.zeros(128)], 1).astype(np.float32),
        "nonce": np.full((128, 1), 1.2345e7, np.float32),
        "cst_in": np.tile(np.array([[1e-5, 1024.0, 12345678.0, 1e5, 0.0, 0.0]],
                                   np.float32), (128, 1)),
    }
    _NCALL[0] += 1
    n0 = (time.time_ns() ^ (_os.getpid() << 12)) & 0x7FFFFFFF or 1
    n1 = (_NCALL[0] * 2654435761 + 12345) & 0x7FFFFFFF or 2
    in_maps = []
    for c in range(8):
        b, hf = c // 2, c % 2
        side = c % 2
        im = dict(shared)
        im["x_own"] = np.ascontiguousarray(x[b, hf * TO:(hf + 1) * TO])
        im["x_extra"] = np.ascontiguousarray(np.concatenate(
            [x[b2, h2 * TO:(h2 + 1) * TO] for b2 in range(B) for h2 in (0, 1)
             if not (b2 == b and h2 == hf)], 0))
        im["xhalo"] = (np.zeros((16, D), np.float32) if hf == 0
                       else np.ascontiguousarray(x[b, TO - 16:TO]))
        im["maskF"] = bf(maskF_h if hf == 0 else maskA_h)
        ps = np.zeros((128, 1), np.float32)
        ps[:10] = 4.0
        im["pscale"] = ps
        im["tok2"] = np.array([[n0, n1], [n0, n1]], np.int32)
        im["si_grad8"] = (side * 1024 + np.arange(NCH)[None, :] * 128
                          + np.arange(128)[:, None]).astype(np.int32)
        im["gi_grad8"] = ((1 - side) * 1024 + np.arange(NCH)[None, :] * 128
                          + np.arange(128)[:, None]).astype(np.int32)
        im["fi_own"] = np.array([[side * 2], [side * 2 + 1]], np.int32)
        im["fi_peer"] = np.array([[(1 - side) * 2], [(1 - side) * 2 + 1]], np.int32)
        in_maps.append(im)
    res = run_bass_kernel_spmd(p.nc, in_maps, core_ids=list(range(8)))
    out = np.empty((B, S, D), np.float32)
    for c in range(8):
        b, hf = c // 2, c % 2
        out[b, hf * TO:(hf + 1) * TO] = np.asarray(res.results[c]["out_y"])
    return out
